# version 20
# baseline (speedup 1.0000x reference)
import numpy as np
import concourse.bass as bass
import concourse.mybir as mybir
from concourse.bass_utils import run_bass_kernel_spmd

F32 = mybir.dt.float32
BF16 = mybir.dt.bfloat16
AF = mybir.ActivationFunctionType
ALU = mybir.AluOpType
AX = mybir.AxisListType

P = 128
D = 1024
NH = 16
HD = 64
DFC = 4096
DFF = 2816
EPS = 1e-5
NCORES = 8

ENGS = ("sync", "gpsimd", "tensor", "vector", "scalar")


class Sched:
    def __init__(self):
        self.ops = {e: [] for e in ENGS}
        self.counts = {}

    def at(self, sem):
        return self.counts.get(sem, 0)

    def now(self, *sems):
        return [(s, self.at(s)) for s in sems]

    def emit(self, eng, fn, waits=(), inc=None, amt=1):
        ws = [(s, v) for (s, v) in waits if v > 0]
        self.ops[eng].append((fn, ws, inc, amt))
        if inc is not None:
            self.counts[inc] = self.counts.get(inc, 0) + amt
        return (inc, self.counts.get(inc, 0)) if inc else None


class Slots:
    def __init__(self, n):
        self.n = n
        self.guard = [[] for _ in range(n)]
        self.i = 0

    def use(self):
        s = self.i % self.n
        self.i += 1
        w = self.guard[s]
        self.guard[s] = []
        return s, w

    def consumed(self, s, sem_val):
        if sem_val is not None:
            self.guard[s].append(sem_val)


def ts(i, n):
    return slice(i * n, (i + 1) * n)


def build(n_tok=2048, dump=(), stop=6, asub=7):
    assert n_tok % P == 0
    NT = n_tok // P
    IH_M = DFC // 2
    NCH_M = IH_M // P
    IH_F = DFF // 2
    NCH_F = IH_F // P

    nc = bass.Bass("TRN2", target_bir_lowering=False, detect_race_conditions=False)

    x_hbm = nc.dram_tensor("x", [n_tok, D], F32, kind="ExternalInput")
    wqkv = nc.dram_tensor("wqkv_t", [D, 3 * D], F32, kind="ExternalInput")
    wo = nc.dram_tensor("wo_t", [D, D], F32, kind="ExternalInput")
    wfc = nc.dram_tensor("wfc_t", [D, DFC], F32, kind="ExternalInput")
    wproj = nc.dram_tensor("wproj_t", [DFC, D], F32, kind="ExternalInput")
    wg_n = nc.dram_tensor("wg_n", [DFF, D], F32, kind="ExternalInput")
    wu_n = nc.dram_tensor("wu_n", [DFF, D], F32, kind="ExternalInput")
    wd_n = nc.dram_tensor("wd_n", [D, DFF], F32, kind="ExternalInput")
    ln1w = nc.dram_tensor("ln1w", [P, 8], F32, kind="ExternalInput")
    ln1b = nc.dram_tensor("ln1b", [P, 8], F32, kind="ExternalInput")
    ln2w = nc.dram_tensor("ln2w", [P, 8], F32, kind="ExternalInput")
    ln2b = nc.dram_tensor("ln2b", [P, 8], F32, kind="ExternalInput")
    bqkv = nc.dram_tensor("bqkv", [1, 3 * D], F32, kind="ExternalInput")
    bo = nc.dram_tensor("bo", [1, D], F32, kind="ExternalInput")
    bfc = nc.dram_tensor("bfc", [1, DFC], F32, kind="ExternalInput")
    bproj = nc.dram_tensor("bproj", [1, D], F32, kind="ExternalInput")
    out_hbm = nc.dram_tensor("out", [n_tok, D], F32, kind="ExternalOutput")
    dmp = {}
    for name, shape, dt in dump:
        dmp[name] = nc.dram_tensor("dump_" + name, shape, dt, kind="ExternalOutput")

    from contextlib import ExitStack
    with ExitStack() as ctx:
        e = ctx.enter_context

        racc = e(nc.sbuf_tensor("racc", [P, NT * D], BF16))
        x2T = e(nc.sbuf_tensor("x2T", [P, 8 * n_tok], BF16))
        WARC = 36992
        war = e(nc.sbuf_tensor("war", [P, WARC], BF16))
        TBC = 17664
        tb = e(nc.sbuf_tensor("tb", [P, TBC], BF16))
        xt = e(nc.sbuf_tensor("xt", [P, 2 * D], F32))
        s5 = e(nc.sbuf_tensor("s5", [P, D], F32))
        elu_f = e(nc.sbuf_tensor("elu_f", [P, 1024], F32))
        outb = elu_f
        stage = None
        st = e(nc.sbuf_tensor("st", [P, 48], F32))
        qs = e(nc.sbuf_tensor("qs", [P, 3 * 96], F32))
        ident = e(nc.sbuf_tensor("ident", [P, P], BF16))
        maskS = e(nc.sbuf_tensor("maskS", [P, P], BF16))
        lnv = e(nc.sbuf_tensor("lnv", [P, 32], F32))
        lnb_bf = e(nc.sbuf_tensor("lnb_bf", [P, 16], BF16))
        brf = e(nc.sbuf_tensor("brf", [1, 512], F32))
        brf2 = e(nc.sbuf_tensor("brf2", [1, 512], F32))
        biasf = e(nc.sbuf_tensor("biasf", [1, 3 * D], BF16))

        pb0 = e(nc.psum_tensor("pb0", [P, 512], F32))
        pb1 = e(nc.psum_tensor("pb1", [P, 512], F32))
        ptr0 = e(nc.psum_tensor("ptr0", [P, 1024], BF16))
        ptr1 = e(nc.psum_tensor("ptr1", [P, 1024], BF16))
        ptr2 = e(nc.psum_tensor("ptr2", [P, 1024], BF16))
        ps1a = e(nc.psum_tensor("ps1a", [P, 512], F32))
        ps1b = e(nc.psum_tensor("ps1b", [P, 512], F32))
        ps2 = e(nc.psum_tensor("ps2", [P, 512], F32))
        pbias = ps2

        sems = {}
        for sname in ("hx", "ho", "gw", "gm", "pe", "vv", "ss", "gg"):
            sems[sname] = e(nc.semaphore(sname))
        blk = e(nc.Block())

        S = Sched()
        PB = (pb0, pb1)
        pbig = Slots(2)
        ptrs = Slots(3)
        PTR = None
        ps1s = Slots(2)
        TBP = TBC

        A_XN, A_XNT, A_QKV, A_QT, A_KT, A_VG, A_SS, A_AM, A_AT, A_ER, A_EE, A_JK = (
            0, 1024, 2048, 5120, 7168, 9216, 10256, 10512, 11536, 13584, 14608, 15632)
        M_XN, M_MID, M_MIDT, M_JK = 0, 1024, 3072, 15632
        F_XQD, F_QXN, F_QV, F_B1, F_B2, F_B3, F_B4, F_B5, F_WQN, F_HT, F_HQ, F_HQT, F_JK = (
            0, 1024, 2432, 3840, 5248, 6656, 8064, 9472, 10880, 12288, 13696, 15104, 16512)
        A_WQKV, A_WO, A_BR = 0, 24576, 32768
        M_WFC, M_WPR, M_BR = 0, 16384, 32768
        F_WG, F_WU, F_WD = 0, 11264, 22528

        def mm(psum_ap, lhsT_ap, rhs_ap, start, stop, waits=(), inc=None, tp=None):
            def fn(t, a=psum_ap, b=lhsT_ap, c=rhs_ap, st_=start, sp=stop, tp_=tp):
                return t.matmul(a, b, c, start=st_, stop=sp, tile_position=tp_)
            return S.emit("tensor", fn, waits, inc)

        def vop(fn, waits=()):
            return S.emit("vector", fn, waits, "vv")

        def sop(fn, waits=()):
            return S.emit("scalar", fn, waits, "ss")

        def gop(fn, waits=()):
            return S.emit("gpsimd", fn, waits, "gg")

        def dma(eng, out_ap, in_ap, waits=(), inc="hx"):
            def fn(g, a=out_ap, b=in_ap):
                return g.dma_start(out=a, in_=b)
            return S.emit(eng, fn, waits, inc, amt=16)

        gop(lambda g: g.memset(ident[:, :], 0.0))
        gop(lambda g: g.affine_select(out=ident[:, :], in_=ident[:, :],
                                      compare_op=ALU.not_equal, fill=1.0, base=0,
                                      pattern=[[-1, P]], channel_multiplier=1),
            waits=S.now("gg"))
        Bm = tb[0:8, 0:P]
        gop(lambda g: g.memset(Bm, 1.0), waits=S.now("gg"))
        gop(lambda g: g.affine_select(out=Bm, in_=Bm, compare_op=ALU.is_ge,
                                      fill=0.0, base=0, pattern=[[1, P]],
                                      channel_multiplier=-16), waits=S.now("gg"))
        gop(lambda g: g.affine_select(out=Bm, in_=Bm, compare_op=ALU.is_ge,
                                      fill=0.0, base=15, pattern=[[-1, P]],
                                      channel_multiplier=16), waits=S.now("gg"))
        def _mask_mm(t):
            return t.matmul(ps1a[:, 0:P], Bm, Bm, start=True, stop=True)
        S.emit("tensor", _mask_mm, S.now("gg"), "pe")
        vop(lambda v: v.tensor_copy(maskS[:, :], ps1a[:, 0:P]), waits=S.now("pe"))
        for G in range(16):
            gop(lambda g, G_=G: g.memset(tb[:, A_VG + G_ * 65 + 64: A_VG + G_ * 65 + 65], 1.0))
        CONST = S.now("gg")[0]
        CONSTV = S.now("vv")[0]

        dma("sync", lnv[:, 0:8], ln1w[:, :])
        dma("sync", lnv[:, 8:16], ln1b[:, :])
        dma("sync", lnv[:, 16:24], ln2w[:, :])
        dma("sync", lnv[:, 24:32], ln2b[:, :])
        LNV = S.now("hx")[0]
        dma("gpsimd", lnb_bf[:, 0:8], ln1b[:, :], inc="gw")
        dma("gpsimd", lnb_bf[:, 8:16], ln2b[:, :], inc="gw")
        LNBF = S.now("gw")[0]

        ln_state = {"i": 0, "act": {0: None, 1: None}}

        def ln_apply(src_ap, out_ap, width, waits_v, waits_s, out_guards=(), jkc=None):
            n = ln_state["i"]
            ln_state["i"] += 1
            b = 16 * (n % 2)
            g = ln_state["act"][n % 2]
            gg = [g] if g else []
            jk = tb[:, jkc:jkc + width]
            c_sum, c_sq, c_mu, c_var, c_t, c_inv, c_nmi = (b, b + 2, b + 4, b + 6,
                                                           b + 8, b + 10, b + 12)
            red = vop(lambda v, a=src_ap: v.reduce_sum(st[:, c_sum:c_sum + 1], a,
                                                       axis=AX.X),
                      waits=list(waits_v) + gg)
            sop(lambda s, a=src_ap: s.activation(jk, a, AF.Square,
                                                 accum_out=st[:, c_sq:c_sq + 1]),
                waits=list(waits_s) + gg)
            sqd = S.now("ss")[0]
            w1 = vop(lambda v: v.tensor_scalar(st[:, c_mu:c_mu + 1],
                                               st[:, c_sum:c_sum + 1],
                                               1.0 / width, None, ALU.mult),
                     waits=[red])
            w2 = vop(lambda v: v.tensor_scalar(st[:, c_var:c_var + 1],
                                               st[:, c_sq:c_sq + 1],
                                               1.0 / width, None, ALU.mult),
                     waits=[sqd])
            w3 = vop(lambda v: v.tensor_mul(st[:, c_t:c_t + 1], st[:, c_mu:c_mu + 1],
                                            st[:, c_mu:c_mu + 1]), waits=[w1])
            w4 = vop(lambda v: v.tensor_tensor(st[:, c_var:c_var + 1],
                                               st[:, c_var:c_var + 1],
                                               st[:, c_t:c_t + 1], ALU.subtract),
                     waits=[w2, w3])
            ved = vop(lambda v: v.tensor_scalar(st[:, c_var:c_var + 1],
                                                st[:, c_var:c_var + 1],
                                                EPS, None, ALU.add), waits=[w4])
            sop(lambda s: s.activation(st[:, c_t:c_t + 1], st[:, c_var:c_var + 1],
                                       AF.Ln), waits=[ved])
            sop(lambda s: s.activation(st[:, c_inv:c_inv + 1], st[:, c_t:c_t + 1],
                                       AF.Exp, scale=np.float32(-0.5)))
            inv = S.now("ss")[0]
            w5 = vop(lambda v: v.tensor_mul(st[:, c_nmi:c_nmi + 1],
                                            st[:, c_mu:c_mu + 1],
                                            st[:, c_inv:c_inv + 1]),
                     waits=[inv, w1])
            nmi = vop(lambda v: v.tensor_scalar(st[:, c_nmi:c_nmi + 1],
                                                st[:, c_nmi:c_nmi + 1],
                                                -1.0, None, ALU.mult), waits=[w5])
            act = sop(lambda s, a=src_ap, o=out_ap: s.activation(
                o, a, AF.Identity, bias=st[:, c_nmi:c_nmi + 1],
                scale=st[:, c_inv:c_inv + 1]),
                waits=[nmi] + list(out_guards))
            ln_state["act"][n % 2] = act
            return act

        def do_transpose(src_ap, width, dst_ap, waits, ee):
            sl, g = ptrs.use()
            pt = (ptr0, ptr1, ptr2)[sl]
            transpose_ap = pt[0:width, 0:P]

            def fn(t, a=transpose_ap, b=src_ap):
                return t.transpose(a, b, ident[:, :])
            S.emit("tensor", fn, list(waits) + list(g), "pe")
            ped = S.now("pe")[0]
            if ee == "v":
                ev = vop(lambda v, a=dst_ap, b=transpose_ap: v.tensor_copy(a, b),
                         waits=[ped])
            else:
                ev = sop(lambda s, a=dst_ap, b=transpose_ap: s.copy(a, b), waits=[ped])
            ptrs.consumed(sl, ev)
            return ev

        def project(nchunks, lhsT_of, rhs_of, out_widths, bias_col, first_waits, evac,
                    og_waits=None):
            col0 = 0
            for og, wdt in enumerate(out_widths):
                sl, guard = pbig.use()
                pb = PB[sl]
                fw = list(first_waits) + list(guard)
                if og_waits is not None:
                    fw += list(og_waits(og, col0, wdt))
                first = True
                for c in range(nchunks):
                    last = c == nchunks - 1
                    mm(pb[:, 0:wdt], lhsT_of(c), rhs_of(c, col0, wdt), first, last,
                       waits=fw if first else [], inc="pe" if last else None)
                    first, fw = False, []
                done = S.now("pe")[0]
                ev = evac(og, pb, wdt, done, col0)
                pbig.consumed(sl, ev)
                col0 += wdt

        def quant_emit(src2d, W, dst2d, waits_v, xn32=None, v32=None):
            nb = W // 16
            src3 = src2d.rearrange("p (b k) -> p b k", k=16)
            QXN = xn32 if xn32 is not None else tb[:, F_QXN:F_QXN + W]
            QV = v32 if v32 is not None else tb[:, F_QV:F_QV + W]
            B1 = tb[:, F_B1:F_B1 + W]
            B2 = tb[:, F_B2:F_B2 + W]
            B3 = tb[:, F_B3:F_B3 + W]
            B4 = tb[:, F_B4:F_B4 + W]
            B5 = tb[:, F_B5:F_B5 + W]
            bmr = vop(lambda v: v.tensor_reduce(qs[:, 0:nb], src3, axis=AX.X,
                                                op=ALU.max, apply_absolute_value=True),
                      waits=waits_v)
            scw = vop(lambda v: v.tensor_scalar(qs[:, 96:96 + nb], qs[:, 0:nb],
                                                1.0 / 6.0, 1e-12, ALU.mult, ALU.max),
                      waits=[bmr])
            rsr = vop(lambda v: v.reciprocal(qs[:, 192:192 + nb], qs[:, 96:96 + nb]),
                      waits=[scw])
            pv = qstate["sd"]
            vop(lambda v: v.tensor_tensor(
                QXN.rearrange("p (b k) -> p b k", k=16), src3,
                qs[:, 192:192 + nb].unsqueeze(2).broadcast_to([P, nb, 16]), ALU.mult),
                waits=([pv] if pv else []) + [rsr])
            xnd = S.now("vv")[0]
            sop(lambda s: s.activation(QV, QXN, AF.Abs), waits=[xnd])
            sop(lambda s: s.activation(B5, QXN, AF.Sign))
            sd = S.now("ss")[0]
            pvg = qstate["s_guard"]
            qstate["sd"] = sd
            NUD = 1.0 + 2.0 ** -40
            if True:
                vop(lambda v: v.tensor_scalar(B1, QV, 0.25 * NUD, None, ALU.is_gt),
                    waits=[sd])
                vop(lambda v: v.tensor_scalar(B2, QV, 0.75 * NUD, None, ALU.is_gt))
                vop(lambda v: v.tensor_add(B1, B1, B2))
                vop(lambda v: v.tensor_scalar(B2, QV, 1.25 * NUD, None, ALU.is_gt))
                vop(lambda v: v.tensor_scalar(B3, QV, 1.75 * NUD, None, ALU.is_gt))
                vop(lambda v: v.tensor_add(B2, B2, B3))
                vop(lambda v: v.tensor_add(B1, B1, B2))
                vop(lambda v: v.tensor_scalar(B2, QV, 2.5 * NUD, None, ALU.is_gt))
                vop(lambda v: v.tensor_scalar(B3, QV, 3.5 * NUD, None, ALU.is_gt))
                vop(lambda v: v.tensor_scalar(B4, QV, 5.0 * NUD, None, ALU.is_gt))
                vop(lambda v: v.scalar_tensor_tensor(B3, B4, 2.0, B3, ALU.mult, ALU.add))
                vop(lambda v: v.tensor_add(B3, B3, B2))
                vop(lambda v: v.scalar_tensor_tensor(B1, B1, 0.5, B3, ALU.mult, ALU.add))
            vop(lambda v: v.tensor_mul(B1, B1, B5))
            qv = vop(lambda v, d=dst2d: v.tensor_tensor(
                d.rearrange("p (b k) -> p b k", k=16),
                B1.rearrange("p (b k) -> p b k", k=16),
                qs[:, 96:96 + nb].unsqueeze(2).broadcast_to([P, nb, 16]), ALU.mult),
                waits=S.now("pe"))
            qstate["s_guard"] = qv
            return qv

        qstate = {"s_guard": None, "sd": None}
        tile_qv = [None]
        ht_guard = [None, None]
        ob_guard = [None]

        for c in range(8):
            dma("gpsimd", war[:, A_WQKV + c * 3072: A_WQKV + (c + 1) * 3072],
                bass.AP(wqkv, c * P * 3 * D, [[3 * D, P], [1, 3 * D]]), inc="gw")
        WQKV_RAW = S.now("gw")[0]
        for c in range(8):
            dma("gpsimd", war[:, A_WO + c * D: A_WO + (c + 1) * D],
                bass.AP(wo, c * P * D, [[D, P], [1, D]]), inc="gw")
        dma("gpsimd", biasf[0:1, 0:3072], bqkv[:, :], inc="gw")
        dma("gpsimd", war[:, A_BR + 3072: A_BR + 4096],
            bass.AP(bo, 0, [[D, 1], [0, P], [1, D]]), inc="gw")
        WA_RAW = S.now("gw")[0]
        for og in range(6):
            for c in range(8):
                mm(pbias[0:1, 0:512], lnb_bf[:, c:c + 1],
                   war[:, A_WQKV + c * 3072 + og * 512: A_WQKV + c * 3072 + og * 512 + 512],
                   c == 0, c == 7,
                   waits=([LNBF, WQKV_RAW] + S.now("vv")) if c == 0 else [],
                   inc="pe" if c == 7 else None)
            done = S.now("pe")[0]
            bcw = vop(lambda v: v.tensor_copy(brf[0:1, 0:512], pbias[0:1, 0:512]),
                      waits=[done, WA_RAW])
            vop(lambda v, og_=og: v.tensor_tensor(
                brf2[0:1, 0:512], brf[0:1, 0:512],
                biasf[0:1, og_ * 512: og_ * 512 + 512], ALU.add),
                waits=S.now("gw") + [bcw])
            dma("gpsimd", war[:, A_BR + og * 512: A_BR + og * 512 + 512],
                bass.AP(brf2, 0, [[512, 1], [0, P], [1, 512]]),
                waits=S.now("vv"), inc="gw")
        BROW_A = S.now("vv", "gw")
        for c in range(8):
            vop(lambda v, c_=c: v.tensor_scalar(
                war[:, A_WQKV + c_ * 3072: A_WQKV + (c_ + 1) * 3072],
                war[:, A_WQKV + c_ * 3072: A_WQKV + (c_ + 1) * 3072],
                lnv[:, c_:c_ + 1], None, ALU.mult),
                waits=BROW_A + [LNV])
        WQKV_OK = S.now("vv")[0]

        x_slots = Slots(2)

        def load_x(i, slots):
            sl, g = slots.use()
            dma("sync", xt[:, ts(sl, D)], x_hbm[ts(i, P), :], waits=g, inc="hx")
            return sl, S.now("hx")[0]

        xload = [load_x(0, x_slots)] if stop >= 2 else []
        elu_guard = {0: None, 1: None}
        rec_guard = [None] * 4
        vg_guard = [None]

        stA = {}
        um_guard = [None]

        def emit_stageA(i):
            xsl, xrdy = xload[i]
            xtile = xt[:, ts(xsl, D)]
            XN = tb[:, A_XN:A_XN + D]
            gm_prev = S.now("gm")[0]
            xn_ok = ln_apply(xtile, XN, D, [xrdy], [xrdy], out_guards=S.now("pe"), jkc=A_JK)
            if i == 0 and "xn" in dmp:
                dma("sync", dmp["xn"][0:P, :], XN, waits=[xn_ok], inc="ho")
            if i == 0 and "st" in dmp:
                dma("sync", dmp["st"][0:P, :], st[:, 0:32], waits=[xn_ok], inc="ho")
            for c in range(8):
                do_transpose(XN[:, ts(c, P)], P, tb[:, A_XNT + c * P: A_XNT + (c + 1) * P],
                             [xn_ok] if c == 0 else [], "v" if c % 2 == 0 else "s")
            XNT_OK = S.now("vv", "ss")

            def qkv_evac(og, pb, wdt, done, col0):
                bias = war[:, A_BR + og * 512: A_BR + og * 512 + 512]
                if og < 4:
                    par = og % 2
                    eg = elu_guard[par]
                    ef = elu_f[:, par * 512:(par + 1) * 512]
                    er = tb[:, A_ER + par * 512: A_ER + (par + 1) * 512]
                    ee = tb[:, A_EE + par * 512: A_EE + (par + 1) * 512]
                    td = vop(lambda v, a=pb, ef_=ef, b_=bias: v.tensor_tensor(
                        ef_, a[:, 0:512], b_, ALU.add),
                        waits=[done] + (list(eg) if eg else []))
                    rd = sop(lambda s, ef_=ef, er_=er: s.activation(er_, ef_, AF.Relu),
                             waits=[td])
                    md = vop(lambda v, ef_=ef: v.tensor_scalar(
                        ef_, ef_, 0.0, None, ALU.min), waits=[rd])
                    ed = sop(lambda s, ef_=ef, ee_=ee: s.activation(ee_, ef_, AF.Exp),
                             waits=[md])
                    ad = vop(lambda v, og_=og, er_=er, ee_=ee: v.tensor_add(
                        tb[:, A_QKV + og_ * 512:A_QKV + og_ * 512 + 512], er_, ee_),
                        waits=[ed, gm_prev])
                    elu_guard[par] = (ed, ad)
                    return ad
                else:
                    return vop(lambda v, a=pb, og_=og, b_=bias: v.tensor_tensor(
                        tb[:, A_QKV + og_ * 512:A_QKV + og_ * 512 + 512], a[:, 0:512],
                        b_, ALU.add),
                        waits=[done, gm_prev])

            if asub >= 2:
                def qkv_evac_plain(og, pb, wdt, done, col0):
                    return sop(lambda s_, a=pb, og_=og: s_.copy(
                        tb[:, A_QKV + og_ * 512:A_QKV + og_ * 512 + 512], a[:, 0:512]),
                        waits=[done, gm_prev])
                _ev = qkv_evac if asub not in (20, 21) else qkv_evac_plain
                project(8, lambda c: tb[:, A_XNT + c * P: A_XNT + (c + 1) * P],
                        lambda c, col0, wdt: war[:, A_WQKV + c * 3072 + col0:
                                                 A_WQKV + c * 3072 + col0 + wdt],
                        [512] * 6, None, XNT_OK + [WQKV_OK] + BROW_A, _ev)
            QKV_OK = S.now("vv", "ss")
            if i == 0 and "qkv" in dmp:
                dma("sync", dmp["qkv"][0:P, :], tb[:, A_QKV:A_QKV + 3 * D],
                    waits=QKV_OK, inc="ho")
            return dict(xsl=xsl, xrdy=xrdy, xtile=xtile, QKV_OK=QKV_OK)

        def emit_marshalA(i):
            h = stA[i]
            QKV_OK = h["QKV_OK"]
            for hh in range(NH if asub >= 3 else 0):
                do_transpose(tb[:, A_QKV + hh * HD: A_QKV + (hh + 1) * HD], HD,
                             bass.AP(tb, A_QT + hh, [[TBP, 64], [P, 16], [16, 8]]),
                             QKV_OK if hh == 0 else [], "v" if hh % 2 else "s")
            for hh in range(NH if asub >= 3 else 0):
                do_transpose(tb[:, A_QKV + D + hh * HD: A_QKV + D + (hh + 1) * HD], HD,
                             bass.AP(tb, A_KT + hh, [[TBP, 64], [P, 16], [16, 8]]),
                             [], "s" if hh % 2 else "v")
            h["QKT_OK"] = S.now("vv", "ss")

            vm_first = QKV_OK + ([vg_guard[0]] if vg_guard[0] else []) + [CONST]
            h["gm_base"] = S.at("gm")
            for G in range(16 if asub >= 4 else 0):
                dma("gpsimd",
                    bass.AP(tb, A_VG + G * 65, [[TBP, P], [1, HD]]),
                    bass.AP(tb, (8 * G) * TBP + A_QKV + 2 * D, [[TBP, 8], [HD, NH], [1, HD]]),
                    waits=vm_first if G == 0 else [], inc="gm")

        def emit_coreA(i):
            h = stA[i]
            QKT_OK = h["QKT_OK"]
            gm_base = h["gm_base"]
            xtile = h["xtile"]
            xrdy = h["xrdy"]
            evms = [None] * 16
            amg = [um_guard[0]] if um_guard[0] else []
            NG = 16 if asub >= 5 else 0
            for G in range(NG + 1):
                if G < NG:
                    sl1, g1 = ps1s.use()
                    p1 = (ps1a, ps1b)[sl1]
                    mm(p1[:, 0:P], tb[0:64, A_KT + G * P: A_KT + (G + 1) * P],
                       tb[0:64, A_QT + G * P: A_QT + (G + 1) * P], True, True,
                       waits=(QKT_OK if G == 0 else []) + list(g1), inc="pe")
                    s1d = S.now("pe")[0]
                    ssl = G % 2
                    evms[G] = vop(lambda v, a=p1[:, 0:P], ssl_=ssl: v.tensor_tensor(
                        tb[:, A_SS + ssl_ * P: A_SS + (ssl_ + 1) * P], a, maskS[:, :],
                        ALU.mult), waits=[s1d, CONST, CONSTV])
                    ps1s.consumed(sl1, evms[G])
                if G >= 1:
                    Gp = G - 1
                    sl2, g2 = pbig.use()
                    pb2 = PB[sl2]
                    mm(pb2[:, 0:65],
                       tb[:, A_SS + (Gp % 2) * P: A_SS + (Gp % 2 + 1) * P],
                       tb[:, A_VG + Gp * 65: A_VG + Gp * 65 + 65],
                       True, True,
                       waits=[evms[Gp], ("gm", gm_base + 16 * (Gp + 1))] + list(g2),
                       inc="pe")
                    s2d = S.now("pe")[0]
                    r = Gp % 4
                    rg = rec_guard[r]
                    rv = vop(lambda v, a=pb2[:, 64:65], r_=r: v.reciprocal(
                        st[:, 32 + 2 * r_:33 + 2 * r_], a),
                        waits=[s2d] + ([rg] if rg else []))
                    ev2 = sop(lambda s, a=pb2[:, 0:64], G_=Gp, r_=r: s.activation(
                        tb[:, A_AM + G_ * HD: A_AM + (G_ + 1) * HD], a, AF.Identity,
                        scale=st[:, 32 + 2 * r_:33 + 2 * r_]),
                        waits=[rv] + (amg if Gp == 0 else []))
                    rec_guard[r] = ev2
                    pbig.consumed(sl2, ev2)
            if NG:
                vg_guard[0] = S.now("pe")[0]
            AM_OK = S.now("ss")[0]

            for G in range(16 if asub >= 6 else 0):
                for pb_ in (0, 64):
                    sl, g = ptrs.use()
                    pt = (ptr0, ptr1, ptr2)[sl]
                    tout = pt[pb_:pb_ + 64, 0:P]
                    src_t = tb[:, A_AM + G * HD: A_AM + (G + 1) * HD]

                    def fn(t, a=tout, b=src_t):
                        return t.transpose(a, b, ident[:, :])
                    S.emit("tensor", fn,
                           ([AM_OK] if (G == 0 and pb_ == 0) else []) + list(g), "pe")
                    ped = S.now("pe")[0]
                    par = pb_ // 64
                    src_e = bass.AP(pt, pb_ * 1024 + par,
                                    [[1024, 64], [16, 8], [2, 8]])
                    dst_e = bass.AP(tb, pb_ * TBP + A_AT + 8 * G,
                                    [[TBP, 64], [1, 8], [P, 8]])
                    if (G + par) % 2 == 0:
                        ev = vop(lambda v, a=dst_e, b=src_e: v.tensor_copy(a, b),
                                 waits=[ped])
                    else:
                        ev = sop(lambda s_, a=dst_e, b=src_e: s_.copy(a, b), waits=[ped])
                    ptrs.consumed(sl, ev)
            if asub >= 6:
                um_guard[0] = S.now("pe")[0]
            AT_OK = S.now("vv", "ss")

            def o_evac(og, pb, wdt, done, col0, i_=i, xp=xtile, xr=xrdy):
                vop(lambda v, a=pb, og_=og: v.tensor_tensor(
                    s5[:, og_ * 512: og_ * 512 + 512], a[:, 0:512],
                    war[:, A_BR + 3072 + og_ * 512: A_BR + 3072 + og_ * 512 + 512],
                    ALU.add), waits=[done, WA_RAW])
                return vop(lambda v, og_=og, j_=i_, x_=xp: v.tensor_tensor(
                    racc[:, j_ * D + og_ * 512: j_ * D + og_ * 512 + 512],
                    s5[:, og_ * 512: og_ * 512 + 512],
                    x_[:, og_ * 512: og_ * 512 + 512], ALU.add),
                    waits=[xr])

            if asub >= 7:
                project(8, lambda c: tb[:, A_AT + c * P: A_AT + (c + 1) * P],
                        lambda c, col0, wdt: war[:, A_WO + c * D + col0:
                                                 A_WO + c * D + col0 + wdt],
                        [512] * 2, A_BR + 3072, AT_OK + [WA_RAW], o_evac)
            else:
                vop(lambda v, i_=i, xp=xtile: v.tensor_copy(
                    racc[:, ts(i_, D)], xp), waits=[xrdy])
            x_slots.consumed(h["xsl"], S.now("vv")[0])
            x_slots.consumed(h["xsl"], S.now("ss")[0])
            if i == 0:
                if "attn" in dmp:
                    dma("sync", dmp["attn"][0:P, :], tb[:, A_AM:A_AM + D],
                        waits=[AM_OK], inc="ho")
                if "racc_a" in dmp:
                    dma("sync", dmp["racc_a"][0:P, :], racc[:, 0:D],
                        waits=S.now("vv"), inc="ho")

        for i in range(NT if stop >= 2 else 0):
            if i + 1 < NT:
                xload.append(load_x(i + 1, x_slots))
            if i == 0:
                stA[0] = emit_stageA(0)
            emit_marshalA(i)
            if i + 1 < NT:
                stA[i + 1] = emit_stageA(i + 1)
            emit_coreA(i)
            del stA[i]
        RACC_A = S.now("vv")[0]

        for H in range(2 if stop >= 3 else 0):
            phase_gate = S.now("pe", "vv", "ss")
            for c in range(8):
                dma("gpsimd", war[:, M_WFC + c * IH_M: M_WFC + (c + 1) * IH_M],
                    bass.AP(wfc, c * P * DFC + H * IH_M, [[DFC, P], [1, IH_M]]),
                    waits=phase_gate if c == 0 else [], inc="gw")
            WFC_RAW = S.now("gw")[0]
            for c in range(NCH_M):
                dma("gpsimd", war[:, M_WPR + c * D: M_WPR + (c + 1) * D],
                    bass.AP(wproj, (H * IH_M + c * P) * D, [[D, P], [1, D]]), inc="gw")
            WPR_RAW = S.now("gw")[0]
            dma("gpsimd", biasf[0:1, 0:IH_M],
                bass.AP(bfc, H * IH_M, [[1, 1], [1, IH_M]]), inc="gw")
            if H == 0:
                dma("gpsimd", war[:, M_BR + 2048: M_BR + 3072],
                    bass.AP(bproj, 0, [[D, 1], [0, P], [1, D]]), inc="gw")
            BV_RAW = S.now("gw", "gg")
            for og in range(4):
                for c in range(8):
                    mm(pbias[0:1, 0:512], lnb_bf[:, 8 + c:8 + c + 1],
                       war[:, M_WFC + c * IH_M + og * 512: M_WFC + c * IH_M + og * 512 + 512],
                       c == 0, c == 7,
                       waits=([LNBF, WFC_RAW] + S.now("vv", "ss")) if c == 0 else [],
                       inc="pe" if c == 7 else None)
                done = S.now("pe")[0]
                bcw = vop(lambda v: v.tensor_copy(brf[0:1, 0:512], pbias[0:1, 0:512]),
                          waits=[done] + list(BV_RAW))
                vop(lambda v, og_=og: v.tensor_tensor(
                    brf2[0:1, 0:512], brf[0:1, 0:512],
                    biasf[0:1, og_ * 512: og_ * 512 + 512], ALU.add),
                    waits=S.now("gw") + [bcw])
                dma("gpsimd", war[:, M_BR + og * 512: M_BR + og * 512 + 512],
                    bass.AP(brf2, 0, [[512, 1], [0, P], [1, 512]]),
                    waits=S.now("vv"), inc="gw")
            BROW_M = S.now("vv", "gw")
            for c in range(8):
                vop(lambda v, c_=c: v.tensor_scalar(
                    war[:, M_WFC + c_ * IH_M: M_WFC + (c_ + 1) * IH_M],
                    war[:, M_WFC + c_ * IH_M: M_WFC + (c_ + 1) * IH_M],
                    lnv[:, 16 + c_:16 + c_ + 1], None, ALU.mult),
                    waits=BROW_M + [LNV])
            WFC_OK = S.now("vv")[0]

            fc_sg = {0: None, 1: None}
            for i in range(NT):
                if H == 0:
                    XN = tb[:, M_XN:M_XN + D]
                    xn_ok = ln_apply(racc[:, ts(i, D)], XN, D, [RACC_A], [RACC_A],
                                     out_guards=S.now("pe"), jkc=M_JK)
                    for c in range(8):
                        do_transpose(XN[:, ts(c, P)], P,
                                     x2T[:, c * n_tok + i * P: c * n_tok + (i + 1) * P],
                                     [xn_ok] if c == 0 else [], "v" if c % 2 == 0 else "s")
                XN2_OK = S.now("vv", "ss")

                def fc_evac(og, pb, wdt, done, col0):
                    td = vop(lambda v, a=pb, og_=og: v.tensor_tensor(
                        s5[:, og_ % 2 * 512: og_ % 2 * 512 + 512], a[:, 0:512],
                        war[:, M_BR + og_ * 512: M_BR + og_ * 512 + 512], ALU.add),
                        waits=[done] + ([fc_sg[og % 2]] if fc_sg[og % 2] else []))
                    ev = sop(lambda s, og_=og: s.activation(
                        tb[:, M_MID + og_ * 512: M_MID + og_ * 512 + 512],
                        s5[:, og_ % 2 * 512: og_ % 2 * 512 + 512], AF.Gelu),
                        waits=[td])
                    fc_sg[og % 2] = ev
                    return ev

                project(8, lambda c: x2T[:, c * n_tok + i * P: c * n_tok + (i + 1) * P],
                        lambda c, col0, wdt: war[:, M_WFC + c * IH_M + col0:
                                                 M_WFC + c * IH_M + col0 + wdt],
                        [512] * 4, None, XN2_OK + [WFC_OK] + BROW_M, fc_evac)
                MID_OK = S.now("ss")[0]
                for c in range(NCH_M):
                    do_transpose(tb[:, M_MID + c * P: M_MID + (c + 1) * P], P,
                                 tb[:, M_MIDT + c * P: M_MIDT + (c + 1) * P],
                                 [MID_OK] if c == 0 else [], "v" if c % 2 == 0 else "s")
                MIDT_OK = S.now("vv", "ss")

                def pr_evac(og, pb, wdt, done, col0, i_=i, H_=H):
                    if H_ == 0:
                        vop(lambda v, a=pb, og_=og: v.tensor_tensor(
                            s5[:, og_ * 512: og_ * 512 + 512], a[:, 0:512],
                            war[:, M_BR + 2048 + og_ * 512: M_BR + 2048 + og_ * 512 + 512],
                            ALU.add), waits=[done] + list(BV_RAW))
                        return vop(lambda v, og_=og, j_=i_: v.tensor_tensor(
                            racc[:, j_ * D + og_ * 512: j_ * D + og_ * 512 + 512],
                            s5[:, og_ * 512: og_ * 512 + 512],
                            racc[:, j_ * D + og_ * 512: j_ * D + og_ * 512 + 512],
                            ALU.add))
                    return vop(lambda v, a=pb, og_=og, j_=i_: v.tensor_tensor(
                        racc[:, j_ * D + og_ * 512: j_ * D + og_ * 512 + 512], a[:, 0:512],
                        racc[:, j_ * D + og_ * 512: j_ * D + og_ * 512 + 512], ALU.add),
                        waits=[done])

                project(NCH_M, lambda c: tb[:, M_MIDT + c * P: M_MIDT + (c + 1) * P],
                        lambda c, col0, wdt: war[:, M_WPR + c * D + col0:
                                                 M_WPR + c * D + col0 + wdt],
                        [512] * 2, None,
                        MIDT_OK + [WPR_RAW] + list(BV_RAW), pr_evac)
                if i == 0 and H == 0 and "mid" in dmp:
                    dma("sync", dmp["mid"][0:P, :], tb[:, M_MID:M_MID + IH_M],
                        waits=[MID_OK], inc="ho")
        RACC_M = S.now("vv")[0]

        xqT = x2T
        x_slots2 = Slots(2)
        for _sl in range(2):
            for g_ in S.now("vv", "ss"):
                x_slots2.guard[_sl].append(g_)
        xload2 = [None] * NT
        if "racc15" in dmp and stop >= 4:
            dma("sync", dmp["racc15"][0:P, :], racc[:, (NT - 1) * D: NT * D],
                waits=[RACC_M], inc="ho")
        if stop >= 4:
            sl0, g0 = x_slots2.use()
            dma("sync", xt[:, ts(sl0, D)], x_hbm[0:P, :], waits=list(g0), inc="hx")
            xload2[0] = (sl0, S.now("hx")[0])
        for i in range(NT if stop >= 4 else 0):
            xsl, xrdy = xload2[i]
            if i + 1 < NT:
                sl_, g_ = x_slots2.use()
                dma("sync", xt[:, ts(sl_, D)], x_hbm[ts(i + 1, P), :], waits=list(g_), inc="hx")
                xload2[i + 1] = (sl_, S.now("hx")[0])
            xtile = xt[:, ts(xsl, D)]
            rtile = racc[:, ts(i, D)]
            sad = vop(lambda v, a=xtile, b=rtile: v.tensor_tensor(s5[:, :], a, b, ALU.add),
                      waits=[xrdy, RACC_M] + S.now("ss"))
            x_slots2.consumed(xsl, sad)
            x4_ok = ln_apply(s5[:, :], rtile, D, [sad], [sad], out_guards=[], jkc=F_JK)
            qd = quant_emit(rtile, D, tb[:, F_XQD:F_XQD + D], [x4_ok] + S.now("pe"))
            for c in range(8):
                do_transpose(tb[:, F_XQD + c * P: F_XQD + (c + 1) * P], P,
                             xqT[:, c * n_tok + i * P: c * n_tok + (i + 1) * P],
                             [qd] if c == 0 else [], "v" if c % 2 == 0 else "s")
            if i == 0:
                if "x4" in dmp:
                    dma("sync", dmp["x4"][0:P, :], rtile, waits=[x4_ok], inc="ho")
                if "xq" in dmp:
                    dma("sync", dmp["xq"][0:P, :], tb[:, F_XQD:F_XQD + D],
                        waits=[qd], inc="ho")
            if i == NT - 1:
                if "s5f" in dmp:
                    dma("sync", dmp["s5f"][0:P, :], s5[:, :], waits=[x4_ok], inc="ho")
                if "x4f" in dmp:
                    dma("sync", dmp["x4f"][0:P, :], rtile, waits=[x4_ok], inc="ho")
        XQT_OK = S.now("vv", "ss")

        stg = Slots(2)
        stgq = [("sync", "hx"), ("gpsimd", "gw")]
        nst = [0]
        for H in range(2 if stop >= 5 else 0):
            phase_gate = S.now("pe", "vv", "ss")
            nfact = [0]
            ms = {"g": [], "u": [], "d": []}
            for wkey, wdram, base in (("g", wg_n, F_WG), ("u", wu_n, F_WU)):
                for j in range(NCH_F):
                    slg, gg_ = stg.use()
                    qe, qsem = stgq[nst[0] % 2]
                    nst[0] += 1
                    dma(qe, xt[:, ts(slg, D)],
                        bass.AP(wdram, (H * IH_F + j * P) * D, [[D, P], [1, D]]),
                        waits=list(gg_) + (phase_gate if nfact[0] < 2 else []),
                        inc=qsem)
                    nfact[0] += 1
                    srdy = S.now(qsem)[0]
                    qd = quant_emit(xt[:, ts(slg, D)], D, tb[:, F_WQN:F_WQN + D],
                                    [srdy], xn32=s5[:, 0:D], v32=elu_f[:, 0:D])
                    stg.consumed(slg, qd)
                    for c in range(8):
                        do_transpose(tb[:, F_WQN + c * P: F_WQN + (c + 1) * P], P,
                                     war[:, base + c * IH_F + j * P: base + c * IH_F + (j + 1) * P],
                                     [qd] if c == 0 else [], "v" if c % 2 == 0 else "s")
                    ms[wkey].append(S.now("vv", "ss"))
            for j in range(8):
                sl0, g0 = stg.use()
                sl1, g1 = stg.use()
                qe, qsem = stgq[nst[0] % 2]
                nst[0] += 1
                dma(qe, xt[:, 0:IH_F],
                    bass.AP(wd_n, j * P * DFF + H * IH_F, [[DFF, P], [1, IH_F]]),
                    waits=list(g0) + list(g1) + (phase_gate if nfact[0] < 2 else []),
                    inc=qsem)
                nfact[0] += 1
                srdy = S.now(qsem)[0]
                HW_ = IH_F // 2
                quant_emit(xt[:, 0:HW_], HW_, tb[:, F_WQN:F_WQN + HW_], [srdy],
                           xn32=s5[:, 0:HW_], v32=elu_f[:, 0:HW_])
                qd = quant_emit(xt[:, HW_:IH_F], HW_, tb[:, F_WQN + HW_:F_WQN + IH_F],
                                [srdy], xn32=s5[:, 0:HW_], v32=elu_f[:, 0:HW_])
                stg.consumed(sl0, qd)
                stg.consumed(sl1, qd)
                for c in range(NCH_F):
                    do_transpose(tb[:, F_WQN + c * P: F_WQN + (c + 1) * P], P,
                                 war[:, F_WD + c * D + j * P: F_WD + c * D + (j + 1) * P],
                                 [qd] if c == 0 else [], "v" if c % 2 == 0 else "s")
                ms["d"].append(S.now("vv", "ss"))
            WQ_OK = S.now("vv", "ss")

            def wq_gate(wkey, col0, wdt):
                j_hi = min((col0 + wdt - 1) // P, len(ms[wkey]) - 1)
                return ms[wkey][j_hi]
            if H == 0 and "wgq" in dmp:
                dma("sync", dmp["wgq"][0:P, :], war[:, F_WG:F_WG + IH_F], waits=WQ_OK, inc="ho")
            if H == 0 and "wuq" in dmp:
                dma("sync", dmp["wuq"][0:P, :], war[:, F_WU:F_WU + 8 * IH_F], waits=WQ_OK, inc="ho")

            OGW = [512, 512, 384]
            HTB = (F_HT, F_HQ)

            def stageF(i):
                par = i % 2
                hb = HTB[par]

                def gate_evac(og, pb, wdt, done, col0):
                    w8 = ([tile_qv[0]] if tile_qv[0] else []) + \
                         ([ht_guard[par]] if ht_guard[par] else [])
                    return sop(lambda s, a=pb, c0=col0, w=wdt: s.activation(
                        tb[:, hb + c0: hb + c0 + w], a[:, 0:w], AF.Silu),
                        waits=[done] + w8)

                project(8, lambda c: xqT[:, c * n_tok + i * P: c * n_tok + (i + 1) * P],
                        lambda c, col0, wdt: war[:, F_WG + c * IH_F + col0:
                                                 F_WG + c * IH_F + col0 + wdt],
                        OGW, None, XQT_OK, gate_evac,
                        og_waits=lambda og, c0, w: wq_gate("g", c0, w))
                SG_OK = S.now("ss")[0]
                if H == 0 and i == 0 and "sg" in dmp:
                    dma("sync", dmp["sg"][0:P, :], tb[:, hb:hb + IH_F],
                        waits=[SG_OK], inc="ho")
                    qstate["sd"] = S.now("ho")[0]

                def up_evac(og, pb, wdt, done, col0):
                    return vop(lambda v, a=pb, c0=col0, w=wdt: v.tensor_tensor(
                        tb[:, hb + c0: hb + c0 + w], a[:, 0:w],
                        tb[:, hb + c0: hb + c0 + w], ALU.mult),
                        waits=[done, SG_OK])

                project(8, lambda c: xqT[:, c * n_tok + i * P: c * n_tok + (i + 1) * P],
                        lambda c, col0, wdt: war[:, F_WU + c * IH_F + col0:
                                                 F_WU + c * IH_F + col0 + wdt],
                        OGW, None, [], up_evac,
                        og_waits=lambda og, c0, w: wq_gate("u", c0, w))
                return dict(H_OK=S.now("vv")[0], hb=hb, par=par)

            def coreF(i, h):
                hb = h["hb"]
                qd = quant_emit(tb[:, hb:hb + IH_F], IH_F, tb[:, hb:hb + IH_F],
                                [h["H_OK"]])
                tile_qv[0] = qd
                for c in range(NCH_F):
                    do_transpose(tb[:, hb + c * P: hb + (c + 1) * P], P,
                                 tb[:, F_HQT + c * P: F_HQT + (c + 1) * P],
                                 [qd] if c == 0 else [], "v" if c % 2 == 0 else "s")
                ht_guard[h["par"]] = S.now("pe")[0]
                HQT_OK = S.now("vv", "ss")

                def dn_evac(og, pb, wdt, done, col0, i_=i):
                    return vop(lambda v, a=pb, og_=og, j_=i_: v.tensor_tensor(
                        racc[:, j_ * D + og_ * 512: j_ * D + og_ * 512 + 512], a[:, 0:512],
                        racc[:, j_ * D + og_ * 512: j_ * D + og_ * 512 + 512], ALU.add),
                        waits=[done])

                project(NCH_F, lambda c: tb[:, F_HQT + c * P: F_HQT + (c + 1) * P],
                        lambda c, col0, wdt: war[:, F_WD + c * D + col0:
                                                 F_WD + c * D + col0 + wdt],
                        [512] * 2, None, HQT_OK, dn_evac,
                        og_waits=lambda og, c0, w: wq_gate("d", c0, w))
                if H == 1 and stop >= 6:
                    DN_OK = S.now("vv")[0]
                    og_ = [ob_guard[0]] if ob_guard[0] else []
                    ev = ln_apply(racc[:, ts(i, D)], outb[:, :], D,
                                  [], [DN_OK] + (list(WQ_OK) if i == 0 else []),
                                  out_guards=og_, jkc=F_JK)
                    dma("sync", out_hbm[ts(i, P), :], outb[:, :], waits=[ev], inc="ho")
                    ob_guard[0] = S.now("ho")[0]

            hF = stageF(0)
            for i in range(NT):
                hN = stageF(i + 1) if i + 1 < NT else None
                coreF(i, hF)
                hF = hN
        X5_OK = S.now("vv")[0]

        if stop < 6:
            for i in range(NT):
                og_ = [ob_guard[0]] if ob_guard[0] else []
                gate = S.now("vv", "ss", "pe")
                ev = sop(lambda s_, b=racc[:, ts(i, D)]: s_.copy(outb[:, :], b),
                         waits=(gate if i == 0 else []) + og_)
                dma("sync", out_hbm[ts(i, P), :], outb[:, :], waits=[ev], inc="ho")
                ob_guard[0] = S.now("ho")[0]
        FINAL_HO = S.at("ho")

        def realize(eng_name):
            def body(eng):
                for fn, waits, inc, amt in S.ops[eng_name]:
                    for (sn, v) in waits:
                        eng.wait_ge(sems[sn], v)
                    r = fn(eng)
                    if inc is not None:
                        r.then_inc(sems[inc], amt)
                if eng_name == "sync":
                    eng.wait_ge(sems["ho"], FINAL_HO)
            return body

        blk.sync(realize("sync"))
        blk.gpsimd(realize("gpsimd"))
        blk.tensor(realize("tensor"))
        blk.vector(realize("vector"))
        blk.scalar(realize("scalar"))

    return nc


def _prep_inputs(L0_attn_x, L0_attn_ln1_w, L0_attn_ln1_b, L0_attn_q_w, L0_attn_q_b,
                 L0_attn_k_w, L0_attn_k_b, L0_attn_v_w, L0_attn_v_b, L0_attn_o_w,
                 L0_attn_o_b, L0_attn_ln2_w, L0_attn_ln2_b, L0_attn_fc_w, L0_attn_fc_b,
                 L0_attn_proj_w, L0_attn_proj_b, L0_ffn_w_gate, L0_ffn_w_up, L0_ffn_w_down):
    f = np.float32
    c = np.ascontiguousarray

    def t(a):
        return c(np.asarray(a).T.astype(f, copy=False))

    base = {
        "wqkv_t": c(np.concatenate([np.asarray(L0_attn_q_w), np.asarray(L0_attn_k_w),
                                    np.asarray(L0_attn_v_w)], axis=0).T.astype(f, copy=False)),
        "wo_t": t(L0_attn_o_w),
        "wfc_t": t(L0_attn_fc_w),
        "wproj_t": t(L0_attn_proj_w),
        "wg_n": c(np.asarray(L0_ffn_w_gate).astype(f, copy=False)),
        "wu_n": c(np.asarray(L0_ffn_w_up).astype(f, copy=False)),
        "wd_n": c(np.asarray(L0_ffn_w_down).astype(f, copy=False)),
        "ln1w": c(np.asarray(L0_attn_ln1_w).reshape(8, P).T.astype(f, copy=False)),
        "ln1b": c(np.asarray(L0_attn_ln1_b).reshape(8, P).T.astype(f, copy=False)),
        "ln2w": c(np.asarray(L0_attn_ln2_w).reshape(8, P).T.astype(f, copy=False)),
        "ln2b": c(np.asarray(L0_attn_ln2_b).reshape(8, P).T.astype(f, copy=False)),
        "bqkv": c(np.concatenate([np.asarray(L0_attn_q_b), np.asarray(L0_attn_k_b),
                                  np.asarray(L0_attn_v_b)]).reshape(1, -1).astype(f, copy=False)),
        "bo": c(np.asarray(L0_attn_o_b).reshape(1, -1).astype(f, copy=False)),
        "bfc": c(np.asarray(L0_attn_fc_b).reshape(1, -1).astype(f, copy=False)),
        "bproj": c(np.asarray(L0_attn_proj_b).reshape(1, -1).astype(f, copy=False)),
    }
    xfull = np.ascontiguousarray(np.asarray(L0_attn_x).astype(f, copy=False).reshape(-1, D))
    return xfull, base


def kernel(**inputs):
    xfull, base = _prep_inputs(**inputs)
    n_tok = xfull.shape[0] // NCORES
    shards = np.split(xfull, NCORES, axis=0)
    nc = build(n_tok=n_tok)
    in_maps = [dict(base, x=np.ascontiguousarray(s)) for s in shards]
    res = run_bass_kernel_spmd(nc, in_maps, core_ids=list(range(NCORES)))
    out = np.concatenate([np.asarray(r["out"]) for r in res.results], axis=0)
    return out.reshape(np.asarray(inputs["L0_attn_x"]).shape)


# version 21
# speedup vs baseline: 1.0159x; 1.0159x over previous
import numpy as np
import concourse.bass as bass
import concourse.mybir as mybir
from concourse.bass_utils import run_bass_kernel_spmd

F32 = mybir.dt.float32
BF16 = mybir.dt.bfloat16
AF = mybir.ActivationFunctionType
ALU = mybir.AluOpType
AX = mybir.AxisListType

P = 128
D = 1024
NH = 16
HD = 64
DFC = 4096
DFF = 2816
EPS = 1e-5
NCORES = 8

ENGS = ("sync", "gpsimd", "tensor", "vector", "scalar")


class Sched:
    def __init__(self):
        self.ops = {e: [] for e in ENGS}
        self.counts = {}

    def at(self, sem):
        return self.counts.get(sem, 0)

    def now(self, *sems):
        return [(s, self.at(s)) for s in sems]

    def emit(self, eng, fn, waits=(), inc=None, amt=1):
        ws = [(s, v) for (s, v) in waits if v > 0]
        self.ops[eng].append((fn, ws, inc, amt))
        if inc is not None:
            self.counts[inc] = self.counts.get(inc, 0) + amt
        return (inc, self.counts.get(inc, 0)) if inc else None


class Slots:
    def __init__(self, n):
        self.n = n
        self.guard = [[] for _ in range(n)]
        self.i = 0

    def use(self):
        s = self.i % self.n
        self.i += 1
        w = self.guard[s]
        self.guard[s] = []
        return s, w

    def consumed(self, s, sem_val):
        if sem_val is not None:
            self.guard[s].append(sem_val)


def ts(i, n):
    return slice(i * n, (i + 1) * n)


def build(n_tok=2048, dump=(), stop=6, asub=7):
    assert n_tok % P == 0
    NT = n_tok // P
    IH_M = DFC // 2
    NCH_M = IH_M // P
    IH_F = DFF // 2
    NCH_F = IH_F // P

    nc = bass.Bass("TRN2", target_bir_lowering=False, detect_race_conditions=False)

    x_hbm = nc.dram_tensor("x", [n_tok, D], F32, kind="ExternalInput")
    wqkv = nc.dram_tensor("wqkv_t", [D, 3 * D], F32, kind="ExternalInput")
    wo = nc.dram_tensor("wo_t", [D, D], F32, kind="ExternalInput")
    wfc = nc.dram_tensor("wfc_t", [D, DFC], F32, kind="ExternalInput")
    wproj = nc.dram_tensor("wproj_t", [DFC, D], F32, kind="ExternalInput")
    wg_n = nc.dram_tensor("wg_n", [DFF, D], F32, kind="ExternalInput")
    wu_n = nc.dram_tensor("wu_n", [DFF, D], F32, kind="ExternalInput")
    wd_n = nc.dram_tensor("wd_n", [D, DFF], F32, kind="ExternalInput")
    ln1w = nc.dram_tensor("ln1w", [P, 8], F32, kind="ExternalInput")
    ln1b = nc.dram_tensor("ln1b", [P, 8], F32, kind="ExternalInput")
    ln2w = nc.dram_tensor("ln2w", [P, 8], F32, kind="ExternalInput")
    ln2b = nc.dram_tensor("ln2b", [P, 8], F32, kind="ExternalInput")
    bqkv = nc.dram_tensor("bqkv", [1, 3 * D], F32, kind="ExternalInput")
    bo = nc.dram_tensor("bo", [1, D], F32, kind="ExternalInput")
    bfc = nc.dram_tensor("bfc", [1, DFC], F32, kind="ExternalInput")
    bproj = nc.dram_tensor("bproj", [1, D], F32, kind="ExternalInput")
    out_hbm = nc.dram_tensor("out", [n_tok, D], F32, kind="ExternalOutput")
    dmp = {}
    for name, shape, dt in dump:
        dmp[name] = nc.dram_tensor("dump_" + name, shape, dt, kind="ExternalOutput")

    from contextlib import ExitStack
    with ExitStack() as ctx:
        e = ctx.enter_context

        racc = e(nc.sbuf_tensor("racc", [P, NT * D], BF16))
        x2T = e(nc.sbuf_tensor("x2T", [P, 8 * n_tok], BF16))
        WARC = 36992
        war = e(nc.sbuf_tensor("war", [P, WARC], BF16))
        TBC = 17664
        tb = e(nc.sbuf_tensor("tb", [P, TBC], BF16))
        xt = e(nc.sbuf_tensor("xt", [P, 2 * D], F32))
        s5 = e(nc.sbuf_tensor("s5", [P, D], F32))
        elu_f = e(nc.sbuf_tensor("elu_f", [P, 1024], F32))
        outb = elu_f
        stage = None
        st = e(nc.sbuf_tensor("st", [P, 48], F32))
        qs = e(nc.sbuf_tensor("qs", [P, 3 * 96], F32))
        ident = e(nc.sbuf_tensor("ident", [P, P], BF16))
        maskS = e(nc.sbuf_tensor("maskS", [P, P], BF16))
        lnv = e(nc.sbuf_tensor("lnv", [P, 32], F32))
        lnb_bf = e(nc.sbuf_tensor("lnb_bf", [P, 16], BF16))
        brf = e(nc.sbuf_tensor("brf", [1, 512], F32))
        brf2 = e(nc.sbuf_tensor("brf2", [1, 512], F32))
        biasf = e(nc.sbuf_tensor("biasf", [1, 3 * D], BF16))

        pb0 = e(nc.psum_tensor("pb0", [P, 512], F32))
        pb1 = e(nc.psum_tensor("pb1", [P, 512], F32))
        ptr0 = e(nc.psum_tensor("ptr0", [P, 1024], BF16))
        ptr1 = e(nc.psum_tensor("ptr1", [P, 1024], BF16))
        ptr2 = e(nc.psum_tensor("ptr2", [P, 1024], BF16))
        ps1a = e(nc.psum_tensor("ps1a", [P, 512], F32))
        ps1b = e(nc.psum_tensor("ps1b", [P, 512], F32))
        ps2 = e(nc.psum_tensor("ps2", [P, 512], F32))
        pbias = ps2

        sems = {}
        for sname in ("hx", "ho", "gw", "gm", "pe", "vv", "ss", "gg"):
            sems[sname] = e(nc.semaphore(sname))
        blk = e(nc.Block())

        S = Sched()
        PB = (pb0, pb1)
        pbig = Slots(2)
        ptrs = Slots(3)
        PTR = None
        ps1s = Slots(2)
        TBP = TBC

        A_XN, A_XNT, A_QKV, A_QT, A_KT, A_VG, A_SS, A_AM, A_AT, A_ER, A_EE, A_JK = (
            0, 1024, 2048, 5120, 7168, 9216, 10256, 10512, 11536, 13584, 14608, 15632)
        M_XN, M_MID, M_MIDT, M_JK = 0, 1024, 3072, 15632
        F_XQD, F_QXN, F_QV, F_B1, F_B2, F_B3, F_B4, F_B5, F_WQN, F_HT, F_HQ, F_HQT, F_JK = (
            0, 1024, 2432, 3840, 5248, 6656, 8064, 9472, 10880, 12288, 13696, 15104, 16512)
        A_WQKV, A_WO, A_BR = 0, 24576, 32768
        M_WFC, M_WPR, M_BR = 0, 16384, 32768
        F_WG, F_WU, F_WD = 0, 11264, 22528

        def mm(psum_ap, lhsT_ap, rhs_ap, start, stop, waits=(), inc=None, tp=None):
            def fn(t, a=psum_ap, b=lhsT_ap, c=rhs_ap, st_=start, sp=stop, tp_=tp):
                return t.matmul(a, b, c, start=st_, stop=sp, tile_position=tp_)
            return S.emit("tensor", fn, waits, inc)

        def vop(fn, waits=()):
            return S.emit("vector", fn, waits, "vv")

        def sop(fn, waits=()):
            return S.emit("scalar", fn, waits, "ss")

        def gop(fn, waits=()):
            return S.emit("gpsimd", fn, waits, "gg")

        def dma(eng, out_ap, in_ap, waits=(), inc="hx"):
            def fn(g, a=out_ap, b=in_ap):
                return g.dma_start(out=a, in_=b)
            return S.emit(eng, fn, waits, inc, amt=16)

        gop(lambda g: g.memset(ident[:, :], 0.0))
        gop(lambda g: g.affine_select(out=ident[:, :], in_=ident[:, :],
                                      compare_op=ALU.not_equal, fill=1.0, base=0,
                                      pattern=[[-1, P]], channel_multiplier=1),
            waits=S.now("gg"))
        Bm = tb[0:8, 0:P]
        gop(lambda g: g.memset(Bm, 1.0), waits=S.now("gg"))
        gop(lambda g: g.affine_select(out=Bm, in_=Bm, compare_op=ALU.is_ge,
                                      fill=0.0, base=0, pattern=[[1, P]],
                                      channel_multiplier=-16), waits=S.now("gg"))
        gop(lambda g: g.affine_select(out=Bm, in_=Bm, compare_op=ALU.is_ge,
                                      fill=0.0, base=15, pattern=[[-1, P]],
                                      channel_multiplier=16), waits=S.now("gg"))
        def _mask_mm(t):
            return t.matmul(ps1a[:, 0:P], Bm, Bm, start=True, stop=True)
        S.emit("tensor", _mask_mm, S.now("gg"), "pe")
        vop(lambda v: v.tensor_copy(maskS[:, :], ps1a[:, 0:P]), waits=S.now("pe"))
        for G in range(16):
            gop(lambda g, G_=G: g.memset(tb[:, A_VG + G_ * 65 + 64: A_VG + G_ * 65 + 65], 1.0))
        CONST = S.now("gg")[0]
        CONSTV = S.now("vv")[0]

        dma("sync", lnv[:, 0:8], ln1w[:, :])
        dma("sync", lnv[:, 8:16], ln1b[:, :])
        dma("sync", lnv[:, 16:24], ln2w[:, :])
        dma("sync", lnv[:, 24:32], ln2b[:, :])
        LNV = S.now("hx")[0]
        dma("gpsimd", lnb_bf[:, 0:8], ln1b[:, :], inc="gw")
        dma("gpsimd", lnb_bf[:, 8:16], ln2b[:, :], inc="gw")
        LNBF = S.now("gw")[0]

        ln_state = {"i": 0, "act": {0: None, 1: None}}

        def ln_apply(src_ap, out_ap, width, waits_v, waits_s, out_guards=(), jkc=None):
            n = ln_state["i"]
            ln_state["i"] += 1
            b = 16 * (n % 2)
            g = ln_state["act"][n % 2]
            gg = [g] if g else []
            jk = tb[:, jkc:jkc + width]
            c_sum, c_sq, c_mu, c_var, c_t, c_inv, c_nmi = (b, b + 2, b + 4, b + 6,
                                                           b + 8, b + 10, b + 12)
            red = vop(lambda v, a=src_ap: v.reduce_sum(st[:, c_sum:c_sum + 1], a,
                                                       axis=AX.X),
                      waits=list(waits_v) + gg)
            sop(lambda s, a=src_ap: s.activation(jk, a, AF.Square,
                                                 accum_out=st[:, c_sq:c_sq + 1]),
                waits=list(waits_s) + gg)
            sqd = S.now("ss")[0]
            w1 = vop(lambda v: v.tensor_scalar(st[:, c_mu:c_mu + 1],
                                               st[:, c_sum:c_sum + 1],
                                               1.0 / width, None, ALU.mult),
                     waits=[red])
            w2 = vop(lambda v: v.tensor_scalar(st[:, c_var:c_var + 1],
                                               st[:, c_sq:c_sq + 1],
                                               1.0 / width, None, ALU.mult),
                     waits=[sqd])
            w3 = vop(lambda v: v.tensor_mul(st[:, c_t:c_t + 1], st[:, c_mu:c_mu + 1],
                                            st[:, c_mu:c_mu + 1]), waits=[w1])
            w4 = vop(lambda v: v.tensor_tensor(st[:, c_var:c_var + 1],
                                               st[:, c_var:c_var + 1],
                                               st[:, c_t:c_t + 1], ALU.subtract),
                     waits=[w2, w3])
            ved = vop(lambda v: v.tensor_scalar(st[:, c_var:c_var + 1],
                                                st[:, c_var:c_var + 1],
                                                EPS, None, ALU.add), waits=[w4])
            sop(lambda s: s.activation(st[:, c_t:c_t + 1], st[:, c_var:c_var + 1],
                                       AF.Ln), waits=[ved])
            sop(lambda s: s.activation(st[:, c_inv:c_inv + 1], st[:, c_t:c_t + 1],
                                       AF.Exp, scale=np.float32(-0.5)))
            inv = S.now("ss")[0]
            w5 = vop(lambda v: v.tensor_mul(st[:, c_nmi:c_nmi + 1],
                                            st[:, c_mu:c_mu + 1],
                                            st[:, c_inv:c_inv + 1]),
                     waits=[inv, w1])
            nmi = vop(lambda v: v.tensor_scalar(st[:, c_nmi:c_nmi + 1],
                                                st[:, c_nmi:c_nmi + 1],
                                                -1.0, None, ALU.mult), waits=[w5])
            act = sop(lambda s, a=src_ap, o=out_ap: s.activation(
                o, a, AF.Identity, bias=st[:, c_nmi:c_nmi + 1],
                scale=st[:, c_inv:c_inv + 1]),
                waits=[nmi] + list(out_guards))
            ln_state["act"][n % 2] = act
            return act

        def do_transpose(src_ap, width, dst_ap, waits, ee):
            sl, g = ptrs.use()
            pt = (ptr0, ptr1, ptr2)[sl]
            transpose_ap = pt[0:width, 0:P]

            def fn(t, a=transpose_ap, b=src_ap):
                return t.transpose(a, b, ident[:, :])
            S.emit("tensor", fn, list(waits) + list(g), "pe")
            ped = S.now("pe")[0]
            if ee == "v":
                ev = vop(lambda v, a=dst_ap, b=transpose_ap: v.tensor_copy(a, b),
                         waits=[ped])
            else:
                ev = sop(lambda s, a=dst_ap, b=transpose_ap: s.copy(a, b), waits=[ped])
            ptrs.consumed(sl, ev)
            return ev

        def project(nchunks, lhsT_of, rhs_of, out_widths, bias_col, first_waits, evac,
                    og_waits=None):
            col0 = 0
            for og, wdt in enumerate(out_widths):
                sl, guard = pbig.use()
                pb = PB[sl]
                fw = list(first_waits) + list(guard)
                if og_waits is not None:
                    fw += list(og_waits(og, col0, wdt))
                first = True
                for c in range(nchunks):
                    last = c == nchunks - 1
                    mm(pb[:, 0:wdt], lhsT_of(c), rhs_of(c, col0, wdt), first, last,
                       waits=fw if first else [], inc="pe" if last else None)
                    first, fw = False, []
                done = S.now("pe")[0]
                ev = evac(og, pb, wdt, done, col0)
                pbig.consumed(sl, ev)
                col0 += wdt

        def quant_emit(src2d, W, dst2d, waits_v, xn32=None, v32=None):
            nb = W // 16
            src3 = src2d.rearrange("p (b k) -> p b k", k=16)
            QXN = xn32 if xn32 is not None else tb[:, F_QXN:F_QXN + W]
            QV = v32 if v32 is not None else tb[:, F_QV:F_QV + W]
            B1 = tb[:, F_B1:F_B1 + W]
            B2 = tb[:, F_B2:F_B2 + W]
            B3 = tb[:, F_B3:F_B3 + W]
            B4 = tb[:, F_B4:F_B4 + W]
            B5 = tb[:, F_B5:F_B5 + W]
            bmr = vop(lambda v: v.tensor_reduce(qs[:, 0:nb], src3, axis=AX.X,
                                                op=ALU.max, apply_absolute_value=True),
                      waits=waits_v)
            scw = vop(lambda v: v.tensor_scalar(qs[:, 96:96 + nb], qs[:, 0:nb],
                                                1.0 / 6.0, 1e-12, ALU.mult, ALU.max),
                      waits=[bmr])
            rsr = vop(lambda v: v.reciprocal(qs[:, 192:192 + nb], qs[:, 96:96 + nb]),
                      waits=[scw])
            pv = qstate["sd"]
            vop(lambda v: v.tensor_tensor(
                QXN.rearrange("p (b k) -> p b k", k=16), src3,
                qs[:, 192:192 + nb].unsqueeze(2).broadcast_to([P, nb, 16]), ALU.mult),
                waits=([pv] if pv else []) + [rsr])
            xnd = S.now("vv")[0]
            sop(lambda s: s.activation(QV, QXN, AF.Abs), waits=[xnd])
            sop(lambda s: s.activation(B5, QXN, AF.Sign))
            sd = S.now("ss")[0]
            pvg = qstate["s_guard"]
            qstate["sd"] = sd
            NUD = 1.0 + 2.0 ** -40
            if True:
                vop(lambda v: v.tensor_scalar(B1, QV, 0.25 * NUD, None, ALU.is_gt),
                    waits=[sd])
                vop(lambda v: v.tensor_scalar(B2, QV, 0.75 * NUD, None, ALU.is_gt))
                vop(lambda v: v.tensor_add(B1, B1, B2))
                vop(lambda v: v.tensor_scalar(B2, QV, 1.25 * NUD, None, ALU.is_gt))
                vop(lambda v: v.tensor_scalar(B3, QV, 1.75 * NUD, None, ALU.is_gt))
                vop(lambda v: v.tensor_add(B2, B2, B3))
                vop(lambda v: v.tensor_add(B1, B1, B2))
                vop(lambda v: v.tensor_scalar(B2, QV, 2.5 * NUD, None, ALU.is_gt))
                vop(lambda v: v.tensor_scalar(B3, QV, 3.5 * NUD, None, ALU.is_gt))
                vop(lambda v: v.tensor_scalar(B4, QV, 5.0 * NUD, None, ALU.is_gt))
                vop(lambda v: v.scalar_tensor_tensor(B3, B4, 2.0, B3, ALU.mult, ALU.add))
                vop(lambda v: v.tensor_add(B3, B3, B2))
                vop(lambda v: v.scalar_tensor_tensor(B1, B1, 0.5, B3, ALU.mult, ALU.add))
            vop(lambda v: v.tensor_mul(B1, B1, B5))
            qv = vop(lambda v, d=dst2d: v.tensor_tensor(
                d.rearrange("p (b k) -> p b k", k=16),
                B1.rearrange("p (b k) -> p b k", k=16),
                qs[:, 96:96 + nb].unsqueeze(2).broadcast_to([P, nb, 16]), ALU.mult),
                waits=S.now("pe"))
            qstate["s_guard"] = qv
            return qv

        qstate = {"s_guard": None, "sd": None}
        tile_qv = [None]
        ht_guard = [None, None]
        ob_guard = [None]

        for c in range(8):
            dma("gpsimd", war[:, A_WQKV + c * 3072: A_WQKV + (c + 1) * 3072],
                bass.AP(wqkv, c * P * 3 * D, [[3 * D, P], [1, 3 * D]]), inc="gw")
        WQKV_RAW = S.now("gw")[0]
        for c in range(8):
            dma("gpsimd", war[:, A_WO + c * D: A_WO + (c + 1) * D],
                bass.AP(wo, c * P * D, [[D, P], [1, D]]), inc="gw")
        dma("gpsimd", biasf[0:1, 0:3072], bqkv[:, :], inc="gw")
        dma("gpsimd", war[:, A_BR + 3072: A_BR + 4096],
            bass.AP(bo, 0, [[D, 1], [0, P], [1, D]]), inc="gw")
        WA_RAW = S.now("gw")[0]
        for og in range(6):
            for c in range(8):
                mm(pbias[0:1, 0:512], lnb_bf[:, c:c + 1],
                   war[:, A_WQKV + c * 3072 + og * 512: A_WQKV + c * 3072 + og * 512 + 512],
                   c == 0, c == 7,
                   waits=([LNBF, WQKV_RAW] + S.now("vv")) if c == 0 else [],
                   inc="pe" if c == 7 else None)
            done = S.now("pe")[0]
            bcw = vop(lambda v: v.tensor_copy(brf[0:1, 0:512], pbias[0:1, 0:512]),
                      waits=[done, WA_RAW])
            vop(lambda v, og_=og: v.tensor_tensor(
                brf2[0:1, 0:512], brf[0:1, 0:512],
                biasf[0:1, og_ * 512: og_ * 512 + 512], ALU.add),
                waits=S.now("gw") + [bcw])
            dma("gpsimd", war[:, A_BR + og * 512: A_BR + og * 512 + 512],
                bass.AP(brf2, 0, [[512, 1], [0, P], [1, 512]]),
                waits=S.now("vv"), inc="gw")
        BROW_A = S.now("vv", "gw")
        for c in range(8):
            vop(lambda v, c_=c: v.tensor_scalar(
                war[:, A_WQKV + c_ * 3072: A_WQKV + (c_ + 1) * 3072],
                war[:, A_WQKV + c_ * 3072: A_WQKV + (c_ + 1) * 3072],
                lnv[:, c_:c_ + 1], None, ALU.mult),
                waits=BROW_A + [LNV])
        WQKV_OK = S.now("vv")[0]

        x_slots = Slots(2)

        def load_x(i, slots):
            sl, g = slots.use()
            dma("sync", xt[:, ts(sl, D)], x_hbm[ts(i, P), :], waits=g, inc="hx")
            return sl, S.now("hx")[0]

        xload = [load_x(0, x_slots)] if stop >= 2 else []
        elu_guard = {0: None, 1: None}
        rec_guard = [None] * 4
        vg_guard = [None]

        stA = {}
        um_guard = [None]

        def emit_stageA(i):
            xsl, xrdy = xload[i]
            xtile = xt[:, ts(xsl, D)]
            XN = tb[:, A_XN:A_XN + D]
            gm_prev = S.now("gm")[0]
            xn_ok = ln_apply(xtile, XN, D, [xrdy], [xrdy], out_guards=S.now("pe"), jkc=A_JK)
            if i == 0 and "xn" in dmp:
                dma("sync", dmp["xn"][0:P, :], XN, waits=[xn_ok], inc="ho")
            if i == 0 and "st" in dmp:
                dma("sync", dmp["st"][0:P, :], st[:, 0:32], waits=[xn_ok], inc="ho")
            for c in range(8):
                do_transpose(XN[:, ts(c, P)], P, tb[:, A_XNT + c * P: A_XNT + (c + 1) * P],
                             [xn_ok] if c == 0 else [], "v" if c % 2 == 0 else "s")
            XNT_OK = S.now("vv", "ss")

            def qkv_evac(og, pb, wdt, done, col0):
                bias = war[:, A_BR + og * 512: A_BR + og * 512 + 512]
                if og < 4:
                    par = og % 2
                    eg = elu_guard[par]
                    ef = elu_f[:, par * 512:(par + 1) * 512]
                    er = tb[:, A_ER + par * 512: A_ER + (par + 1) * 512]
                    ee = tb[:, A_EE + par * 512: A_EE + (par + 1) * 512]
                    td = vop(lambda v, a=pb, ef_=ef, b_=bias: v.tensor_tensor(
                        ef_, a[:, 0:512], b_, ALU.add),
                        waits=[done] + (list(eg) if eg else []))
                    rd = sop(lambda s, ef_=ef, er_=er: s.activation(er_, ef_, AF.Relu),
                             waits=[td])
                    md = vop(lambda v, ef_=ef: v.tensor_scalar(
                        ef_, ef_, 0.0, None, ALU.min), waits=[rd])
                    ed = sop(lambda s, ef_=ef, ee_=ee: s.activation(ee_, ef_, AF.Exp),
                             waits=[md])
                    ad = vop(lambda v, og_=og, er_=er, ee_=ee: v.tensor_add(
                        tb[:, A_QKV + og_ * 512:A_QKV + og_ * 512 + 512], er_, ee_),
                        waits=[ed, gm_prev])
                    elu_guard[par] = (ed, ad)
                    return ad
                else:
                    return vop(lambda v, a=pb, og_=og, b_=bias: v.tensor_tensor(
                        tb[:, A_QKV + og_ * 512:A_QKV + og_ * 512 + 512], a[:, 0:512],
                        b_, ALU.add),
                        waits=[done, gm_prev])

            if asub >= 2:
                def qkv_evac_plain(og, pb, wdt, done, col0):
                    return sop(lambda s_, a=pb, og_=og: s_.copy(
                        tb[:, A_QKV + og_ * 512:A_QKV + og_ * 512 + 512], a[:, 0:512]),
                        waits=[done, gm_prev])
                _ev = qkv_evac if asub not in (20, 21) else qkv_evac_plain
                project(8, lambda c: tb[:, A_XNT + c * P: A_XNT + (c + 1) * P],
                        lambda c, col0, wdt: war[:, A_WQKV + c * 3072 + col0:
                                                 A_WQKV + c * 3072 + col0 + wdt],
                        [512] * 6, None, XNT_OK + [WQKV_OK] + BROW_A, _ev)
            QKV_OK = S.now("vv", "ss")
            if i == 0 and "qkv" in dmp:
                dma("sync", dmp["qkv"][0:P, :], tb[:, A_QKV:A_QKV + 3 * D],
                    waits=QKV_OK, inc="ho")
            return dict(xsl=xsl, xrdy=xrdy, xtile=xtile, QKV_OK=QKV_OK)

        def emit_marshalA(i):
            h = stA[i]
            QKV_OK = h["QKV_OK"]
            for hh in range(NH if asub >= 3 else 0):
                do_transpose(tb[:, A_QKV + hh * HD: A_QKV + (hh + 1) * HD], HD,
                             bass.AP(tb, A_QT + hh, [[TBP, 64], [P, 16], [16, 8]]),
                             QKV_OK if hh == 0 else [], "v" if hh % 2 else "s")
            for hh in range(NH if asub >= 3 else 0):
                do_transpose(tb[:, A_QKV + D + hh * HD: A_QKV + D + (hh + 1) * HD], HD,
                             bass.AP(tb, A_KT + hh, [[TBP, 64], [P, 16], [16, 8]]),
                             [], "s" if hh % 2 else "v")
            h["QKT_OK"] = S.now("vv", "ss")

            vm_first = QKV_OK + ([vg_guard[0]] if vg_guard[0] else []) + [CONST]
            h["gm_base"] = S.at("gm")
            for G in range(16 if asub >= 4 else 0):
                dma("gpsimd",
                    bass.AP(tb, A_VG + G * 65, [[TBP, P], [1, HD]]),
                    bass.AP(tb, (8 * G) * TBP + A_QKV + 2 * D, [[TBP, 8], [HD, NH], [1, HD]]),
                    waits=vm_first if G == 0 else [], inc="gm")

        def emit_coreA(i):
            h = stA[i]
            QKT_OK = h["QKT_OK"]
            gm_base = h["gm_base"]
            xtile = h["xtile"]
            xrdy = h["xrdy"]
            evms = [None] * 16
            am_evs = [None] * 16
            amg = [um_guard[0]] if um_guard[0] else []
            NG = 16 if asub >= 5 else 0
            UNM = asub >= 6

            def unmarshal(G):
                for pb_ in (0, 64):
                    sl, g = ptrs.use()
                    pt = (ptr0, ptr1, ptr2)[sl]
                    tout = pt[pb_:pb_ + 64, 0:P]
                    src_t = tb[:, A_AM + G * HD: A_AM + (G + 1) * HD]

                    def fn(t, a=tout, b=src_t):
                        return t.transpose(a, b, ident[:, :])
                    S.emit("tensor", fn,
                           ([am_evs[G]] if pb_ == 0 else []) + list(g), "pe")
                    ped = S.now("pe")[0]
                    par = pb_ // 64
                    src_e = bass.AP(pt, pb_ * 1024 + par,
                                    [[1024, 64], [16, 8], [2, 8]])
                    dst_e = bass.AP(tb, pb_ * TBP + A_AT + 8 * G,
                                    [[TBP, 64], [1, 8], [P, 8]])
                    if (G + par) % 2 == 0:
                        ev = vop(lambda v, a=dst_e, b=src_e: v.tensor_copy(a, b),
                                 waits=[ped])
                    else:
                        ev = sop(lambda s_, a=dst_e, b=src_e: s_.copy(a, b), waits=[ped])
                    ptrs.consumed(sl, ev)

            for G in range(NG + 1):
                if G < NG:
                    sl1, g1 = ps1s.use()
                    p1 = (ps1a, ps1b)[sl1]
                    mm(p1[:, 0:P], tb[0:64, A_KT + G * P: A_KT + (G + 1) * P],
                       tb[0:64, A_QT + G * P: A_QT + (G + 1) * P], True, True,
                       waits=(QKT_OK if G == 0 else []) + list(g1), inc="pe")
                    s1d = S.now("pe")[0]
                    ssl = G % 2
                    evms[G] = vop(lambda v, a=p1[:, 0:P], ssl_=ssl: v.tensor_tensor(
                        tb[:, A_SS + ssl_ * P: A_SS + (ssl_ + 1) * P], a, maskS[:, :],
                        ALU.mult), waits=[s1d, CONST, CONSTV])
                    ps1s.consumed(sl1, evms[G])
                if G >= 1:
                    Gp = G - 1
                    sl2, g2 = pbig.use()
                    pb2 = PB[sl2]
                    mm(pb2[:, 0:65],
                       tb[:, A_SS + (Gp % 2) * P: A_SS + (Gp % 2 + 1) * P],
                       tb[:, A_VG + Gp * 65: A_VG + Gp * 65 + 65],
                       True, True,
                       waits=[evms[Gp], ("gm", gm_base + 16 * (Gp + 1))] + list(g2),
                       inc="pe")
                    s2d = S.now("pe")[0]
                    r = Gp % 4
                    rg = rec_guard[r]
                    rv = vop(lambda v, a=pb2[:, 64:65], r_=r: v.reciprocal(
                        st[:, 32 + 2 * r_:33 + 2 * r_], a),
                        waits=[s2d] + ([rg] if rg else []))
                    ev2 = sop(lambda s, a=pb2[:, 0:64], G_=Gp, r_=r: s.activation(
                        tb[:, A_AM + G_ * HD: A_AM + (G_ + 1) * HD], a, AF.Identity,
                        scale=st[:, 32 + 2 * r_:33 + 2 * r_]),
                        waits=[rv] + (amg if Gp == 0 else []))
                    am_evs[Gp] = ev2
                    rec_guard[r] = ev2
                    pbig.consumed(sl2, ev2)
                    if UNM and Gp >= 2:
                        unmarshal(Gp - 2)
            if NG:
                vg_guard[0] = S.now("pe")[0]
            AM_OK = S.now("ss")[0]
            if UNM:
                for G in (13, 14, 15):
                    unmarshal(G)
            if UNM:
                um_guard[0] = S.now("pe")[0]
            AT_OK = S.now("vv", "ss")

            def o_evac(og, pb, wdt, done, col0, i_=i, xp=xtile, xr=xrdy):
                vop(lambda v, a=pb, og_=og: v.tensor_tensor(
                    s5[:, og_ * 512: og_ * 512 + 512], a[:, 0:512],
                    war[:, A_BR + 3072 + og_ * 512: A_BR + 3072 + og_ * 512 + 512],
                    ALU.add), waits=[done, WA_RAW])
                return vop(lambda v, og_=og, j_=i_, x_=xp: v.tensor_tensor(
                    racc[:, j_ * D + og_ * 512: j_ * D + og_ * 512 + 512],
                    s5[:, og_ * 512: og_ * 512 + 512],
                    x_[:, og_ * 512: og_ * 512 + 512], ALU.add),
                    waits=[xr])

            if asub >= 7:
                project(8, lambda c: tb[:, A_AT + c * P: A_AT + (c + 1) * P],
                        lambda c, col0, wdt: war[:, A_WO + c * D + col0:
                                                 A_WO + c * D + col0 + wdt],
                        [512] * 2, A_BR + 3072, AT_OK + [WA_RAW], o_evac)
            else:
                vop(lambda v, i_=i, xp=xtile: v.tensor_copy(
                    racc[:, ts(i_, D)], xp), waits=[xrdy])
            x_slots.consumed(h["xsl"], S.now("vv")[0])
            x_slots.consumed(h["xsl"], S.now("ss")[0])
            if i == 0:
                if "attn" in dmp:
                    dma("sync", dmp["attn"][0:P, :], tb[:, A_AM:A_AM + D],
                        waits=[AM_OK], inc="ho")
                if "racc_a" in dmp:
                    dma("sync", dmp["racc_a"][0:P, :], racc[:, 0:D],
                        waits=S.now("vv"), inc="ho")

        for i in range(NT if stop >= 2 else 0):
            if i + 1 < NT:
                xload.append(load_x(i + 1, x_slots))
            if i == 0:
                stA[0] = emit_stageA(0)
            emit_marshalA(i)
            if i + 1 < NT:
                stA[i + 1] = emit_stageA(i + 1)
            emit_coreA(i)
            del stA[i]
        RACC_A = S.now("vv")[0]

        for H in range(2 if stop >= 3 else 0):
            phase_gate = S.now("pe", "vv", "ss")
            for c in range(8):
                dma("gpsimd", war[:, M_WFC + c * IH_M: M_WFC + (c + 1) * IH_M],
                    bass.AP(wfc, c * P * DFC + H * IH_M, [[DFC, P], [1, IH_M]]),
                    waits=phase_gate if c == 0 else [], inc="gw")
            WFC_RAW = S.now("gw")[0]
            for c in range(NCH_M):
                dma("gpsimd", war[:, M_WPR + c * D: M_WPR + (c + 1) * D],
                    bass.AP(wproj, (H * IH_M + c * P) * D, [[D, P], [1, D]]), inc="gw")
            WPR_RAW = S.now("gw")[0]
            dma("gpsimd", biasf[0:1, 0:IH_M],
                bass.AP(bfc, H * IH_M, [[1, 1], [1, IH_M]]), inc="gw")
            if H == 0:
                dma("gpsimd", war[:, M_BR + 2048: M_BR + 3072],
                    bass.AP(bproj, 0, [[D, 1], [0, P], [1, D]]), inc="gw")
            BV_RAW = S.now("gw", "gg")
            for og in range(4):
                for c in range(8):
                    mm(pbias[0:1, 0:512], lnb_bf[:, 8 + c:8 + c + 1],
                       war[:, M_WFC + c * IH_M + og * 512: M_WFC + c * IH_M + og * 512 + 512],
                       c == 0, c == 7,
                       waits=([LNBF, WFC_RAW] + S.now("vv", "ss")) if c == 0 else [],
                       inc="pe" if c == 7 else None)
                done = S.now("pe")[0]
                bcw = vop(lambda v: v.tensor_copy(brf[0:1, 0:512], pbias[0:1, 0:512]),
                          waits=[done] + list(BV_RAW))
                vop(lambda v, og_=og: v.tensor_tensor(
                    brf2[0:1, 0:512], brf[0:1, 0:512],
                    biasf[0:1, og_ * 512: og_ * 512 + 512], ALU.add),
                    waits=S.now("gw") + [bcw])
                dma("gpsimd", war[:, M_BR + og * 512: M_BR + og * 512 + 512],
                    bass.AP(brf2, 0, [[512, 1], [0, P], [1, 512]]),
                    waits=S.now("vv"), inc="gw")
            BROW_M = S.now("vv", "gw")
            for c in range(8):
                vop(lambda v, c_=c: v.tensor_scalar(
                    war[:, M_WFC + c_ * IH_M: M_WFC + (c_ + 1) * IH_M],
                    war[:, M_WFC + c_ * IH_M: M_WFC + (c_ + 1) * IH_M],
                    lnv[:, 16 + c_:16 + c_ + 1], None, ALU.mult),
                    waits=BROW_M + [LNV])
            WFC_OK = S.now("vv")[0]

            fc_sg = {0: None, 1: None}
            for i in range(NT):
                if H == 0:
                    XN = tb[:, M_XN:M_XN + D]
                    xn_ok = ln_apply(racc[:, ts(i, D)], XN, D, [RACC_A], [RACC_A],
                                     out_guards=S.now("pe"), jkc=M_JK)
                    for c in range(8):
                        do_transpose(XN[:, ts(c, P)], P,
                                     x2T[:, c * n_tok + i * P: c * n_tok + (i + 1) * P],
                                     [xn_ok] if c == 0 else [], "v" if c % 2 == 0 else "s")
                XN2_OK = S.now("vv", "ss")

                def fc_evac(og, pb, wdt, done, col0):
                    td = vop(lambda v, a=pb, og_=og: v.tensor_tensor(
                        s5[:, og_ % 2 * 512: og_ % 2 * 512 + 512], a[:, 0:512],
                        war[:, M_BR + og_ * 512: M_BR + og_ * 512 + 512], ALU.add),
                        waits=[done] + ([fc_sg[og % 2]] if fc_sg[og % 2] else []))
                    ev = sop(lambda s, og_=og: s.activation(
                        tb[:, M_MID + og_ * 512: M_MID + og_ * 512 + 512],
                        s5[:, og_ % 2 * 512: og_ % 2 * 512 + 512], AF.Gelu),
                        waits=[td])
                    fc_sg[og % 2] = ev
                    return ev

                project(8, lambda c: x2T[:, c * n_tok + i * P: c * n_tok + (i + 1) * P],
                        lambda c, col0, wdt: war[:, M_WFC + c * IH_M + col0:
                                                 M_WFC + c * IH_M + col0 + wdt],
                        [512] * 4, None, XN2_OK + [WFC_OK] + BROW_M, fc_evac)
                MID_OK = S.now("ss")[0]
                for c in range(NCH_M):
                    do_transpose(tb[:, M_MID + c * P: M_MID + (c + 1) * P], P,
                                 tb[:, M_MIDT + c * P: M_MIDT + (c + 1) * P],
                                 [MID_OK] if c == 0 else [], "v" if c % 2 == 0 else "s")
                MIDT_OK = S.now("vv", "ss")

                def pr_evac(og, pb, wdt, done, col0, i_=i, H_=H):
                    if H_ == 0:
                        vop(lambda v, a=pb, og_=og: v.tensor_tensor(
                            s5[:, og_ * 512: og_ * 512 + 512], a[:, 0:512],
                            war[:, M_BR + 2048 + og_ * 512: M_BR + 2048 + og_ * 512 + 512],
                            ALU.add), waits=[done] + list(BV_RAW))
                        return vop(lambda v, og_=og, j_=i_: v.tensor_tensor(
                            racc[:, j_ * D + og_ * 512: j_ * D + og_ * 512 + 512],
                            s5[:, og_ * 512: og_ * 512 + 512],
                            racc[:, j_ * D + og_ * 512: j_ * D + og_ * 512 + 512],
                            ALU.add))
                    return vop(lambda v, a=pb, og_=og, j_=i_: v.tensor_tensor(
                        racc[:, j_ * D + og_ * 512: j_ * D + og_ * 512 + 512], a[:, 0:512],
                        racc[:, j_ * D + og_ * 512: j_ * D + og_ * 512 + 512], ALU.add),
                        waits=[done])

                project(NCH_M, lambda c: tb[:, M_MIDT + c * P: M_MIDT + (c + 1) * P],
                        lambda c, col0, wdt: war[:, M_WPR + c * D + col0:
                                                 M_WPR + c * D + col0 + wdt],
                        [512] * 2, None,
                        MIDT_OK + [WPR_RAW] + list(BV_RAW), pr_evac)
                if i == 0 and H == 0 and "mid" in dmp:
                    dma("sync", dmp["mid"][0:P, :], tb[:, M_MID:M_MID + IH_M],
                        waits=[MID_OK], inc="ho")
        RACC_M = S.now("vv")[0]

        xqT = x2T
        x_slots2 = Slots(2)
        for _sl in range(2):
            for g_ in S.now("vv", "ss"):
                x_slots2.guard[_sl].append(g_)
        xload2 = [None] * NT
        if "racc15" in dmp and stop >= 4:
            dma("sync", dmp["racc15"][0:P, :], racc[:, (NT - 1) * D: NT * D],
                waits=[RACC_M], inc="ho")
        if stop >= 4:
            sl0, g0 = x_slots2.use()
            dma("sync", xt[:, ts(sl0, D)], x_hbm[0:P, :], waits=list(g0), inc="hx")
            xload2[0] = (sl0, S.now("hx")[0])
        for i in range(NT if stop >= 4 else 0):
            xsl, xrdy = xload2[i]
            if i + 1 < NT:
                sl_, g_ = x_slots2.use()
                dma("sync", xt[:, ts(sl_, D)], x_hbm[ts(i + 1, P), :], waits=list(g_), inc="hx")
                xload2[i + 1] = (sl_, S.now("hx")[0])
            xtile = xt[:, ts(xsl, D)]
            rtile = racc[:, ts(i, D)]
            sad = vop(lambda v, a=xtile, b=rtile: v.tensor_tensor(s5[:, :], a, b, ALU.add),
                      waits=[xrdy, RACC_M] + S.now("ss"))
            x_slots2.consumed(xsl, sad)
            x4_ok = ln_apply(s5[:, :], rtile, D, [sad], [sad], out_guards=[], jkc=F_JK)
            qd = quant_emit(rtile, D, tb[:, F_XQD:F_XQD + D], [x4_ok] + S.now("pe"))
            for c in range(8):
                do_transpose(tb[:, F_XQD + c * P: F_XQD + (c + 1) * P], P,
                             xqT[:, c * n_tok + i * P: c * n_tok + (i + 1) * P],
                             [qd] if c == 0 else [], "v" if c % 2 == 0 else "s")
            if i == 0:
                if "x4" in dmp:
                    dma("sync", dmp["x4"][0:P, :], rtile, waits=[x4_ok], inc="ho")
                if "xq" in dmp:
                    dma("sync", dmp["xq"][0:P, :], tb[:, F_XQD:F_XQD + D],
                        waits=[qd], inc="ho")
            if i == NT - 1:
                if "s5f" in dmp:
                    dma("sync", dmp["s5f"][0:P, :], s5[:, :], waits=[x4_ok], inc="ho")
                if "x4f" in dmp:
                    dma("sync", dmp["x4f"][0:P, :], rtile, waits=[x4_ok], inc="ho")
        XQT_OK = S.now("vv", "ss")

        stg = Slots(2)
        stgq = [("sync", "hx"), ("gpsimd", "gw")]
        nst = [0]
        for H in range(2 if stop >= 5 else 0):
            phase_gate = S.now("pe", "vv", "ss")
            nfact = [0]
            ms = {"g": [], "u": [], "d": []}
            for wkey, wdram, base in (("g", wg_n, F_WG), ("u", wu_n, F_WU)):
                for j in range(NCH_F):
                    slg, gg_ = stg.use()
                    qe, qsem = stgq[nst[0] % 2]
                    nst[0] += 1
                    dma(qe, xt[:, ts(slg, D)],
                        bass.AP(wdram, (H * IH_F + j * P) * D, [[D, P], [1, D]]),
                        waits=list(gg_) + (phase_gate if nfact[0] < 2 else []),
                        inc=qsem)
                    nfact[0] += 1
                    srdy = S.now(qsem)[0]
                    qd = quant_emit(xt[:, ts(slg, D)], D, tb[:, F_WQN:F_WQN + D],
                                    [srdy], xn32=s5[:, 0:D], v32=elu_f[:, 0:D])
                    stg.consumed(slg, qd)
                    for c in range(8):
                        do_transpose(tb[:, F_WQN + c * P: F_WQN + (c + 1) * P], P,
                                     war[:, base + c * IH_F + j * P: base + c * IH_F + (j + 1) * P],
                                     [qd] if c == 0 else [], "v" if c % 2 == 0 else "s")
                    ms[wkey].append(S.now("vv", "ss"))
            for j in range(8):
                sl0, g0 = stg.use()
                sl1, g1 = stg.use()
                qe, qsem = stgq[nst[0] % 2]
                nst[0] += 1
                dma(qe, xt[:, 0:IH_F],
                    bass.AP(wd_n, j * P * DFF + H * IH_F, [[DFF, P], [1, IH_F]]),
                    waits=list(g0) + list(g1) + (phase_gate if nfact[0] < 2 else []),
                    inc=qsem)
                nfact[0] += 1
                srdy = S.now(qsem)[0]
                HW_ = IH_F // 2
                quant_emit(xt[:, 0:HW_], HW_, tb[:, F_WQN:F_WQN + HW_], [srdy],
                           xn32=s5[:, 0:HW_], v32=elu_f[:, 0:HW_])
                qd = quant_emit(xt[:, HW_:IH_F], HW_, tb[:, F_WQN + HW_:F_WQN + IH_F],
                                [srdy], xn32=s5[:, 0:HW_], v32=elu_f[:, 0:HW_])
                stg.consumed(sl0, qd)
                stg.consumed(sl1, qd)
                for c in range(NCH_F):
                    do_transpose(tb[:, F_WQN + c * P: F_WQN + (c + 1) * P], P,
                                 war[:, F_WD + c * D + j * P: F_WD + c * D + (j + 1) * P],
                                 [qd] if c == 0 else [], "v" if c % 2 == 0 else "s")
                ms["d"].append(S.now("vv", "ss"))
            WQ_OK = S.now("vv", "ss")

            def wq_gate(wkey, col0, wdt):
                j_hi = min((col0 + wdt - 1) // P, len(ms[wkey]) - 1)
                return ms[wkey][j_hi]
            if H == 0 and "wgq" in dmp:
                dma("sync", dmp["wgq"][0:P, :], war[:, F_WG:F_WG + IH_F], waits=WQ_OK, inc="ho")
            if H == 0 and "wuq" in dmp:
                dma("sync", dmp["wuq"][0:P, :], war[:, F_WU:F_WU + 8 * IH_F], waits=WQ_OK, inc="ho")

            OGW = [512, 512, 384]
            HTB = (F_HT, F_HQ)

            def stageF(i):
                par = i % 2
                hb = HTB[par]

                def gate_evac(og, pb, wdt, done, col0):
                    w8 = ([tile_qv[0]] if tile_qv[0] else []) + \
                         ([ht_guard[par]] if ht_guard[par] else [])
                    return sop(lambda s, a=pb, c0=col0, w=wdt: s.activation(
                        tb[:, hb + c0: hb + c0 + w], a[:, 0:w], AF.Silu),
                        waits=[done] + w8)

                project(8, lambda c: xqT[:, c * n_tok + i * P: c * n_tok + (i + 1) * P],
                        lambda c, col0, wdt: war[:, F_WG + c * IH_F + col0:
                                                 F_WG + c * IH_F + col0 + wdt],
                        OGW, None, XQT_OK, gate_evac,
                        og_waits=lambda og, c0, w: wq_gate("g", c0, w))
                SG_OK = S.now("ss")[0]
                if H == 0 and i == 0 and "sg" in dmp:
                    dma("sync", dmp["sg"][0:P, :], tb[:, hb:hb + IH_F],
                        waits=[SG_OK], inc="ho")
                    qstate["sd"] = S.now("ho")[0]

                def up_evac(og, pb, wdt, done, col0):
                    return vop(lambda v, a=pb, c0=col0, w=wdt: v.tensor_tensor(
                        tb[:, hb + c0: hb + c0 + w], a[:, 0:w],
                        tb[:, hb + c0: hb + c0 + w], ALU.mult),
                        waits=[done, SG_OK])

                project(8, lambda c: xqT[:, c * n_tok + i * P: c * n_tok + (i + 1) * P],
                        lambda c, col0, wdt: war[:, F_WU + c * IH_F + col0:
                                                 F_WU + c * IH_F + col0 + wdt],
                        OGW, None, [], up_evac,
                        og_waits=lambda og, c0, w: wq_gate("u", c0, w))
                return dict(H_OK=S.now("vv")[0], hb=hb, par=par)

            def coreF(i, h):
                hb = h["hb"]
                qd = quant_emit(tb[:, hb:hb + IH_F], IH_F, tb[:, hb:hb + IH_F],
                                [h["H_OK"]])
                tile_qv[0] = qd
                for c in range(NCH_F):
                    do_transpose(tb[:, hb + c * P: hb + (c + 1) * P], P,
                                 tb[:, F_HQT + c * P: F_HQT + (c + 1) * P],
                                 [qd] if c == 0 else [], "v" if c % 2 == 0 else "s")
                ht_guard[h["par"]] = S.now("pe")[0]
                HQT_OK = S.now("vv", "ss")

                def dn_evac(og, pb, wdt, done, col0, i_=i):
                    return vop(lambda v, a=pb, og_=og, j_=i_: v.tensor_tensor(
                        racc[:, j_ * D + og_ * 512: j_ * D + og_ * 512 + 512], a[:, 0:512],
                        racc[:, j_ * D + og_ * 512: j_ * D + og_ * 512 + 512], ALU.add),
                        waits=[done])

                project(NCH_F, lambda c: tb[:, F_HQT + c * P: F_HQT + (c + 1) * P],
                        lambda c, col0, wdt: war[:, F_WD + c * D + col0:
                                                 F_WD + c * D + col0 + wdt],
                        [512] * 2, None, HQT_OK, dn_evac,
                        og_waits=lambda og, c0, w: wq_gate("d", c0, w))
                if H == 1 and stop >= 6:
                    DN_OK = S.now("vv")[0]
                    og_ = [ob_guard[0]] if ob_guard[0] else []
                    ev = ln_apply(racc[:, ts(i, D)], outb[:, :], D,
                                  [], [DN_OK] + (list(WQ_OK) if i == 0 else []),
                                  out_guards=og_, jkc=F_JK)
                    dma("sync", out_hbm[ts(i, P), :], outb[:, :], waits=[ev], inc="ho")
                    ob_guard[0] = S.now("ho")[0]

            hF = stageF(0)
            for i in range(NT):
                hN = stageF(i + 1) if i + 1 < NT else None
                coreF(i, hF)
                hF = hN
        X5_OK = S.now("vv")[0]

        if stop < 6:
            for i in range(NT):
                og_ = [ob_guard[0]] if ob_guard[0] else []
                gate = S.now("vv", "ss", "pe")
                ev = sop(lambda s_, b=racc[:, ts(i, D)]: s_.copy(outb[:, :], b),
                         waits=(gate if i == 0 else []) + og_)
                dma("sync", out_hbm[ts(i, P), :], outb[:, :], waits=[ev], inc="ho")
                ob_guard[0] = S.now("ho")[0]
        FINAL_HO = S.at("ho")

        def realize(eng_name):
            def body(eng):
                for fn, waits, inc, amt in S.ops[eng_name]:
                    for (sn, v) in waits:
                        eng.wait_ge(sems[sn], v)
                    r = fn(eng)
                    if inc is not None:
                        r.then_inc(sems[inc], amt)
                if eng_name == "sync":
                    eng.wait_ge(sems["ho"], FINAL_HO)
            return body

        blk.sync(realize("sync"))
        blk.gpsimd(realize("gpsimd"))
        blk.tensor(realize("tensor"))
        blk.vector(realize("vector"))
        blk.scalar(realize("scalar"))

    return nc


def _prep_inputs(L0_attn_x, L0_attn_ln1_w, L0_attn_ln1_b, L0_attn_q_w, L0_attn_q_b,
                 L0_attn_k_w, L0_attn_k_b, L0_attn_v_w, L0_attn_v_b, L0_attn_o_w,
                 L0_attn_o_b, L0_attn_ln2_w, L0_attn_ln2_b, L0_attn_fc_w, L0_attn_fc_b,
                 L0_attn_proj_w, L0_attn_proj_b, L0_ffn_w_gate, L0_ffn_w_up, L0_ffn_w_down):
    f = np.float32
    c = np.ascontiguousarray

    def t(a):
        return c(np.asarray(a).T.astype(f, copy=False))

    base = {
        "wqkv_t": c(np.concatenate([np.asarray(L0_attn_q_w), np.asarray(L0_attn_k_w),
                                    np.asarray(L0_attn_v_w)], axis=0).T.astype(f, copy=False)),
        "wo_t": t(L0_attn_o_w),
        "wfc_t": t(L0_attn_fc_w),
        "wproj_t": t(L0_attn_proj_w),
        "wg_n": c(np.asarray(L0_ffn_w_gate).astype(f, copy=False)),
        "wu_n": c(np.asarray(L0_ffn_w_up).astype(f, copy=False)),
        "wd_n": c(np.asarray(L0_ffn_w_down).astype(f, copy=False)),
        "ln1w": c(np.asarray(L0_attn_ln1_w).reshape(8, P).T.astype(f, copy=False)),
        "ln1b": c(np.asarray(L0_attn_ln1_b).reshape(8, P).T.astype(f, copy=False)),
        "ln2w": c(np.asarray(L0_attn_ln2_w).reshape(8, P).T.astype(f, copy=False)),
        "ln2b": c(np.asarray(L0_attn_ln2_b).reshape(8, P).T.astype(f, copy=False)),
        "bqkv": c(np.concatenate([np.asarray(L0_attn_q_b), np.asarray(L0_attn_k_b),
                                  np.asarray(L0_attn_v_b)]).reshape(1, -1).astype(f, copy=False)),
        "bo": c(np.asarray(L0_attn_o_b).reshape(1, -1).astype(f, copy=False)),
        "bfc": c(np.asarray(L0_attn_fc_b).reshape(1, -1).astype(f, copy=False)),
        "bproj": c(np.asarray(L0_attn_proj_b).reshape(1, -1).astype(f, copy=False)),
    }
    xfull = np.ascontiguousarray(np.asarray(L0_attn_x).astype(f, copy=False).reshape(-1, D))
    return xfull, base


def kernel(**inputs):
    xfull, base = _prep_inputs(**inputs)
    n_tok = xfull.shape[0] // NCORES
    shards = np.split(xfull, NCORES, axis=0)
    nc = build(n_tok=n_tok)
    in_maps = [dict(base, x=np.ascontiguousarray(s)) for s in shards]
    res = run_bass_kernel_spmd(nc, in_maps, core_ids=list(range(NCORES)))
    out = np.concatenate([np.asarray(r["out"]) for r in res.results], axis=0)
    return out.reshape(np.asarray(inputs["L0_attn_x"]).shape)


# version 22
# speedup vs baseline: 1.0198x; 1.0039x over previous
import numpy as np
import concourse.bass as bass
import concourse.mybir as mybir
from concourse.bass_utils import run_bass_kernel_spmd

F32 = mybir.dt.float32
BF16 = mybir.dt.bfloat16
AF = mybir.ActivationFunctionType
ALU = mybir.AluOpType
AX = mybir.AxisListType

P = 128
D = 1024
NH = 16
HD = 64
DFC = 4096
DFF = 2816
EPS = 1e-5
NCORES = 8

ENGS = ("sync", "gpsimd", "tensor", "vector", "scalar")


class Sched:
    def __init__(self):
        self.ops = {e: [] for e in ENGS}
        self.counts = {}

    def at(self, sem):
        return self.counts.get(sem, 0)

    def now(self, *sems):
        return [(s, self.at(s)) for s in sems]

    def emit(self, eng, fn, waits=(), inc=None, amt=1):
        ws = [(s, v) for (s, v) in waits if v > 0]
        self.ops[eng].append((fn, ws, inc, amt))
        if inc is not None:
            self.counts[inc] = self.counts.get(inc, 0) + amt
        return (inc, self.counts.get(inc, 0)) if inc else None


class Slots:
    def __init__(self, n):
        self.n = n
        self.guard = [[] for _ in range(n)]
        self.i = 0

    def use(self):
        s = self.i % self.n
        self.i += 1
        w = self.guard[s]
        self.guard[s] = []
        return s, w

    def consumed(self, s, sem_val):
        if sem_val is not None:
            self.guard[s].append(sem_val)


def ts(i, n):
    return slice(i * n, (i + 1) * n)


def build(n_tok=2048, dump=(), stop=6, asub=7):
    assert n_tok % P == 0
    NT = n_tok // P
    IH_M = DFC // 2
    NCH_M = IH_M // P
    IH_F = DFF // 2
    NCH_F = IH_F // P

    nc = bass.Bass("TRN2", target_bir_lowering=False, detect_race_conditions=False)

    x_hbm = nc.dram_tensor("x", [n_tok, D], F32, kind="ExternalInput")
    wqkv = nc.dram_tensor("wqkv_t", [D, 3 * D], F32, kind="ExternalInput")
    wo = nc.dram_tensor("wo_t", [D, D], F32, kind="ExternalInput")
    wfc = nc.dram_tensor("wfc_t", [D, DFC], F32, kind="ExternalInput")
    wproj = nc.dram_tensor("wproj_t", [DFC, D], F32, kind="ExternalInput")
    wg_n = nc.dram_tensor("wg_n", [DFF, D], F32, kind="ExternalInput")
    wu_n = nc.dram_tensor("wu_n", [DFF, D], F32, kind="ExternalInput")
    wd_n = nc.dram_tensor("wd_n", [D, DFF], F32, kind="ExternalInput")
    ln1w = nc.dram_tensor("ln1w", [P, 8], F32, kind="ExternalInput")
    ln1b = nc.dram_tensor("ln1b", [P, 8], F32, kind="ExternalInput")
    ln2w = nc.dram_tensor("ln2w", [P, 8], F32, kind="ExternalInput")
    ln2b = nc.dram_tensor("ln2b", [P, 8], F32, kind="ExternalInput")
    bqkv = nc.dram_tensor("bqkv", [1, 3 * D], F32, kind="ExternalInput")
    bo = nc.dram_tensor("bo", [1, D], F32, kind="ExternalInput")
    bfc = nc.dram_tensor("bfc", [1, DFC], F32, kind="ExternalInput")
    bproj = nc.dram_tensor("bproj", [1, D], F32, kind="ExternalInput")
    out_hbm = nc.dram_tensor("out", [n_tok, D], F32, kind="ExternalOutput")
    dmp = {}
    for name, shape, dt in dump:
        dmp[name] = nc.dram_tensor("dump_" + name, shape, dt, kind="ExternalOutput")

    from contextlib import ExitStack
    with ExitStack() as ctx:
        e = ctx.enter_context

        racc = e(nc.sbuf_tensor("racc", [P, NT * D], BF16))
        x2T = e(nc.sbuf_tensor("x2T", [P, 8 * n_tok], BF16))
        WARC = 36992
        war = e(nc.sbuf_tensor("war", [P, WARC], BF16))
        TBC = 17664
        tb = e(nc.sbuf_tensor("tb", [P, TBC], BF16))
        xt = e(nc.sbuf_tensor("xt", [P, 2 * D], F32))
        s5 = e(nc.sbuf_tensor("s5", [P, D], F32))
        elu_f = e(nc.sbuf_tensor("elu_f", [P, 1024], F32))
        outb = elu_f
        stage = None
        st = e(nc.sbuf_tensor("st", [P, 48], F32))
        qs = e(nc.sbuf_tensor("qs", [P, 3 * 96], F32))
        ident = e(nc.sbuf_tensor("ident", [P, P], BF16))
        maskS = e(nc.sbuf_tensor("maskS", [P, P], BF16))
        lnv = e(nc.sbuf_tensor("lnv", [P, 32], F32))
        lnb_bf = e(nc.sbuf_tensor("lnb_bf", [P, 16], BF16))
        brf = e(nc.sbuf_tensor("brf", [1, 512], F32))
        brf2 = e(nc.sbuf_tensor("brf2", [1, 512], F32))
        biasf = e(nc.sbuf_tensor("biasf", [1, 3 * D], BF16))

        pb0 = e(nc.psum_tensor("pb0", [P, 512], F32))
        pb1 = e(nc.psum_tensor("pb1", [P, 512], F32))
        ptr0 = e(nc.psum_tensor("ptr0", [P, 1024], BF16))
        ptr1 = e(nc.psum_tensor("ptr1", [P, 1024], BF16))
        ptr2 = e(nc.psum_tensor("ptr2", [P, 1024], BF16))
        ps1a = e(nc.psum_tensor("ps1a", [P, 512], F32))
        ps1b = e(nc.psum_tensor("ps1b", [P, 512], F32))
        ps2 = e(nc.psum_tensor("ps2", [P, 512], F32))
        pbias = ps2

        sems = {}
        for sname in ("hx", "ho", "gw", "gm", "pe", "vv", "ss", "gg"):
            sems[sname] = e(nc.semaphore(sname))
        blk = e(nc.Block())

        S = Sched()
        PB = (pb0, pb1)
        pbig = Slots(2)
        ptrs = Slots(3)
        PTR = None
        ps1s = Slots(2)
        TBP = TBC

        A_XN, A_XNT, A_QKV, A_QT, A_KT, A_VG, A_SS, A_AM, A_AT, A_ER, A_EE, A_JK = (
            0, 1024, 2048, 5120, 7168, 9216, 10256, 10512, 11536, 13584, 14608, 15632)
        M_XN, M_MID, M_MIDT, M_JK = 0, 1024, 3072, 15632
        F_XQD, F_QXN, F_QV, F_B1, F_B2, F_B3, F_B4, F_B5, F_WQN, F_HT, F_HQ, F_HQT, F_JK = (
            0, 1024, 2432, 3840, 5248, 6656, 8064, 9472, 10880, 12288, 13696, 15104, 16512)
        A_WQKV, A_WO, A_BR = 0, 24576, 32768
        M_WFC, M_WPR, M_BR = 0, 16384, 32768
        F_WG, F_WU, F_WD = 0, 11264, 22528

        def mm(psum_ap, lhsT_ap, rhs_ap, start, stop, waits=(), inc=None, tp=None):
            def fn(t, a=psum_ap, b=lhsT_ap, c=rhs_ap, st_=start, sp=stop, tp_=tp):
                return t.matmul(a, b, c, start=st_, stop=sp, tile_position=tp_)
            return S.emit("tensor", fn, waits, inc)

        def vop(fn, waits=()):
            return S.emit("vector", fn, waits, "vv")

        def sop(fn, waits=()):
            return S.emit("scalar", fn, waits, "ss")

        def gop(fn, waits=()):
            return S.emit("gpsimd", fn, waits, "gg")

        def dma(eng, out_ap, in_ap, waits=(), inc="hx"):
            def fn(g, a=out_ap, b=in_ap):
                return g.dma_start(out=a, in_=b)
            return S.emit(eng, fn, waits, inc, amt=16)

        gop(lambda g: g.memset(ident[:, :], 0.0))
        gop(lambda g: g.affine_select(out=ident[:, :], in_=ident[:, :],
                                      compare_op=ALU.not_equal, fill=1.0, base=0,
                                      pattern=[[-1, P]], channel_multiplier=1),
            waits=S.now("gg"))
        Bm = tb[0:8, 0:P]
        gop(lambda g: g.memset(Bm, 1.0), waits=S.now("gg"))
        gop(lambda g: g.affine_select(out=Bm, in_=Bm, compare_op=ALU.is_ge,
                                      fill=0.0, base=0, pattern=[[1, P]],
                                      channel_multiplier=-16), waits=S.now("gg"))
        gop(lambda g: g.affine_select(out=Bm, in_=Bm, compare_op=ALU.is_ge,
                                      fill=0.0, base=15, pattern=[[-1, P]],
                                      channel_multiplier=16), waits=S.now("gg"))
        def _mask_mm(t):
            return t.matmul(ps1a[:, 0:P], Bm, Bm, start=True, stop=True)
        S.emit("tensor", _mask_mm, S.now("gg"), "pe")
        vop(lambda v: v.tensor_copy(maskS[:, :], ps1a[:, 0:P]), waits=S.now("pe"))
        for G in range(16):
            gop(lambda g, G_=G: g.memset(tb[:, A_VG + G_ * 65 + 64: A_VG + G_ * 65 + 65], 1.0))
        CONST = S.now("gg")[0]
        CONSTV = S.now("vv")[0]

        dma("sync", lnv[:, 0:8], ln1w[:, :])
        dma("sync", lnv[:, 8:16], ln1b[:, :])
        dma("sync", lnv[:, 16:24], ln2w[:, :])
        dma("sync", lnv[:, 24:32], ln2b[:, :])
        LNV = S.now("hx")[0]
        dma("gpsimd", lnb_bf[:, 0:8], ln1b[:, :], inc="gw")
        dma("gpsimd", lnb_bf[:, 8:16], ln2b[:, :], inc="gw")
        LNBF = S.now("gw")[0]

        ln_state = {"i": 0, "act": {0: None, 1: None}}

        def ln_apply(src_ap, out_ap, width, waits_v, waits_s, out_guards=(), jkc=None):
            n = ln_state["i"]
            ln_state["i"] += 1
            b = 16 * (n % 2)
            g = ln_state["act"][n % 2]
            gg = [g] if g else []
            jk = tb[:, jkc:jkc + width]
            c_sum, c_sq, c_mu, c_var, c_t, c_inv, c_nmi = (b, b + 2, b + 4, b + 6,
                                                           b + 8, b + 10, b + 12)
            red = vop(lambda v, a=src_ap: v.reduce_sum(st[:, c_sum:c_sum + 1], a,
                                                       axis=AX.X),
                      waits=list(waits_v) + gg)
            sop(lambda s, a=src_ap: s.activation(jk, a, AF.Square,
                                                 accum_out=st[:, c_sq:c_sq + 1]),
                waits=list(waits_s) + gg)
            sqd = S.now("ss")[0]
            w1 = vop(lambda v: v.tensor_scalar(st[:, c_mu:c_mu + 1],
                                               st[:, c_sum:c_sum + 1],
                                               1.0 / width, None, ALU.mult),
                     waits=[red])
            w2 = vop(lambda v: v.tensor_scalar(st[:, c_var:c_var + 1],
                                               st[:, c_sq:c_sq + 1],
                                               1.0 / width, None, ALU.mult),
                     waits=[sqd])
            w3 = vop(lambda v: v.tensor_mul(st[:, c_t:c_t + 1], st[:, c_mu:c_mu + 1],
                                            st[:, c_mu:c_mu + 1]), waits=[w1])
            w4 = vop(lambda v: v.tensor_tensor(st[:, c_var:c_var + 1],
                                               st[:, c_var:c_var + 1],
                                               st[:, c_t:c_t + 1], ALU.subtract),
                     waits=[w2, w3])
            ved = vop(lambda v: v.tensor_scalar(st[:, c_var:c_var + 1],
                                                st[:, c_var:c_var + 1],
                                                EPS, None, ALU.add), waits=[w4])
            sop(lambda s: s.activation(st[:, c_t:c_t + 1], st[:, c_var:c_var + 1],
                                       AF.Ln), waits=[ved])
            sop(lambda s: s.activation(st[:, c_inv:c_inv + 1], st[:, c_t:c_t + 1],
                                       AF.Exp, scale=np.float32(-0.5)))
            inv = S.now("ss")[0]
            w5 = vop(lambda v: v.tensor_mul(st[:, c_nmi:c_nmi + 1],
                                            st[:, c_mu:c_mu + 1],
                                            st[:, c_inv:c_inv + 1]),
                     waits=[inv, w1])
            nmi = vop(lambda v: v.tensor_scalar(st[:, c_nmi:c_nmi + 1],
                                                st[:, c_nmi:c_nmi + 1],
                                                -1.0, None, ALU.mult), waits=[w5])
            act = sop(lambda s, a=src_ap, o=out_ap: s.activation(
                o, a, AF.Identity, bias=st[:, c_nmi:c_nmi + 1],
                scale=st[:, c_inv:c_inv + 1]),
                waits=[nmi] + list(out_guards))
            ln_state["act"][n % 2] = act
            return act

        def do_transpose(src_ap, width, dst_ap, waits, ee):
            sl, g = ptrs.use()
            pt = (ptr0, ptr1, ptr2)[sl]
            transpose_ap = pt[0:width, 0:P]

            def fn(t, a=transpose_ap, b=src_ap):
                return t.transpose(a, b, ident[:, :])
            S.emit("tensor", fn, list(waits) + list(g), "pe")
            ped = S.now("pe")[0]
            if ee == "v":
                ev = vop(lambda v, a=dst_ap, b=transpose_ap: v.tensor_copy(a, b),
                         waits=[ped])
            else:
                ev = sop(lambda s, a=dst_ap, b=transpose_ap: s.copy(a, b), waits=[ped])
            ptrs.consumed(sl, ev)
            return ev

        def project(nchunks, lhsT_of, rhs_of, out_widths, bias_col, first_waits, evac,
                    og_waits=None):
            col0 = 0
            for og, wdt in enumerate(out_widths):
                sl, guard = pbig.use()
                pb = PB[sl]
                fw = list(first_waits) + list(guard)
                if og_waits is not None:
                    fw += list(og_waits(og, col0, wdt))
                first = True
                for c in range(nchunks):
                    last = c == nchunks - 1
                    mm(pb[:, 0:wdt], lhsT_of(c), rhs_of(c, col0, wdt), first, last,
                       waits=fw if first else [], inc="pe" if last else None)
                    first, fw = False, []
                done = S.now("pe")[0]
                ev = evac(og, pb, wdt, done, col0)
                pbig.consumed(sl, ev)
                col0 += wdt

        def quant_emit(src2d, W, dst2d, waits_v, xn32=None, v32=None):
            nb = W // 16
            src3 = src2d.rearrange("p (b k) -> p b k", k=16)
            QXN = xn32 if xn32 is not None else tb[:, F_QXN:F_QXN + W]
            QV = v32 if v32 is not None else tb[:, F_QV:F_QV + W]
            B1 = tb[:, F_B1:F_B1 + W]
            B2 = tb[:, F_B2:F_B2 + W]
            B3 = tb[:, F_B3:F_B3 + W]
            B4 = tb[:, F_B4:F_B4 + W]
            B5 = tb[:, F_B5:F_B5 + W]
            bmr = vop(lambda v: v.tensor_reduce(qs[:, 0:nb], src3, axis=AX.X,
                                                op=ALU.max, apply_absolute_value=True),
                      waits=waits_v)
            scw = vop(lambda v: v.tensor_scalar(qs[:, 96:96 + nb], qs[:, 0:nb],
                                                1.0 / 6.0, 1e-12, ALU.mult, ALU.max),
                      waits=[bmr])
            rsr = vop(lambda v: v.reciprocal(qs[:, 192:192 + nb], qs[:, 96:96 + nb]),
                      waits=[scw])
            pv = qstate["sd"]
            vop(lambda v: v.tensor_tensor(
                QXN.rearrange("p (b k) -> p b k", k=16), src3,
                qs[:, 192:192 + nb].unsqueeze(2).broadcast_to([P, nb, 16]), ALU.mult),
                waits=([pv] if pv else []) + [rsr])
            xnd = S.now("vv")[0]
            sop(lambda s: s.activation(QV, QXN, AF.Abs), waits=[xnd])
            sop(lambda s: s.activation(B5, QXN, AF.Sign))
            sd = S.now("ss")[0]
            pvg = qstate["s_guard"]
            qstate["sd"] = sd
            NUD = 1.0 + 2.0 ** -40
            if True:
                vop(lambda v: v.tensor_scalar(B1, QV, 0.25 * NUD, None, ALU.is_gt),
                    waits=[sd])
                vop(lambda v: v.tensor_scalar(B2, QV, 0.75 * NUD, None, ALU.is_gt))
                vop(lambda v: v.tensor_add(B1, B1, B2))
                vop(lambda v: v.tensor_scalar(B2, QV, 1.25 * NUD, None, ALU.is_gt))
                vop(lambda v: v.tensor_scalar(B3, QV, 1.75 * NUD, None, ALU.is_gt))
                vop(lambda v: v.tensor_add(B2, B2, B3))
                vop(lambda v: v.tensor_add(B1, B1, B2))
                vop(lambda v: v.tensor_scalar(B2, QV, 2.5 * NUD, None, ALU.is_gt))
                vop(lambda v: v.tensor_scalar(B3, QV, 3.5 * NUD, None, ALU.is_gt))
                vop(lambda v: v.tensor_scalar(B4, QV, 5.0 * NUD, None, ALU.is_gt))
                vop(lambda v: v.scalar_tensor_tensor(B3, B4, 2.0, B3, ALU.mult, ALU.add))
                vop(lambda v: v.tensor_add(B3, B3, B2))
                vop(lambda v: v.scalar_tensor_tensor(B1, B1, 0.5, B3, ALU.mult, ALU.add))
            vop(lambda v: v.tensor_mul(B1, B1, B5))
            qv = vop(lambda v, d=dst2d: v.tensor_tensor(
                d.rearrange("p (b k) -> p b k", k=16),
                B1.rearrange("p (b k) -> p b k", k=16),
                qs[:, 96:96 + nb].unsqueeze(2).broadcast_to([P, nb, 16]), ALU.mult),
                waits=S.now("pe"))
            qstate["s_guard"] = qv
            return qv

        qstate = {"s_guard": None, "sd": None}
        tile_qv = [None]
        ht_guard = [None, None]
        ob_guard = [None]

        for c in range(8):
            dma("gpsimd", war[:, A_WQKV + c * 3072: A_WQKV + (c + 1) * 3072],
                bass.AP(wqkv, c * P * 3 * D, [[3 * D, P], [1, 3 * D]]), inc="gw")
        WQKV_RAW = S.now("gw")[0]
        for c in range(8):
            dma("gpsimd", war[:, A_WO + c * D: A_WO + (c + 1) * D],
                bass.AP(wo, c * P * D, [[D, P], [1, D]]), inc="gw")
        dma("gpsimd", biasf[0:1, 0:3072], bqkv[:, :], inc="gw")
        dma("gpsimd", war[:, A_BR + 3072: A_BR + 4096],
            bass.AP(bo, 0, [[D, 1], [0, P], [1, D]]), inc="gw")
        WA_RAW = S.now("gw")[0]
        for og in range(6):
            for c in range(8):
                mm(pbias[0:1, 0:512], lnb_bf[:, c:c + 1],
                   war[:, A_WQKV + c * 3072 + og * 512: A_WQKV + c * 3072 + og * 512 + 512],
                   c == 0, c == 7,
                   waits=([LNBF, WQKV_RAW] + S.now("vv")) if c == 0 else [],
                   inc="pe" if c == 7 else None)
            done = S.now("pe")[0]
            bcw = vop(lambda v: v.tensor_copy(brf[0:1, 0:512], pbias[0:1, 0:512]),
                      waits=[done, WA_RAW])
            vop(lambda v, og_=og: v.tensor_tensor(
                brf2[0:1, 0:512], brf[0:1, 0:512],
                biasf[0:1, og_ * 512: og_ * 512 + 512], ALU.add),
                waits=S.now("gw") + [bcw])
            dma("gpsimd", war[:, A_BR + og * 512: A_BR + og * 512 + 512],
                bass.AP(brf2, 0, [[512, 1], [0, P], [1, 512]]),
                waits=S.now("vv"), inc="gw")
        BROW_A = S.now("vv", "gw")
        for c in range(8):
            vop(lambda v, c_=c: v.tensor_scalar(
                war[:, A_WQKV + c_ * 3072: A_WQKV + (c_ + 1) * 3072],
                war[:, A_WQKV + c_ * 3072: A_WQKV + (c_ + 1) * 3072],
                lnv[:, c_:c_ + 1], None, ALU.mult),
                waits=BROW_A + [LNV])
        WQKV_OK = S.now("vv")[0]

        x_slots = Slots(2)

        def load_x(i, slots):
            sl, g = slots.use()
            dma("sync", xt[:, ts(sl, D)], x_hbm[ts(i, P), :], waits=g, inc="hx")
            return sl, S.now("hx")[0]

        xload = [load_x(0, x_slots)] if stop >= 2 else []
        elu_guard = {0: None, 1: None}
        rec_guard = [None] * 4
        vg_guard = [None]

        stA = {}
        um_guard = [None]

        def emit_stageA(i):
            xsl, xrdy = xload[i]
            xtile = xt[:, ts(xsl, D)]
            XN = tb[:, A_XN:A_XN + D]
            gm_prev = S.now("gm")[0]
            xn_ok = ln_apply(xtile, XN, D, [xrdy], [xrdy], out_guards=S.now("pe"), jkc=A_JK)
            if i == 0 and "xn" in dmp:
                dma("sync", dmp["xn"][0:P, :], XN, waits=[xn_ok], inc="ho")
            if i == 0 and "st" in dmp:
                dma("sync", dmp["st"][0:P, :], st[:, 0:32], waits=[xn_ok], inc="ho")
            for c in range(8):
                do_transpose(XN[:, ts(c, P)], P, tb[:, A_XNT + c * P: A_XNT + (c + 1) * P],
                             [xn_ok] if c == 0 else [], "v" if c % 2 == 0 else "s")
            XNT_OK = S.now("vv", "ss")

            def qkv_evac(og, pb, wdt, done, col0):
                bias = war[:, A_BR + og * 512: A_BR + og * 512 + 512]
                if og < 4:
                    par = og % 2
                    eg = elu_guard[par]
                    ef = elu_f[:, par * 512:(par + 1) * 512]
                    er = tb[:, A_ER + par * 512: A_ER + (par + 1) * 512]
                    ee = tb[:, A_EE + par * 512: A_EE + (par + 1) * 512]
                    td = vop(lambda v, a=pb, ef_=ef, b_=bias: v.tensor_tensor(
                        ef_, a[:, 0:512], b_, ALU.add),
                        waits=[done] + (list(eg) if eg else []))
                    rd = sop(lambda s, ef_=ef, er_=er: s.activation(er_, ef_, AF.Relu),
                             waits=[td])
                    md = vop(lambda v, ef_=ef: v.tensor_scalar(
                        ef_, ef_, 0.0, None, ALU.min), waits=[rd])
                    ed = sop(lambda s, ef_=ef, ee_=ee: s.activation(ee_, ef_, AF.Exp),
                             waits=[md])
                    ad = vop(lambda v, og_=og, er_=er, ee_=ee: v.tensor_add(
                        tb[:, A_QKV + og_ * 512:A_QKV + og_ * 512 + 512], er_, ee_),
                        waits=[ed, gm_prev])
                    elu_guard[par] = (ed, ad)
                    return ad
                else:
                    return vop(lambda v, a=pb, og_=og, b_=bias: v.tensor_tensor(
                        tb[:, A_QKV + og_ * 512:A_QKV + og_ * 512 + 512], a[:, 0:512],
                        b_, ALU.add),
                        waits=[done, gm_prev])

            qkv_evs = [None] * 6

            if asub >= 2:
                def qkv_evac_plain(og, pb, wdt, done, col0):
                    return sop(lambda s_, a=pb, og_=og: s_.copy(
                        tb[:, A_QKV + og_ * 512:A_QKV + og_ * 512 + 512], a[:, 0:512]),
                        waits=[done, gm_prev])
                _ev0 = qkv_evac if asub not in (20, 21) else qkv_evac_plain

                def _ev(og, pb, wdt, done, col0):
                    r = _ev0(og, pb, wdt, done, col0)
                    qkv_evs[og] = r
                    return r
                project(8, lambda c: tb[:, A_XNT + c * P: A_XNT + (c + 1) * P],
                        lambda c, col0, wdt: war[:, A_WQKV + c * 3072 + col0:
                                                 A_WQKV + c * 3072 + col0 + wdt],
                        [512] * 6, None, XNT_OK + [WQKV_OK] + BROW_A, _ev)
            QKV_OK = S.now("vv", "ss")
            if i == 0 and "qkv" in dmp:
                dma("sync", dmp["qkv"][0:P, :], tb[:, A_QKV:A_QKV + 3 * D],
                    waits=QKV_OK, inc="ho")
            return dict(xsl=xsl, xrdy=xrdy, xtile=xtile, QKV_OK=QKV_OK,
                        qkv_evs=qkv_evs)

        def emit_marshalA(i):
            h = stA[i]
            QKV_OK = h["QKV_OK"]
            evs = h["qkv_evs"]

            def ogw(og):
                return [evs[og]] if evs[og] else QKV_OK
            for hh in range(NH if asub >= 3 else 0):
                do_transpose(tb[:, A_QKV + hh * HD: A_QKV + (hh + 1) * HD], HD,
                             bass.AP(tb, A_QT + hh, [[TBP, 64], [P, 16], [16, 8]]),
                             ogw(hh // 8) if hh % 8 == 0 else [],
                             "v" if hh % 2 else "s")
            for hh in range(NH if asub >= 3 else 0):
                do_transpose(tb[:, A_QKV + D + hh * HD: A_QKV + D + (hh + 1) * HD], HD,
                             bass.AP(tb, A_KT + hh, [[TBP, 64], [P, 16], [16, 8]]),
                             ogw(2 + hh // 8) if hh % 8 == 0 else [],
                             "s" if hh % 2 else "v")
            h["QKT_OK"] = S.now("vv", "ss")

            vm_first = ([evs[4], evs[5]] if evs[5] else QKV_OK) + \
                       ([vg_guard[0]] if vg_guard[0] else []) + [CONST]
            h["gm_base"] = S.at("gm")
            for G in range(16 if asub >= 4 else 0):
                dma("gpsimd",
                    bass.AP(tb, A_VG + G * 65, [[TBP, P], [1, HD]]),
                    bass.AP(tb, (8 * G) * TBP + A_QKV + 2 * D, [[TBP, 8], [HD, NH], [1, HD]]),
                    waits=vm_first if G == 0 else [], inc="gm")

        def emit_coreA(i):
            h = stA[i]
            QKT_OK = h["QKT_OK"]
            gm_base = h["gm_base"]
            xtile = h["xtile"]
            xrdy = h["xrdy"]
            evms = [None] * 16
            am_evs = [None] * 16
            amg = [um_guard[0]] if um_guard[0] else []
            NG = 16 if asub >= 5 else 0
            UNM = asub >= 6

            def unmarshal(G):
                for pb_ in (0, 64):
                    sl, g = ptrs.use()
                    pt = (ptr0, ptr1, ptr2)[sl]
                    tout = pt[pb_:pb_ + 64, 0:P]
                    src_t = tb[:, A_AM + G * HD: A_AM + (G + 1) * HD]

                    def fn(t, a=tout, b=src_t):
                        return t.transpose(a, b, ident[:, :])
                    S.emit("tensor", fn,
                           ([am_evs[G]] if pb_ == 0 else []) + list(g), "pe")
                    ped = S.now("pe")[0]
                    par = pb_ // 64
                    src_e = bass.AP(pt, pb_ * 1024 + par,
                                    [[1024, 64], [16, 8], [2, 8]])
                    dst_e = bass.AP(tb, pb_ * TBP + A_AT + 8 * G,
                                    [[TBP, 64], [1, 8], [P, 8]])
                    if (G + par) % 2 == 0:
                        ev = vop(lambda v, a=dst_e, b=src_e: v.tensor_copy(a, b),
                                 waits=[ped])
                    else:
                        ev = sop(lambda s_, a=dst_e, b=src_e: s_.copy(a, b), waits=[ped])
                    ptrs.consumed(sl, ev)

            for G in range(NG + 1):
                if G < NG:
                    sl1, g1 = ps1s.use()
                    p1 = (ps1a, ps1b)[sl1]
                    mm(p1[:, 0:P], tb[0:64, A_KT + G * P: A_KT + (G + 1) * P],
                       tb[0:64, A_QT + G * P: A_QT + (G + 1) * P], True, True,
                       waits=(QKT_OK if G == 0 else []) + list(g1), inc="pe")
                    s1d = S.now("pe")[0]
                    ssl = G % 2
                    evms[G] = vop(lambda v, a=p1[:, 0:P], ssl_=ssl: v.tensor_tensor(
                        tb[:, A_SS + ssl_ * P: A_SS + (ssl_ + 1) * P], a, maskS[:, :],
                        ALU.mult), waits=[s1d, CONST, CONSTV])
                    ps1s.consumed(sl1, evms[G])
                if G >= 1:
                    Gp = G - 1
                    sl2, g2 = pbig.use()
                    pb2 = PB[sl2]
                    mm(pb2[:, 0:65],
                       tb[:, A_SS + (Gp % 2) * P: A_SS + (Gp % 2 + 1) * P],
                       tb[:, A_VG + Gp * 65: A_VG + Gp * 65 + 65],
                       True, True,
                       waits=[evms[Gp], ("gm", gm_base + 16 * (Gp + 1))] + list(g2),
                       inc="pe")
                    s2d = S.now("pe")[0]
                    r = Gp % 4
                    rg = rec_guard[r]
                    rv = vop(lambda v, a=pb2[:, 64:65], r_=r: v.reciprocal(
                        st[:, 32 + 2 * r_:33 + 2 * r_], a),
                        waits=[s2d] + ([rg] if rg else []))
                    ev2 = sop(lambda s, a=pb2[:, 0:64], G_=Gp, r_=r: s.activation(
                        tb[:, A_AM + G_ * HD: A_AM + (G_ + 1) * HD], a, AF.Identity,
                        scale=st[:, 32 + 2 * r_:33 + 2 * r_]),
                        waits=[rv] + (amg if Gp == 0 else []))
                    am_evs[Gp] = ev2
                    rec_guard[r] = ev2
                    pbig.consumed(sl2, ev2)
                    if UNM and Gp >= 2:
                        unmarshal(Gp - 2)
            if NG:
                vg_guard[0] = S.now("pe")[0]
            AM_OK = S.now("ss")[0]
            if UNM:
                for G in (13, 14, 15):
                    unmarshal(G)
            if UNM:
                um_guard[0] = S.now("pe")[0]
            AT_OK = S.now("vv", "ss")

            def o_evac(og, pb, wdt, done, col0, i_=i, xp=xtile, xr=xrdy):
                vop(lambda v, a=pb, og_=og: v.tensor_tensor(
                    s5[:, og_ * 512: og_ * 512 + 512], a[:, 0:512],
                    war[:, A_BR + 3072 + og_ * 512: A_BR + 3072 + og_ * 512 + 512],
                    ALU.add), waits=[done, WA_RAW])
                return vop(lambda v, og_=og, j_=i_, x_=xp: v.tensor_tensor(
                    racc[:, j_ * D + og_ * 512: j_ * D + og_ * 512 + 512],
                    s5[:, og_ * 512: og_ * 512 + 512],
                    x_[:, og_ * 512: og_ * 512 + 512], ALU.add),
                    waits=[xr])

            if asub >= 7:
                project(8, lambda c: tb[:, A_AT + c * P: A_AT + (c + 1) * P],
                        lambda c, col0, wdt: war[:, A_WO + c * D + col0:
                                                 A_WO + c * D + col0 + wdt],
                        [512] * 2, A_BR + 3072, AT_OK + [WA_RAW], o_evac)
            else:
                vop(lambda v, i_=i, xp=xtile: v.tensor_copy(
                    racc[:, ts(i_, D)], xp), waits=[xrdy])
            x_slots.consumed(h["xsl"], S.now("vv")[0])
            x_slots.consumed(h["xsl"], S.now("ss")[0])
            if i == 0:
                if "attn" in dmp:
                    dma("sync", dmp["attn"][0:P, :], tb[:, A_AM:A_AM + D],
                        waits=[AM_OK], inc="ho")
                if "racc_a" in dmp:
                    dma("sync", dmp["racc_a"][0:P, :], racc[:, 0:D],
                        waits=S.now("vv"), inc="ho")

        for i in range(NT if stop >= 2 else 0):
            if i + 1 < NT:
                xload.append(load_x(i + 1, x_slots))
            if i == 0:
                stA[0] = emit_stageA(0)
            emit_marshalA(i)
            if i + 1 < NT:
                stA[i + 1] = emit_stageA(i + 1)
            emit_coreA(i)
            del stA[i]
        RACC_A = S.now("vv")[0]

        for H in range(2 if stop >= 3 else 0):
            phase_gate = S.now("pe", "vv", "ss")
            for c in range(8):
                dma("gpsimd", war[:, M_WFC + c * IH_M: M_WFC + (c + 1) * IH_M],
                    bass.AP(wfc, c * P * DFC + H * IH_M, [[DFC, P], [1, IH_M]]),
                    waits=phase_gate if c == 0 else [], inc="gw")
            WFC_RAW = S.now("gw")[0]
            for c in range(NCH_M):
                dma("gpsimd", war[:, M_WPR + c * D: M_WPR + (c + 1) * D],
                    bass.AP(wproj, (H * IH_M + c * P) * D, [[D, P], [1, D]]), inc="gw")
            WPR_RAW = S.now("gw")[0]
            dma("gpsimd", biasf[0:1, 0:IH_M],
                bass.AP(bfc, H * IH_M, [[1, 1], [1, IH_M]]), inc="gw")
            if H == 0:
                dma("gpsimd", war[:, M_BR + 2048: M_BR + 3072],
                    bass.AP(bproj, 0, [[D, 1], [0, P], [1, D]]), inc="gw")
            BV_RAW = S.now("gw", "gg")
            for og in range(4):
                for c in range(8):
                    mm(pbias[0:1, 0:512], lnb_bf[:, 8 + c:8 + c + 1],
                       war[:, M_WFC + c * IH_M + og * 512: M_WFC + c * IH_M + og * 512 + 512],
                       c == 0, c == 7,
                       waits=([LNBF, WFC_RAW] + S.now("vv", "ss")) if c == 0 else [],
                       inc="pe" if c == 7 else None)
                done = S.now("pe")[0]
                bcw = vop(lambda v: v.tensor_copy(brf[0:1, 0:512], pbias[0:1, 0:512]),
                          waits=[done] + list(BV_RAW))
                vop(lambda v, og_=og: v.tensor_tensor(
                    brf2[0:1, 0:512], brf[0:1, 0:512],
                    biasf[0:1, og_ * 512: og_ * 512 + 512], ALU.add),
                    waits=S.now("gw") + [bcw])
                dma("gpsimd", war[:, M_BR + og * 512: M_BR + og * 512 + 512],
                    bass.AP(brf2, 0, [[512, 1], [0, P], [1, 512]]),
                    waits=S.now("vv"), inc="gw")
            BROW_M = S.now("vv", "gw")
            for c in range(8):
                vop(lambda v, c_=c: v.tensor_scalar(
                    war[:, M_WFC + c_ * IH_M: M_WFC + (c_ + 1) * IH_M],
                    war[:, M_WFC + c_ * IH_M: M_WFC + (c_ + 1) * IH_M],
                    lnv[:, 16 + c_:16 + c_ + 1], None, ALU.mult),
                    waits=BROW_M + [LNV])
            WFC_OK = S.now("vv")[0]

            fc_sg = {0: None, 1: None}
            for i in range(NT):
                if H == 0:
                    XN = tb[:, M_XN:M_XN + D]
                    xn_ok = ln_apply(racc[:, ts(i, D)], XN, D, [RACC_A], [RACC_A],
                                     out_guards=S.now("pe"), jkc=M_JK)
                    for c in range(8):
                        do_transpose(XN[:, ts(c, P)], P,
                                     x2T[:, c * n_tok + i * P: c * n_tok + (i + 1) * P],
                                     [xn_ok] if c == 0 else [], "v" if c % 2 == 0 else "s")
                XN2_OK = S.now("vv", "ss")

                def fc_evac(og, pb, wdt, done, col0):
                    td = vop(lambda v, a=pb, og_=og: v.tensor_tensor(
                        s5[:, og_ % 2 * 512: og_ % 2 * 512 + 512], a[:, 0:512],
                        war[:, M_BR + og_ * 512: M_BR + og_ * 512 + 512], ALU.add),
                        waits=[done] + ([fc_sg[og % 2]] if fc_sg[og % 2] else []))
                    ev = sop(lambda s, og_=og: s.activation(
                        tb[:, M_MID + og_ * 512: M_MID + og_ * 512 + 512],
                        s5[:, og_ % 2 * 512: og_ % 2 * 512 + 512], AF.Gelu),
                        waits=[td])
                    fc_sg[og % 2] = ev
                    return ev

                project(8, lambda c: x2T[:, c * n_tok + i * P: c * n_tok + (i + 1) * P],
                        lambda c, col0, wdt: war[:, M_WFC + c * IH_M + col0:
                                                 M_WFC + c * IH_M + col0 + wdt],
                        [512] * 4, None, XN2_OK + [WFC_OK] + BROW_M, fc_evac)
                MID_OK = S.now("ss")[0]
                for c in range(NCH_M):
                    do_transpose(tb[:, M_MID + c * P: M_MID + (c + 1) * P], P,
                                 tb[:, M_MIDT + c * P: M_MIDT + (c + 1) * P],
                                 [MID_OK] if c == 0 else [], "v" if c % 2 == 0 else "s")
                MIDT_OK = S.now("vv", "ss")

                def pr_evac(og, pb, wdt, done, col0, i_=i, H_=H):
                    if H_ == 0:
                        vop(lambda v, a=pb, og_=og: v.tensor_tensor(
                            s5[:, og_ * 512: og_ * 512 + 512], a[:, 0:512],
                            war[:, M_BR + 2048 + og_ * 512: M_BR + 2048 + og_ * 512 + 512],
                            ALU.add), waits=[done] + list(BV_RAW))
                        return vop(lambda v, og_=og, j_=i_: v.tensor_tensor(
                            racc[:, j_ * D + og_ * 512: j_ * D + og_ * 512 + 512],
                            s5[:, og_ * 512: og_ * 512 + 512],
                            racc[:, j_ * D + og_ * 512: j_ * D + og_ * 512 + 512],
                            ALU.add))
                    return vop(lambda v, a=pb, og_=og, j_=i_: v.tensor_tensor(
                        racc[:, j_ * D + og_ * 512: j_ * D + og_ * 512 + 512], a[:, 0:512],
                        racc[:, j_ * D + og_ * 512: j_ * D + og_ * 512 + 512], ALU.add),
                        waits=[done])

                project(NCH_M, lambda c: tb[:, M_MIDT + c * P: M_MIDT + (c + 1) * P],
                        lambda c, col0, wdt: war[:, M_WPR + c * D + col0:
                                                 M_WPR + c * D + col0 + wdt],
                        [512] * 2, None,
                        MIDT_OK + [WPR_RAW] + list(BV_RAW), pr_evac)
                if i == 0 and H == 0 and "mid" in dmp:
                    dma("sync", dmp["mid"][0:P, :], tb[:, M_MID:M_MID + IH_M],
                        waits=[MID_OK], inc="ho")
        RACC_M = S.now("vv")[0]

        xqT = x2T
        x_slots2 = Slots(2)
        for _sl in range(2):
            for g_ in S.now("vv", "ss"):
                x_slots2.guard[_sl].append(g_)
        xload2 = [None] * NT
        if "racc15" in dmp and stop >= 4:
            dma("sync", dmp["racc15"][0:P, :], racc[:, (NT - 1) * D: NT * D],
                waits=[RACC_M], inc="ho")
        if stop >= 4:
            sl0, g0 = x_slots2.use()
            dma("sync", xt[:, ts(sl0, D)], x_hbm[0:P, :], waits=list(g0), inc="hx")
            xload2[0] = (sl0, S.now("hx")[0])
        for i in range(NT if stop >= 4 else 0):
            xsl, xrdy = xload2[i]
            if i + 1 < NT:
                sl_, g_ = x_slots2.use()
                dma("sync", xt[:, ts(sl_, D)], x_hbm[ts(i + 1, P), :], waits=list(g_), inc="hx")
                xload2[i + 1] = (sl_, S.now("hx")[0])
            xtile = xt[:, ts(xsl, D)]
            rtile = racc[:, ts(i, D)]
            sad = vop(lambda v, a=xtile, b=rtile: v.tensor_tensor(s5[:, :], a, b, ALU.add),
                      waits=[xrdy, RACC_M] + S.now("ss"))
            x_slots2.consumed(xsl, sad)
            x4_ok = ln_apply(s5[:, :], rtile, D, [sad], [sad], out_guards=[], jkc=F_JK)
            qd = quant_emit(rtile, D, tb[:, F_XQD:F_XQD + D], [x4_ok] + S.now("pe"))
            for c in range(8):
                do_transpose(tb[:, F_XQD + c * P: F_XQD + (c + 1) * P], P,
                             xqT[:, c * n_tok + i * P: c * n_tok + (i + 1) * P],
                             [qd] if c == 0 else [], "v" if c % 2 == 0 else "s")
            if i == 0:
                if "x4" in dmp:
                    dma("sync", dmp["x4"][0:P, :], rtile, waits=[x4_ok], inc="ho")
                if "xq" in dmp:
                    dma("sync", dmp["xq"][0:P, :], tb[:, F_XQD:F_XQD + D],
                        waits=[qd], inc="ho")
            if i == NT - 1:
                if "s5f" in dmp:
                    dma("sync", dmp["s5f"][0:P, :], s5[:, :], waits=[x4_ok], inc="ho")
                if "x4f" in dmp:
                    dma("sync", dmp["x4f"][0:P, :], rtile, waits=[x4_ok], inc="ho")
        XQT_OK = S.now("vv", "ss")

        stg = Slots(2)
        stgq = [("sync", "hx"), ("gpsimd", "gw")]
        nst = [0]
        for H in range(2 if stop >= 5 else 0):
            phase_gate = S.now("pe", "vv", "ss")
            nfact = [0]
            ms = {"g": [], "u": [], "d": []}
            for wkey, wdram, base in (("g", wg_n, F_WG), ("u", wu_n, F_WU)):
                for j in range(NCH_F):
                    slg, gg_ = stg.use()
                    qe, qsem = stgq[nst[0] % 2]
                    nst[0] += 1
                    dma(qe, xt[:, ts(slg, D)],
                        bass.AP(wdram, (H * IH_F + j * P) * D, [[D, P], [1, D]]),
                        waits=list(gg_) + (phase_gate if nfact[0] < 2 else []),
                        inc=qsem)
                    nfact[0] += 1
                    srdy = S.now(qsem)[0]
                    qd = quant_emit(xt[:, ts(slg, D)], D, tb[:, F_WQN:F_WQN + D],
                                    [srdy], xn32=s5[:, 0:D], v32=elu_f[:, 0:D])
                    stg.consumed(slg, qd)
                    for c in range(8):
                        do_transpose(tb[:, F_WQN + c * P: F_WQN + (c + 1) * P], P,
                                     war[:, base + c * IH_F + j * P: base + c * IH_F + (j + 1) * P],
                                     [qd] if c == 0 else [], "v" if c % 2 == 0 else "s")
                    ms[wkey].append(S.now("vv", "ss"))
            for j in range(8):
                sl0, g0 = stg.use()
                sl1, g1 = stg.use()
                qe, qsem = stgq[nst[0] % 2]
                nst[0] += 1
                dma(qe, xt[:, 0:IH_F],
                    bass.AP(wd_n, j * P * DFF + H * IH_F, [[DFF, P], [1, IH_F]]),
                    waits=list(g0) + list(g1) + (phase_gate if nfact[0] < 2 else []),
                    inc=qsem)
                nfact[0] += 1
                srdy = S.now(qsem)[0]
                HW_ = IH_F // 2
                quant_emit(xt[:, 0:HW_], HW_, tb[:, F_WQN:F_WQN + HW_], [srdy],
                           xn32=s5[:, 0:HW_], v32=elu_f[:, 0:HW_])
                qd = quant_emit(xt[:, HW_:IH_F], HW_, tb[:, F_WQN + HW_:F_WQN + IH_F],
                                [srdy], xn32=s5[:, 0:HW_], v32=elu_f[:, 0:HW_])
                stg.consumed(sl0, qd)
                stg.consumed(sl1, qd)
                for c in range(NCH_F):
                    do_transpose(tb[:, F_WQN + c * P: F_WQN + (c + 1) * P], P,
                                 war[:, F_WD + c * D + j * P: F_WD + c * D + (j + 1) * P],
                                 [qd] if c == 0 else [], "v" if c % 2 == 0 else "s")
                ms["d"].append(S.now("vv", "ss"))
            WQ_OK = S.now("vv", "ss")

            def wq_gate(wkey, col0, wdt):
                j_hi = min((col0 + wdt - 1) // P, len(ms[wkey]) - 1)
                return ms[wkey][j_hi]
            if H == 0 and "wgq" in dmp:
                dma("sync", dmp["wgq"][0:P, :], war[:, F_WG:F_WG + IH_F], waits=WQ_OK, inc="ho")
            if H == 0 and "wuq" in dmp:
                dma("sync", dmp["wuq"][0:P, :], war[:, F_WU:F_WU + 8 * IH_F], waits=WQ_OK, inc="ho")

            OGW = [512, 512, 384]
            HTB = (F_HT, F_HQ)

            def stageF(i):
                par = i % 2
                hb = HTB[par]

                def gate_evac(og, pb, wdt, done, col0):
                    w8 = ([tile_qv[0]] if tile_qv[0] else []) + \
                         ([ht_guard[par]] if ht_guard[par] else [])
                    return sop(lambda s, a=pb, c0=col0, w=wdt: s.activation(
                        tb[:, hb + c0: hb + c0 + w], a[:, 0:w], AF.Silu),
                        waits=[done] + w8)

                project(8, lambda c: xqT[:, c * n_tok + i * P: c * n_tok + (i + 1) * P],
                        lambda c, col0, wdt: war[:, F_WG + c * IH_F + col0:
                                                 F_WG + c * IH_F + col0 + wdt],
                        OGW, None, XQT_OK, gate_evac,
                        og_waits=lambda og, c0, w: wq_gate("g", c0, w))
                SG_OK = S.now("ss")[0]
                if H == 0 and i == 0 and "sg" in dmp:
                    dma("sync", dmp["sg"][0:P, :], tb[:, hb:hb + IH_F],
                        waits=[SG_OK], inc="ho")
                    qstate["sd"] = S.now("ho")[0]

                def up_evac(og, pb, wdt, done, col0):
                    return vop(lambda v, a=pb, c0=col0, w=wdt: v.tensor_tensor(
                        tb[:, hb + c0: hb + c0 + w], a[:, 0:w],
                        tb[:, hb + c0: hb + c0 + w], ALU.mult),
                        waits=[done, SG_OK])

                project(8, lambda c: xqT[:, c * n_tok + i * P: c * n_tok + (i + 1) * P],
                        lambda c, col0, wdt: war[:, F_WU + c * IH_F + col0:
                                                 F_WU + c * IH_F + col0 + wdt],
                        OGW, None, [], up_evac,
                        og_waits=lambda og, c0, w: wq_gate("u", c0, w))
                return dict(H_OK=S.now("vv")[0], hb=hb, par=par)

            def coreF(i, h):
                hb = h["hb"]
                qd = quant_emit(tb[:, hb:hb + IH_F], IH_F, tb[:, hb:hb + IH_F],
                                [h["H_OK"]])
                tile_qv[0] = qd
                for c in range(NCH_F):
                    do_transpose(tb[:, hb + c * P: hb + (c + 1) * P], P,
                                 tb[:, F_HQT + c * P: F_HQT + (c + 1) * P],
                                 [qd] if c == 0 else [], "v" if c % 2 == 0 else "s")
                ht_guard[h["par"]] = S.now("pe")[0]
                HQT_OK = S.now("vv", "ss")

                def dn_evac(og, pb, wdt, done, col0, i_=i):
                    return vop(lambda v, a=pb, og_=og, j_=i_: v.tensor_tensor(
                        racc[:, j_ * D + og_ * 512: j_ * D + og_ * 512 + 512], a[:, 0:512],
                        racc[:, j_ * D + og_ * 512: j_ * D + og_ * 512 + 512], ALU.add),
                        waits=[done])

                project(NCH_F, lambda c: tb[:, F_HQT + c * P: F_HQT + (c + 1) * P],
                        lambda c, col0, wdt: war[:, F_WD + c * D + col0:
                                                 F_WD + c * D + col0 + wdt],
                        [512] * 2, None, HQT_OK, dn_evac,
                        og_waits=lambda og, c0, w: wq_gate("d", c0, w))
                if H == 1 and stop >= 6:
                    DN_OK = S.now("vv")[0]
                    og_ = [ob_guard[0]] if ob_guard[0] else []
                    ev = ln_apply(racc[:, ts(i, D)], outb[:, :], D,
                                  [], [DN_OK] + (list(WQ_OK) if i == 0 else []),
                                  out_guards=og_, jkc=F_JK)
                    dma("sync", out_hbm[ts(i, P), :], outb[:, :], waits=[ev], inc="ho")
                    ob_guard[0] = S.now("ho")[0]

            hF = stageF(0)
            for i in range(NT):
                hN = stageF(i + 1) if i + 1 < NT else None
                coreF(i, hF)
                hF = hN
        X5_OK = S.now("vv")[0]

        if stop < 6:
            for i in range(NT):
                og_ = [ob_guard[0]] if ob_guard[0] else []
                gate = S.now("vv", "ss", "pe")
                ev = sop(lambda s_, b=racc[:, ts(i, D)]: s_.copy(outb[:, :], b),
                         waits=(gate if i == 0 else []) + og_)
                dma("sync", out_hbm[ts(i, P), :], outb[:, :], waits=[ev], inc="ho")
                ob_guard[0] = S.now("ho")[0]
        FINAL_HO = S.at("ho")

        def realize(eng_name):
            def body(eng):
                for fn, waits, inc, amt in S.ops[eng_name]:
                    for (sn, v) in waits:
                        eng.wait_ge(sems[sn], v)
                    r = fn(eng)
                    if inc is not None:
                        r.then_inc(sems[inc], amt)
                if eng_name == "sync":
                    eng.wait_ge(sems["ho"], FINAL_HO)
            return body

        blk.sync(realize("sync"))
        blk.gpsimd(realize("gpsimd"))
        blk.tensor(realize("tensor"))
        blk.vector(realize("vector"))
        blk.scalar(realize("scalar"))

    return nc


def _prep_inputs(L0_attn_x, L0_attn_ln1_w, L0_attn_ln1_b, L0_attn_q_w, L0_attn_q_b,
                 L0_attn_k_w, L0_attn_k_b, L0_attn_v_w, L0_attn_v_b, L0_attn_o_w,
                 L0_attn_o_b, L0_attn_ln2_w, L0_attn_ln2_b, L0_attn_fc_w, L0_attn_fc_b,
                 L0_attn_proj_w, L0_attn_proj_b, L0_ffn_w_gate, L0_ffn_w_up, L0_ffn_w_down):
    f = np.float32
    c = np.ascontiguousarray

    def t(a):
        return c(np.asarray(a).T.astype(f, copy=False))

    base = {
        "wqkv_t": c(np.concatenate([np.asarray(L0_attn_q_w), np.asarray(L0_attn_k_w),
                                    np.asarray(L0_attn_v_w)], axis=0).T.astype(f, copy=False)),
        "wo_t": t(L0_attn_o_w),
        "wfc_t": t(L0_attn_fc_w),
        "wproj_t": t(L0_attn_proj_w),
        "wg_n": c(np.asarray(L0_ffn_w_gate).astype(f, copy=False)),
        "wu_n": c(np.asarray(L0_ffn_w_up).astype(f, copy=False)),
        "wd_n": c(np.asarray(L0_ffn_w_down).astype(f, copy=False)),
        "ln1w": c(np.asarray(L0_attn_ln1_w).reshape(8, P).T.astype(f, copy=False)),
        "ln1b": c(np.asarray(L0_attn_ln1_b).reshape(8, P).T.astype(f, copy=False)),
        "ln2w": c(np.asarray(L0_attn_ln2_w).reshape(8, P).T.astype(f, copy=False)),
        "ln2b": c(np.asarray(L0_attn_ln2_b).reshape(8, P).T.astype(f, copy=False)),
        "bqkv": c(np.concatenate([np.asarray(L0_attn_q_b), np.asarray(L0_attn_k_b),
                                  np.asarray(L0_attn_v_b)]).reshape(1, -1).astype(f, copy=False)),
        "bo": c(np.asarray(L0_attn_o_b).reshape(1, -1).astype(f, copy=False)),
        "bfc": c(np.asarray(L0_attn_fc_b).reshape(1, -1).astype(f, copy=False)),
        "bproj": c(np.asarray(L0_attn_proj_b).reshape(1, -1).astype(f, copy=False)),
    }
    xfull = np.ascontiguousarray(np.asarray(L0_attn_x).astype(f, copy=False).reshape(-1, D))
    return xfull, base


def kernel(**inputs):
    xfull, base = _prep_inputs(**inputs)
    n_tok = xfull.shape[0] // NCORES
    shards = np.split(xfull, NCORES, axis=0)
    nc = build(n_tok=n_tok)
    in_maps = [dict(base, x=np.ascontiguousarray(s)) for s in shards]
    res = run_bass_kernel_spmd(nc, in_maps, core_ids=list(range(NCORES)))
    out = np.concatenate([np.asarray(r["out"]) for r in res.results], axis=0)
    return out.reshape(np.asarray(inputs["L0_attn_x"]).shape)


# version 23 (final)
# speedup vs baseline: 1.0226x; 1.0028x over previous
import numpy as np
import concourse.bass as bass
import concourse.mybir as mybir
from concourse.bass_utils import run_bass_kernel_spmd

F32 = mybir.dt.float32
BF16 = mybir.dt.bfloat16
AF = mybir.ActivationFunctionType
ALU = mybir.AluOpType
AX = mybir.AxisListType

P = 128
D = 1024
NH = 16
HD = 64
DFC = 4096
DFF = 2816
EPS = 1e-5
NCORES = 8

ENGS = ("sync", "gpsimd", "tensor", "vector", "scalar")


class Sched:
    def __init__(self):
        self.ops = {e: [] for e in ENGS}
        self.counts = {}

    def at(self, sem):
        return self.counts.get(sem, 0)

    def now(self, *sems):
        return [(s, self.at(s)) for s in sems]

    def emit(self, eng, fn, waits=(), inc=None, amt=1):
        ws = [(s, v) for (s, v) in waits if v > 0]
        self.ops[eng].append((fn, ws, inc, amt))
        if inc is not None:
            self.counts[inc] = self.counts.get(inc, 0) + amt
        return (inc, self.counts.get(inc, 0)) if inc else None


class Slots:
    def __init__(self, n):
        self.n = n
        self.guard = [[] for _ in range(n)]
        self.i = 0

    def use(self):
        s = self.i % self.n
        self.i += 1
        w = self.guard[s]
        self.guard[s] = []
        return s, w

    def consumed(self, s, sem_val):
        if sem_val is not None:
            self.guard[s].append(sem_val)


def ts(i, n):
    return slice(i * n, (i + 1) * n)


def build(n_tok=2048, dump=(), stop=6, asub=7):
    assert n_tok % P == 0
    NT = n_tok // P
    IH_M = DFC // 2
    NCH_M = IH_M // P
    IH_F = DFF // 2
    NCH_F = IH_F // P

    nc = bass.Bass("TRN2", target_bir_lowering=False, detect_race_conditions=False)

    x_hbm = nc.dram_tensor("x", [n_tok, D], F32, kind="ExternalInput")
    wqkv = nc.dram_tensor("wqkv_t", [D, 3 * D], F32, kind="ExternalInput")
    wo = nc.dram_tensor("wo_t", [D, D], F32, kind="ExternalInput")
    wfc = nc.dram_tensor("wfc_t", [D, DFC], F32, kind="ExternalInput")
    wproj = nc.dram_tensor("wproj_t", [DFC, D], F32, kind="ExternalInput")
    wg_n = nc.dram_tensor("wg_n", [DFF, D], F32, kind="ExternalInput")
    wu_n = nc.dram_tensor("wu_n", [DFF, D], F32, kind="ExternalInput")
    wd_n = nc.dram_tensor("wd_n", [D, DFF], F32, kind="ExternalInput")
    ln1w = nc.dram_tensor("ln1w", [P, 8], F32, kind="ExternalInput")
    ln1b = nc.dram_tensor("ln1b", [P, 8], F32, kind="ExternalInput")
    ln2w = nc.dram_tensor("ln2w", [P, 8], F32, kind="ExternalInput")
    ln2b = nc.dram_tensor("ln2b", [P, 8], F32, kind="ExternalInput")
    bqkv = nc.dram_tensor("bqkv", [1, 3 * D], F32, kind="ExternalInput")
    bo = nc.dram_tensor("bo", [1, D], F32, kind="ExternalInput")
    bfc = nc.dram_tensor("bfc", [1, DFC], F32, kind="ExternalInput")
    bproj = nc.dram_tensor("bproj", [1, D], F32, kind="ExternalInput")
    out_hbm = nc.dram_tensor("out", [n_tok, D], F32, kind="ExternalOutput")
    dmp = {}
    for name, shape, dt in dump:
        dmp[name] = nc.dram_tensor("dump_" + name, shape, dt, kind="ExternalOutput")

    from contextlib import ExitStack
    with ExitStack() as ctx:
        e = ctx.enter_context

        racc = e(nc.sbuf_tensor("racc", [P, NT * D], BF16))
        x2T = e(nc.sbuf_tensor("x2T", [P, 8 * n_tok], BF16))
        WARC = 36992
        war = e(nc.sbuf_tensor("war", [P, WARC], BF16))
        TBC = 17664
        tb = e(nc.sbuf_tensor("tb", [P, TBC], BF16))
        xt = e(nc.sbuf_tensor("xt", [P, 2 * D], F32))
        s5 = e(nc.sbuf_tensor("s5", [P, D], F32))
        elu_f = e(nc.sbuf_tensor("elu_f", [P, 1024], F32))
        outb = elu_f
        stage = None
        st = e(nc.sbuf_tensor("st", [P, 48], F32))
        qs = e(nc.sbuf_tensor("qs", [P, 3 * 96], F32))
        ident = e(nc.sbuf_tensor("ident", [P, P], BF16))
        maskS = e(nc.sbuf_tensor("maskS", [P, P], BF16))
        lnv = e(nc.sbuf_tensor("lnv", [P, 32], F32))
        lnb_bf = e(nc.sbuf_tensor("lnb_bf", [P, 16], BF16))
        brf = e(nc.sbuf_tensor("brf", [1, 512], F32))
        brf2 = e(nc.sbuf_tensor("brf2", [1, 512], F32))
        biasf = e(nc.sbuf_tensor("biasf", [1, 3 * D], BF16))

        pb0 = e(nc.psum_tensor("pb0", [P, 512], F32))
        pb1 = e(nc.psum_tensor("pb1", [P, 512], F32))
        ptr0 = e(nc.psum_tensor("ptr0", [P, 1024], BF16))
        ptr1 = e(nc.psum_tensor("ptr1", [P, 1024], BF16))
        ptr2 = e(nc.psum_tensor("ptr2", [P, 1024], BF16))
        ps1a = e(nc.psum_tensor("ps1a", [P, 512], F32))
        ps1b = e(nc.psum_tensor("ps1b", [P, 512], F32))
        ps2 = e(nc.psum_tensor("ps2", [P, 512], F32))
        pbias = ps2

        sems = {}
        for sname in ("hx", "ho", "gw", "gm", "pe", "vv", "ss", "gg"):
            sems[sname] = e(nc.semaphore(sname))
        blk = e(nc.Block())

        S = Sched()
        PB = (pb0, pb1)
        pbig = Slots(2)
        ptrs = Slots(3)
        PTR = None
        ps1s = Slots(2)
        TBP = TBC

        A_XN, A_XNT, A_QKV, A_QT, A_KT, A_VG, A_SS, A_AM, A_AT, A_ER, A_EE, A_JK = (
            0, 1024, 2048, 5120, 7168, 9216, 10256, 10512, 11536, 13584, 14608, 15632)
        M_XN, M_MID, M_MIDT, M_JK = 0, 1024, 3072, 15632
        F_XQD, F_QXN, F_QV, F_B1, F_B2, F_B3, F_B4, F_B5, F_WQN, F_HT, F_HQ, F_HQT, F_JK = (
            0, 1024, 2432, 3840, 5248, 6656, 8064, 9472, 10880, 12288, 13696, 15104, 16512)
        A_WQKV, A_WO, A_BR = 0, 24576, 32768
        M_WFC, M_WPR, M_BR = 0, 16384, 32768
        F_WG, F_WU, F_WD = 0, 11264, 22528

        def mm(psum_ap, lhsT_ap, rhs_ap, start, stop, waits=(), inc=None, tp=None):
            def fn(t, a=psum_ap, b=lhsT_ap, c=rhs_ap, st_=start, sp=stop, tp_=tp):
                return t.matmul(a, b, c, start=st_, stop=sp, tile_position=tp_)
            return S.emit("tensor", fn, waits, inc)

        def vop(fn, waits=()):
            return S.emit("vector", fn, waits, "vv")

        def sop(fn, waits=()):
            return S.emit("scalar", fn, waits, "ss")

        def gop(fn, waits=()):
            return S.emit("gpsimd", fn, waits, "gg")

        def dma(eng, out_ap, in_ap, waits=(), inc="hx"):
            def fn(g, a=out_ap, b=in_ap):
                return g.dma_start(out=a, in_=b)
            return S.emit(eng, fn, waits, inc, amt=16)

        gop(lambda g: g.memset(ident[:, :], 0.0))
        gop(lambda g: g.affine_select(out=ident[:, :], in_=ident[:, :],
                                      compare_op=ALU.not_equal, fill=1.0, base=0,
                                      pattern=[[-1, P]], channel_multiplier=1),
            waits=S.now("gg"))
        Bm = tb[0:8, 0:P]
        gop(lambda g: g.memset(Bm, 1.0), waits=S.now("gg"))
        gop(lambda g: g.affine_select(out=Bm, in_=Bm, compare_op=ALU.is_ge,
                                      fill=0.0, base=0, pattern=[[1, P]],
                                      channel_multiplier=-16), waits=S.now("gg"))
        gop(lambda g: g.affine_select(out=Bm, in_=Bm, compare_op=ALU.is_ge,
                                      fill=0.0, base=15, pattern=[[-1, P]],
                                      channel_multiplier=16), waits=S.now("gg"))
        def _mask_mm(t):
            return t.matmul(ps1a[:, 0:P], Bm, Bm, start=True, stop=True)
        S.emit("tensor", _mask_mm, S.now("gg"), "pe")
        vop(lambda v: v.tensor_copy(maskS[:, :], ps1a[:, 0:P]), waits=S.now("pe"))
        for G in range(16):
            gop(lambda g, G_=G: g.memset(tb[:, A_VG + G_ * 65 + 64: A_VG + G_ * 65 + 65], 1.0))
        CONST = S.now("gg")[0]
        CONSTV = S.now("vv")[0]

        dma("sync", lnv[:, 0:8], ln1w[:, :])
        dma("sync", lnv[:, 8:16], ln1b[:, :])
        dma("sync", lnv[:, 16:24], ln2w[:, :])
        dma("sync", lnv[:, 24:32], ln2b[:, :])
        LNV = S.now("hx")[0]
        dma("gpsimd", lnb_bf[:, 0:8], ln1b[:, :], inc="gw")
        dma("gpsimd", lnb_bf[:, 8:16], ln2b[:, :], inc="gw")
        LNBF = S.now("gw")[0]

        ln_state = {"i": 0, "act": {0: None, 1: None}}

        def ln_apply(src_ap, out_ap, width, waits_v, waits_s, out_guards=(), jkc=None):
            n = ln_state["i"]
            ln_state["i"] += 1
            b = 16 * (n % 2)
            g = ln_state["act"][n % 2]
            gg = [g] if g else []
            jk = tb[:, jkc:jkc + width]
            c_sum, c_sq, c_mu, c_var, c_t, c_inv, c_nmi = (b, b + 2, b + 4, b + 6,
                                                           b + 8, b + 10, b + 12)
            red = vop(lambda v, a=src_ap: v.reduce_sum(st[:, c_sum:c_sum + 1], a,
                                                       axis=AX.X),
                      waits=list(waits_v) + gg)
            sop(lambda s, a=src_ap: s.activation(jk, a, AF.Square,
                                                 accum_out=st[:, c_sq:c_sq + 1]),
                waits=list(waits_s) + gg)
            sqd = S.now("ss")[0]
            w1 = vop(lambda v: v.tensor_scalar(st[:, c_mu:c_mu + 1],
                                               st[:, c_sum:c_sum + 1],
                                               1.0 / width, None, ALU.mult),
                     waits=[red])
            w2 = vop(lambda v: v.tensor_scalar(st[:, c_var:c_var + 1],
                                               st[:, c_sq:c_sq + 1],
                                               1.0 / width, None, ALU.mult),
                     waits=[sqd])
            w3 = vop(lambda v: v.tensor_mul(st[:, c_t:c_t + 1], st[:, c_mu:c_mu + 1],
                                            st[:, c_mu:c_mu + 1]), waits=[w1])
            w4 = vop(lambda v: v.tensor_tensor(st[:, c_var:c_var + 1],
                                               st[:, c_var:c_var + 1],
                                               st[:, c_t:c_t + 1], ALU.subtract),
                     waits=[w2, w3])
            ved = vop(lambda v: v.tensor_scalar(st[:, c_var:c_var + 1],
                                                st[:, c_var:c_var + 1],
                                                EPS, None, ALU.add), waits=[w4])
            sop(lambda s: s.activation(st[:, c_t:c_t + 1], st[:, c_var:c_var + 1],
                                       AF.Ln), waits=[ved])
            sop(lambda s: s.activation(st[:, c_inv:c_inv + 1], st[:, c_t:c_t + 1],
                                       AF.Exp, scale=np.float32(-0.5)))
            inv = S.now("ss")[0]
            w5 = vop(lambda v: v.tensor_mul(st[:, c_nmi:c_nmi + 1],
                                            st[:, c_mu:c_mu + 1],
                                            st[:, c_inv:c_inv + 1]),
                     waits=[inv, w1])
            nmi = vop(lambda v: v.tensor_scalar(st[:, c_nmi:c_nmi + 1],
                                                st[:, c_nmi:c_nmi + 1],
                                                -1.0, None, ALU.mult), waits=[w5])
            act = sop(lambda s, a=src_ap, o=out_ap: s.activation(
                o, a, AF.Identity, bias=st[:, c_nmi:c_nmi + 1],
                scale=st[:, c_inv:c_inv + 1]),
                waits=[nmi] + list(out_guards))
            ln_state["act"][n % 2] = act
            return act

        def do_transpose(src_ap, width, dst_ap, waits, ee):
            sl, g = ptrs.use()
            pt = (ptr0, ptr1, ptr2)[sl]
            transpose_ap = pt[0:width, 0:P]

            def fn(t, a=transpose_ap, b=src_ap):
                return t.transpose(a, b, ident[:, :])
            S.emit("tensor", fn, list(waits) + list(g), "pe")
            ped = S.now("pe")[0]
            if ee == "v":
                ev = vop(lambda v, a=dst_ap, b=transpose_ap: v.tensor_copy(a, b),
                         waits=[ped])
            else:
                ev = sop(lambda s, a=dst_ap, b=transpose_ap: s.copy(a, b), waits=[ped])
            ptrs.consumed(sl, ev)
            return ev

        def project(nchunks, lhsT_of, rhs_of, out_widths, bias_col, first_waits, evac,
                    og_waits=None):
            col0 = 0
            for og, wdt in enumerate(out_widths):
                sl, guard = pbig.use()
                pb = PB[sl]
                fw = list(first_waits) + list(guard)
                if og_waits is not None:
                    fw += list(og_waits(og, col0, wdt))
                first = True
                for c in range(nchunks):
                    last = c == nchunks - 1
                    mm(pb[:, 0:wdt], lhsT_of(c), rhs_of(c, col0, wdt), first, last,
                       waits=fw if first else [], inc="pe" if last else None)
                    first, fw = False, []
                done = S.now("pe")[0]
                ev = evac(og, pb, wdt, done, col0)
                pbig.consumed(sl, ev)
                col0 += wdt

        def quant_emit(src2d, W, dst2d, waits_v, xn32=None, v32=None):
            nb = W // 16
            src3 = src2d.rearrange("p (b k) -> p b k", k=16)
            QXN = xn32 if xn32 is not None else tb[:, F_QXN:F_QXN + W]
            QV = v32 if v32 is not None else tb[:, F_QV:F_QV + W]
            B1 = tb[:, F_B1:F_B1 + W]
            B2 = tb[:, F_B2:F_B2 + W]
            B3 = tb[:, F_B3:F_B3 + W]
            B4 = tb[:, F_B4:F_B4 + W]
            B5 = tb[:, F_B5:F_B5 + W]
            bmr = vop(lambda v: v.tensor_reduce(qs[:, 0:nb], src3, axis=AX.X,
                                                op=ALU.max, apply_absolute_value=True),
                      waits=waits_v)
            scw = vop(lambda v: v.tensor_scalar(qs[:, 96:96 + nb], qs[:, 0:nb],
                                                1.0 / 6.0, 1e-12, ALU.mult, ALU.max),
                      waits=[bmr])
            rsr = vop(lambda v: v.reciprocal(qs[:, 192:192 + nb], qs[:, 96:96 + nb]),
                      waits=[scw])
            pv = qstate["sd"]
            vop(lambda v: v.tensor_tensor(
                QXN.rearrange("p (b k) -> p b k", k=16), src3,
                qs[:, 192:192 + nb].unsqueeze(2).broadcast_to([P, nb, 16]), ALU.mult),
                waits=([pv] if pv else []) + [rsr])
            xnd = S.now("vv")[0]
            sop(lambda s: s.activation(QV, QXN, AF.Abs), waits=[xnd])
            sop(lambda s: s.activation(B5, QXN, AF.Sign))
            sd = S.now("ss")[0]
            pvg = qstate["s_guard"]
            qstate["sd"] = sd
            NUD = 1.0 + 2.0 ** -40
            if True:
                vop(lambda v: v.tensor_scalar(B1, QV, 0.25 * NUD, None, ALU.is_gt),
                    waits=[sd])
                vop(lambda v: v.tensor_scalar(B2, QV, 0.75 * NUD, None, ALU.is_gt))
                vop(lambda v: v.tensor_add(B1, B1, B2))
                vop(lambda v: v.tensor_scalar(B2, QV, 1.25 * NUD, None, ALU.is_gt))
                vop(lambda v: v.tensor_scalar(B3, QV, 1.75 * NUD, None, ALU.is_gt))
                vop(lambda v: v.tensor_add(B2, B2, B3))
                vop(lambda v: v.tensor_add(B1, B1, B2))
                vop(lambda v: v.tensor_scalar(B2, QV, 2.5 * NUD, None, ALU.is_gt))
                vop(lambda v: v.tensor_scalar(B3, QV, 3.5 * NUD, None, ALU.is_gt))
                vop(lambda v: v.tensor_scalar(B4, QV, 5.0 * NUD, None, ALU.is_gt))
                vop(lambda v: v.scalar_tensor_tensor(B3, B4, 2.0, B3, ALU.mult, ALU.add))
                vop(lambda v: v.tensor_add(B3, B3, B2))
                vop(lambda v: v.scalar_tensor_tensor(B1, B1, 0.5, B3, ALU.mult, ALU.add))
            vop(lambda v: v.tensor_mul(B1, B1, B5))
            qv = vop(lambda v, d=dst2d: v.tensor_tensor(
                d.rearrange("p (b k) -> p b k", k=16),
                B1.rearrange("p (b k) -> p b k", k=16),
                qs[:, 96:96 + nb].unsqueeze(2).broadcast_to([P, nb, 16]), ALU.mult),
                waits=S.now("pe"))
            qstate["s_guard"] = qv
            return qv

        qstate = {"s_guard": None, "sd": None}
        tile_qv = [None]
        ht_guard = [None, None]
        ob_guard = [None]

        for c in range(8):
            dma("gpsimd", war[:, A_WQKV + c * 3072: A_WQKV + (c + 1) * 3072],
                bass.AP(wqkv, c * P * 3 * D, [[3 * D, P], [1, 3 * D]]), inc="gw")
        WQKV_RAW = S.now("gw")[0]
        for c in range(8):
            dma("gpsimd", war[:, A_WO + c * D: A_WO + (c + 1) * D],
                bass.AP(wo, c * P * D, [[D, P], [1, D]]), inc="gw")
        dma("gpsimd", biasf[0:1, 0:3072], bqkv[:, :], inc="gw")
        dma("gpsimd", war[:, A_BR + 3072: A_BR + 4096],
            bass.AP(bo, 0, [[D, 1], [0, P], [1, D]]), inc="gw")
        WA_RAW = S.now("gw")[0]
        for og in range(6):
            for c in range(8):
                mm(pbias[0:1, 0:512], lnb_bf[:, c:c + 1],
                   war[:, A_WQKV + c * 3072 + og * 512: A_WQKV + c * 3072 + og * 512 + 512],
                   c == 0, c == 7,
                   waits=([LNBF, WQKV_RAW] + S.now("vv")) if c == 0 else [],
                   inc="pe" if c == 7 else None)
            done = S.now("pe")[0]
            bcw = vop(lambda v: v.tensor_copy(brf[0:1, 0:512], pbias[0:1, 0:512]),
                      waits=[done, WA_RAW])
            vop(lambda v, og_=og: v.tensor_tensor(
                brf2[0:1, 0:512], brf[0:1, 0:512],
                biasf[0:1, og_ * 512: og_ * 512 + 512], ALU.add),
                waits=S.now("gw") + [bcw])
            dma("gpsimd", war[:, A_BR + og * 512: A_BR + og * 512 + 512],
                bass.AP(brf2, 0, [[512, 1], [0, P], [1, 512]]),
                waits=S.now("vv"), inc="gw")
        BROW_A = S.now("vv", "gw")
        for c in range(8):
            vop(lambda v, c_=c: v.tensor_scalar(
                war[:, A_WQKV + c_ * 3072: A_WQKV + (c_ + 1) * 3072],
                war[:, A_WQKV + c_ * 3072: A_WQKV + (c_ + 1) * 3072],
                lnv[:, c_:c_ + 1], None, ALU.mult),
                waits=BROW_A + [LNV])
        WQKV_OK = S.now("vv")[0]

        x_slots = Slots(2)

        def load_x(i, slots):
            sl, g = slots.use()
            dma("sync", xt[:, ts(sl, D)], x_hbm[ts(i, P), :], waits=g, inc="hx")
            return sl, S.now("hx")[0]

        xload = [load_x(0, x_slots)] if stop >= 2 else []
        elu_guard = {0: None, 1: None}
        rec_guard = [None] * 4
        vg_guard = [None]

        stA = {}
        um_guard = [None]

        def emit_stageA(i):
            xsl, xrdy = xload[i]
            xtile = xt[:, ts(xsl, D)]
            XN = tb[:, A_XN:A_XN + D]
            gm_prev = S.now("gm")[0]
            xn_ok = ln_apply(xtile, XN, D, [xrdy], [xrdy], out_guards=S.now("pe"), jkc=A_JK)
            if i == 0 and "xn" in dmp:
                dma("sync", dmp["xn"][0:P, :], XN, waits=[xn_ok], inc="ho")
            if i == 0 and "st" in dmp:
                dma("sync", dmp["st"][0:P, :], st[:, 0:32], waits=[xn_ok], inc="ho")
            for c in range(8):
                do_transpose(XN[:, ts(c, P)], P, tb[:, A_XNT + c * P: A_XNT + (c + 1) * P],
                             [xn_ok] if c == 0 else [], "v" if c % 2 == 0 else "s")
            XNT_OK = S.now("vv", "ss")

            def qkv_evac(og, pb, wdt, done, col0):
                bias = war[:, A_BR + og * 512: A_BR + og * 512 + 512]
                if og < 4:
                    par = og % 2
                    eg = elu_guard[par]
                    ef = elu_f[:, par * 512:(par + 1) * 512]
                    er = tb[:, A_ER + par * 512: A_ER + (par + 1) * 512]
                    ee = tb[:, A_EE + par * 512: A_EE + (par + 1) * 512]
                    td = vop(lambda v, a=pb, ef_=ef, b_=bias: v.tensor_tensor(
                        ef_, a[:, 0:512], b_, ALU.add),
                        waits=[done] + (list(eg) if eg else []))
                    rd = sop(lambda s, ef_=ef, er_=er: s.activation(er_, ef_, AF.Relu),
                             waits=[td])
                    md = vop(lambda v, ef_=ef: v.tensor_scalar(
                        ef_, ef_, 0.0, None, ALU.min), waits=[rd])
                    ed = sop(lambda s, ef_=ef, ee_=ee: s.activation(ee_, ef_, AF.Exp),
                             waits=[md])
                    ad = vop(lambda v, og_=og, er_=er, ee_=ee: v.tensor_add(
                        tb[:, A_QKV + og_ * 512:A_QKV + og_ * 512 + 512], er_, ee_),
                        waits=[ed, gm_prev])
                    elu_guard[par] = (ed, ad)
                    return ad
                else:
                    return vop(lambda v, a=pb, og_=og, b_=bias: v.tensor_tensor(
                        tb[:, A_QKV + og_ * 512:A_QKV + og_ * 512 + 512], a[:, 0:512],
                        b_, ALU.add),
                        waits=[done, gm_prev])

            qkv_evs = [None] * 6

            if asub >= 2:
                def qkv_evac_plain(og, pb, wdt, done, col0):
                    return sop(lambda s_, a=pb, og_=og: s_.copy(
                        tb[:, A_QKV + og_ * 512:A_QKV + og_ * 512 + 512], a[:, 0:512]),
                        waits=[done, gm_prev])
                _ev0 = qkv_evac if asub not in (20, 21) else qkv_evac_plain

                def _ev(og, pb, wdt, done, col0):
                    r = _ev0(og, pb, wdt, done, col0)
                    qkv_evs[og] = r
                    return r
                project(8, lambda c: tb[:, A_XNT + c * P: A_XNT + (c + 1) * P],
                        lambda c, col0, wdt: war[:, A_WQKV + c * 3072 + col0:
                                                 A_WQKV + c * 3072 + col0 + wdt],
                        [512] * 6, None, XNT_OK + [WQKV_OK] + BROW_A, _ev)
            QKV_OK = S.now("vv", "ss")
            if i == 0 and "qkv" in dmp:
                dma("sync", dmp["qkv"][0:P, :], tb[:, A_QKV:A_QKV + 3 * D],
                    waits=QKV_OK, inc="ho")
            return dict(xsl=xsl, xrdy=xrdy, xtile=xtile, QKV_OK=QKV_OK,
                        qkv_evs=qkv_evs)

        def emit_marshalA(i):
            h = stA[i]
            QKV_OK = h["QKV_OK"]
            evs = h["qkv_evs"]

            def ogw(og):
                return [evs[og]] if evs[og] else QKV_OK
            for hh in range(NH if asub >= 3 else 0):
                do_transpose(tb[:, A_QKV + hh * HD: A_QKV + (hh + 1) * HD], HD,
                             bass.AP(tb, A_QT + hh, [[TBP, 64], [P, 16], [16, 8]]),
                             ogw(hh // 8) if hh % 8 == 0 else [],
                             "v" if hh % 2 else "s")
            for hh in range(NH if asub >= 3 else 0):
                do_transpose(tb[:, A_QKV + D + hh * HD: A_QKV + D + (hh + 1) * HD], HD,
                             bass.AP(tb, A_KT + hh, [[TBP, 64], [P, 16], [16, 8]]),
                             ogw(2 + hh // 8) if hh % 8 == 0 else [],
                             "s" if hh % 2 else "v")
            h["QKT_OK"] = S.now("vv", "ss")

            vm_first = ([evs[4], evs[5]] if evs[5] else QKV_OK) + \
                       ([vg_guard[0]] if vg_guard[0] else []) + [CONST]
            h["gm_base"] = S.at("gm")
            for G in range(16 if asub >= 4 else 0):
                dma("gpsimd",
                    bass.AP(tb, A_VG + G * 65, [[TBP, P], [1, HD]]),
                    bass.AP(tb, (8 * G) * TBP + A_QKV + 2 * D, [[TBP, 8], [HD, NH], [1, HD]]),
                    waits=vm_first if G == 0 else [], inc="gm")

        def emit_coreA(i):
            h = stA[i]
            QKT_OK = h["QKT_OK"]
            gm_base = h["gm_base"]
            xtile = h["xtile"]
            xrdy = h["xrdy"]
            evms = [None] * 16
            am_evs = [None] * 16
            amg = [um_guard[0]] if um_guard[0] else []
            NG = 16 if asub >= 5 else 0
            UNM = asub >= 6

            def unmarshal(G):
                for pb_ in (0, 64):
                    sl, g = ptrs.use()
                    pt = (ptr0, ptr1, ptr2)[sl]
                    tout = pt[pb_:pb_ + 64, 0:P]
                    src_t = tb[:, A_AM + G * HD: A_AM + (G + 1) * HD]

                    def fn(t, a=tout, b=src_t):
                        return t.transpose(a, b, ident[:, :])
                    S.emit("tensor", fn,
                           ([am_evs[G]] if pb_ == 0 else []) + list(g), "pe")
                    ped = S.now("pe")[0]
                    par = pb_ // 64
                    src_e = bass.AP(pt, pb_ * 1024 + par,
                                    [[1024, 64], [16, 8], [2, 8]])
                    dst_e = bass.AP(tb, pb_ * TBP + A_AT + 8 * G,
                                    [[TBP, 64], [1, 8], [P, 8]])
                    if (G + par) % 2 == 0:
                        ev = vop(lambda v, a=dst_e, b=src_e: v.tensor_copy(a, b),
                                 waits=[ped])
                    else:
                        ev = sop(lambda s_, a=dst_e, b=src_e: s_.copy(a, b), waits=[ped])
                    ptrs.consumed(sl, ev)

            for G in range(NG + 1):
                if G < NG:
                    sl1, g1 = ps1s.use()
                    p1 = (ps1a, ps1b)[sl1]
                    mm(p1[:, 0:P], tb[0:64, A_KT + G * P: A_KT + (G + 1) * P],
                       tb[0:64, A_QT + G * P: A_QT + (G + 1) * P], True, True,
                       waits=(QKT_OK if G == 0 else []) + list(g1), inc="pe")
                    s1d = S.now("pe")[0]
                    ssl = G % 2
                    evms[G] = vop(lambda v, a=p1[:, 0:P], ssl_=ssl: v.tensor_tensor(
                        tb[:, A_SS + ssl_ * P: A_SS + (ssl_ + 1) * P], a, maskS[:, :],
                        ALU.mult), waits=[s1d, CONST, CONSTV])
                    ps1s.consumed(sl1, evms[G])
                if G >= 1:
                    Gp = G - 1
                    sl2, g2 = pbig.use()
                    pb2 = PB[sl2]
                    mm(pb2[:, 0:65],
                       tb[:, A_SS + (Gp % 2) * P: A_SS + (Gp % 2 + 1) * P],
                       tb[:, A_VG + Gp * 65: A_VG + Gp * 65 + 65],
                       True, True,
                       waits=[evms[Gp], ("gm", gm_base + 16 * (Gp + 1))] + list(g2),
                       inc="pe")
                    s2d = S.now("pe")[0]
                    r = Gp % 4
                    rg = rec_guard[r]
                    rv = vop(lambda v, a=pb2[:, 64:65], r_=r: v.reciprocal(
                        st[:, 32 + 2 * r_:33 + 2 * r_], a),
                        waits=[s2d] + ([rg] if rg else []))
                    ev2 = sop(lambda s, a=pb2[:, 0:64], G_=Gp, r_=r: s.activation(
                        tb[:, A_AM + G_ * HD: A_AM + (G_ + 1) * HD], a, AF.Identity,
                        scale=st[:, 32 + 2 * r_:33 + 2 * r_]),
                        waits=[rv] + (amg if Gp == 0 else []))
                    am_evs[Gp] = ev2
                    rec_guard[r] = ev2
                    pbig.consumed(sl2, ev2)
                    if UNM and Gp >= 2:
                        unmarshal(Gp - 2)
            if NG:
                vg_guard[0] = S.now("pe")[0]
            AM_OK = S.now("ss")[0]
            if UNM:
                for G in (13, 14, 15):
                    unmarshal(G)
            if UNM:
                um_guard[0] = S.now("pe")[0]
            AT_OK = S.now("vv", "ss")

            def o_evac(og, pb, wdt, done, col0, i_=i, xp=xtile, xr=xrdy):
                vop(lambda v, a=pb, og_=og: v.tensor_tensor(
                    s5[:, og_ * 512: og_ * 512 + 512], a[:, 0:512],
                    war[:, A_BR + 3072 + og_ * 512: A_BR + 3072 + og_ * 512 + 512],
                    ALU.add), waits=[done, WA_RAW])
                return vop(lambda v, og_=og, j_=i_, x_=xp: v.tensor_tensor(
                    racc[:, j_ * D + og_ * 512: j_ * D + og_ * 512 + 512],
                    s5[:, og_ * 512: og_ * 512 + 512],
                    x_[:, og_ * 512: og_ * 512 + 512], ALU.add),
                    waits=[xr])

            if asub >= 7:
                project(8, lambda c: tb[:, A_AT + c * P: A_AT + (c + 1) * P],
                        lambda c, col0, wdt: war[:, A_WO + c * D + col0:
                                                 A_WO + c * D + col0 + wdt],
                        [512] * 2, A_BR + 3072, AT_OK + [WA_RAW], o_evac)
            else:
                vop(lambda v, i_=i, xp=xtile: v.tensor_copy(
                    racc[:, ts(i_, D)], xp), waits=[xrdy])
            x_slots.consumed(h["xsl"], S.now("vv")[0])
            x_slots.consumed(h["xsl"], S.now("ss")[0])
            if i == 0:
                if "attn" in dmp:
                    dma("sync", dmp["attn"][0:P, :], tb[:, A_AM:A_AM + D],
                        waits=[AM_OK], inc="ho")
                if "racc_a" in dmp:
                    dma("sync", dmp["racc_a"][0:P, :], racc[:, 0:D],
                        waits=S.now("vv"), inc="ho")

        for i in range(NT if stop >= 2 else 0):
            if i + 1 < NT:
                xload.append(load_x(i + 1, x_slots))
            if i == 0:
                stA[0] = emit_stageA(0)
            emit_marshalA(i)
            if i + 1 < NT:
                stA[i + 1] = emit_stageA(i + 1)
            emit_coreA(i)
            del stA[i]
        RACC_A = S.now("vv")[0]

        for H in range(2 if stop >= 3 else 0):
            phase_gate = S.now("pe", "vv", "ss")
            dma("gpsimd", biasf[0:1, 0:IH_M],
                bass.AP(bfc, H * IH_M, [[1, 1], [1, IH_M]]),
                waits=phase_gate, inc="gw")
            if H == 0:
                dma("gpsimd", war[:, M_BR + 2048: M_BR + 3072],
                    bass.AP(bproj, 0, [[D, 1], [0, P], [1, D]]), inc="gw")
            BV_RAW = S.now("gw", "gg")
            for c in range(8):
                dma("gpsimd", war[:, M_WFC + c * IH_M: M_WFC + (c + 1) * IH_M],
                    bass.AP(wfc, c * P * DFC + H * IH_M, [[DFC, P], [1, IH_M]]),
                    inc="gw")
            WFC_RAW = S.now("gw")[0]
            for og in range(4):
                for c in range(8):
                    mm(pbias[0:1, 0:512], lnb_bf[:, 8 + c:8 + c + 1],
                       war[:, M_WFC + c * IH_M + og * 512: M_WFC + c * IH_M + og * 512 + 512],
                       c == 0, c == 7,
                       waits=([LNBF, WFC_RAW] + S.now("vv", "ss")) if c == 0 else [],
                       inc="pe" if c == 7 else None)
                done = S.now("pe")[0]
                bcw = vop(lambda v: v.tensor_copy(brf[0:1, 0:512], pbias[0:1, 0:512]),
                          waits=[done] + list(BV_RAW))
                vop(lambda v, og_=og: v.tensor_tensor(
                    brf2[0:1, 0:512], brf[0:1, 0:512],
                    biasf[0:1, og_ * 512: og_ * 512 + 512], ALU.add),
                    waits=S.now("gw") + [bcw])
                dma("gpsimd", war[:, M_BR + og * 512: M_BR + og * 512 + 512],
                    bass.AP(brf2, 0, [[512, 1], [0, P], [1, 512]]),
                    waits=S.now("vv"), inc="gw")
            BROW_M = S.now("vv", "gw")
            for c in range(8):
                vop(lambda v, c_=c: v.tensor_scalar(
                    war[:, M_WFC + c_ * IH_M: M_WFC + (c_ + 1) * IH_M],
                    war[:, M_WFC + c_ * IH_M: M_WFC + (c_ + 1) * IH_M],
                    lnv[:, 16 + c_:16 + c_ + 1], None, ALU.mult),
                    waits=BROW_M + [LNV])
            WFC_OK = S.now("vv")[0]
            for c in range(NCH_M):
                dma("gpsimd", war[:, M_WPR + c * D: M_WPR + (c + 1) * D],
                    bass.AP(wproj, (H * IH_M + c * P) * D, [[D, P], [1, D]]), inc="gw")
            WPR_RAW = S.now("gw")[0]

            fc_sg = {0: None, 1: None}
            for i in range(NT):
                if H == 0:
                    XN = tb[:, M_XN:M_XN + D]
                    xn_ok = ln_apply(racc[:, ts(i, D)], XN, D, [RACC_A], [RACC_A],
                                     out_guards=S.now("pe"), jkc=M_JK)
                    for c in range(8):
                        do_transpose(XN[:, ts(c, P)], P,
                                     x2T[:, c * n_tok + i * P: c * n_tok + (i + 1) * P],
                                     [xn_ok] if c == 0 else [], "v" if c % 2 == 0 else "s")
                XN2_OK = S.now("vv", "ss")

                def fc_evac(og, pb, wdt, done, col0):
                    td = vop(lambda v, a=pb, og_=og: v.tensor_tensor(
                        s5[:, og_ % 2 * 512: og_ % 2 * 512 + 512], a[:, 0:512],
                        war[:, M_BR + og_ * 512: M_BR + og_ * 512 + 512], ALU.add),
                        waits=[done] + ([fc_sg[og % 2]] if fc_sg[og % 2] else []))
                    ev = sop(lambda s, og_=og: s.activation(
                        tb[:, M_MID + og_ * 512: M_MID + og_ * 512 + 512],
                        s5[:, og_ % 2 * 512: og_ % 2 * 512 + 512], AF.Gelu),
                        waits=[td])
                    fc_sg[og % 2] = ev
                    return ev

                project(8, lambda c: x2T[:, c * n_tok + i * P: c * n_tok + (i + 1) * P],
                        lambda c, col0, wdt: war[:, M_WFC + c * IH_M + col0:
                                                 M_WFC + c * IH_M + col0 + wdt],
                        [512] * 4, None, XN2_OK + [WFC_OK] + BROW_M, fc_evac)
                MID_OK = S.now("ss")[0]
                for c in range(NCH_M):
                    do_transpose(tb[:, M_MID + c * P: M_MID + (c + 1) * P], P,
                                 tb[:, M_MIDT + c * P: M_MIDT + (c + 1) * P],
                                 [MID_OK] if c == 0 else [], "v" if c % 2 == 0 else "s")
                MIDT_OK = S.now("vv", "ss")

                def pr_evac(og, pb, wdt, done, col0, i_=i, H_=H):
                    if H_ == 0:
                        vop(lambda v, a=pb, og_=og: v.tensor_tensor(
                            s5[:, og_ * 512: og_ * 512 + 512], a[:, 0:512],
                            war[:, M_BR + 2048 + og_ * 512: M_BR + 2048 + og_ * 512 + 512],
                            ALU.add), waits=[done] + list(BV_RAW))
                        return vop(lambda v, og_=og, j_=i_: v.tensor_tensor(
                            racc[:, j_ * D + og_ * 512: j_ * D + og_ * 512 + 512],
                            s5[:, og_ * 512: og_ * 512 + 512],
                            racc[:, j_ * D + og_ * 512: j_ * D + og_ * 512 + 512],
                            ALU.add))
                    return vop(lambda v, a=pb, og_=og, j_=i_: v.tensor_tensor(
                        racc[:, j_ * D + og_ * 512: j_ * D + og_ * 512 + 512], a[:, 0:512],
                        racc[:, j_ * D + og_ * 512: j_ * D + og_ * 512 + 512], ALU.add),
                        waits=[done])

                project(NCH_M, lambda c: tb[:, M_MIDT + c * P: M_MIDT + (c + 1) * P],
                        lambda c, col0, wdt: war[:, M_WPR + c * D + col0:
                                                 M_WPR + c * D + col0 + wdt],
                        [512] * 2, None,
                        MIDT_OK + [WPR_RAW] + list(BV_RAW), pr_evac)
                if i == 0 and H == 0 and "mid" in dmp:
                    dma("sync", dmp["mid"][0:P, :], tb[:, M_MID:M_MID + IH_M],
                        waits=[MID_OK], inc="ho")
        RACC_M = S.now("vv")[0]

        xqT = x2T
        x_slots2 = Slots(2)
        for _sl in range(2):
            for g_ in S.now("vv", "ss"):
                x_slots2.guard[_sl].append(g_)
        xload2 = [None] * NT
        if "racc15" in dmp and stop >= 4:
            dma("sync", dmp["racc15"][0:P, :], racc[:, (NT - 1) * D: NT * D],
                waits=[RACC_M], inc="ho")
        if stop >= 4:
            sl0, g0 = x_slots2.use()
            dma("sync", xt[:, ts(sl0, D)], x_hbm[0:P, :], waits=list(g0), inc="hx")
            xload2[0] = (sl0, S.now("hx")[0])
        for i in range(NT if stop >= 4 else 0):
            xsl, xrdy = xload2[i]
            if i + 1 < NT:
                sl_, g_ = x_slots2.use()
                dma("sync", xt[:, ts(sl_, D)], x_hbm[ts(i + 1, P), :], waits=list(g_), inc="hx")
                xload2[i + 1] = (sl_, S.now("hx")[0])
            xtile = xt[:, ts(xsl, D)]
            rtile = racc[:, ts(i, D)]
            sad = vop(lambda v, a=xtile, b=rtile: v.tensor_tensor(s5[:, :], a, b, ALU.add),
                      waits=[xrdy, RACC_M] + S.now("ss"))
            x_slots2.consumed(xsl, sad)
            x4_ok = ln_apply(s5[:, :], rtile, D, [sad], [sad], out_guards=[], jkc=F_JK)
            qd = quant_emit(rtile, D, tb[:, F_XQD:F_XQD + D], [x4_ok] + S.now("pe"))
            for c in range(8):
                do_transpose(tb[:, F_XQD + c * P: F_XQD + (c + 1) * P], P,
                             xqT[:, c * n_tok + i * P: c * n_tok + (i + 1) * P],
                             [qd] if c == 0 else [], "v" if c % 2 == 0 else "s")
            if i == 0:
                if "x4" in dmp:
                    dma("sync", dmp["x4"][0:P, :], rtile, waits=[x4_ok], inc="ho")
                if "xq" in dmp:
                    dma("sync", dmp["xq"][0:P, :], tb[:, F_XQD:F_XQD + D],
                        waits=[qd], inc="ho")
            if i == NT - 1:
                if "s5f" in dmp:
                    dma("sync", dmp["s5f"][0:P, :], s5[:, :], waits=[x4_ok], inc="ho")
                if "x4f" in dmp:
                    dma("sync", dmp["x4f"][0:P, :], rtile, waits=[x4_ok], inc="ho")
        XQT_OK = S.now("vv", "ss")

        stg = Slots(2)
        stgq = [("sync", "hx"), ("gpsimd", "gw")]
        nst = [0]
        for H in range(2 if stop >= 5 else 0):
            phase_gate = S.now("pe", "vv", "ss")
            nfact = [0]
            ms = {"g": [], "u": [], "d": []}
            for wkey, wdram, base in (("g", wg_n, F_WG), ("u", wu_n, F_WU)):
                for j in range(NCH_F):
                    slg, gg_ = stg.use()
                    qe, qsem = stgq[nst[0] % 2]
                    nst[0] += 1
                    dma(qe, xt[:, ts(slg, D)],
                        bass.AP(wdram, (H * IH_F + j * P) * D, [[D, P], [1, D]]),
                        waits=list(gg_) + (phase_gate if nfact[0] < 2 else []),
                        inc=qsem)
                    nfact[0] += 1
                    srdy = S.now(qsem)[0]
                    qd = quant_emit(xt[:, ts(slg, D)], D, tb[:, F_WQN:F_WQN + D],
                                    [srdy], xn32=s5[:, 0:D], v32=elu_f[:, 0:D])
                    stg.consumed(slg, qd)
                    for c in range(8):
                        do_transpose(tb[:, F_WQN + c * P: F_WQN + (c + 1) * P], P,
                                     war[:, base + c * IH_F + j * P: base + c * IH_F + (j + 1) * P],
                                     [qd] if c == 0 else [], "v" if c % 2 == 0 else "s")
                    ms[wkey].append(S.now("vv", "ss"))
            for j in range(8):
                sl0, g0 = stg.use()
                sl1, g1 = stg.use()
                qe, qsem = stgq[nst[0] % 2]
                nst[0] += 1
                dma(qe, xt[:, 0:IH_F],
                    bass.AP(wd_n, j * P * DFF + H * IH_F, [[DFF, P], [1, IH_F]]),
                    waits=list(g0) + list(g1) + (phase_gate if nfact[0] < 2 else []),
                    inc=qsem)
                nfact[0] += 1
                srdy = S.now(qsem)[0]
                HW_ = IH_F // 2
                quant_emit(xt[:, 0:HW_], HW_, tb[:, F_WQN:F_WQN + HW_], [srdy],
                           xn32=s5[:, 0:HW_], v32=elu_f[:, 0:HW_])
                qd = quant_emit(xt[:, HW_:IH_F], HW_, tb[:, F_WQN + HW_:F_WQN + IH_F],
                                [srdy], xn32=s5[:, 0:HW_], v32=elu_f[:, 0:HW_])
                stg.consumed(sl0, qd)
                stg.consumed(sl1, qd)
                for c in range(NCH_F):
                    do_transpose(tb[:, F_WQN + c * P: F_WQN + (c + 1) * P], P,
                                 war[:, F_WD + c * D + j * P: F_WD + c * D + (j + 1) * P],
                                 [qd] if c == 0 else [], "v" if c % 2 == 0 else "s")
                ms["d"].append(S.now("vv", "ss"))
            WQ_OK = S.now("vv", "ss")

            def wq_gate(wkey, col0, wdt):
                j_hi = min((col0 + wdt - 1) // P, len(ms[wkey]) - 1)
                return ms[wkey][j_hi]
            if H == 0 and "wgq" in dmp:
                dma("sync", dmp["wgq"][0:P, :], war[:, F_WG:F_WG + IH_F], waits=WQ_OK, inc="ho")
            if H == 0 and "wuq" in dmp:
                dma("sync", dmp["wuq"][0:P, :], war[:, F_WU:F_WU + 8 * IH_F], waits=WQ_OK, inc="ho")

            OGW = [512, 512, 384]
            HTB = (F_HT, F_HQ)

            def stageF(i):
                par = i % 2
                hb = HTB[par]

                def gate_evac(og, pb, wdt, done, col0):
                    w8 = ([tile_qv[0]] if tile_qv[0] else []) + \
                         ([ht_guard[par]] if ht_guard[par] else [])
                    return sop(lambda s, a=pb, c0=col0, w=wdt: s.activation(
                        tb[:, hb + c0: hb + c0 + w], a[:, 0:w], AF.Silu),
                        waits=[done] + w8)

                project(8, lambda c: xqT[:, c * n_tok + i * P: c * n_tok + (i + 1) * P],
                        lambda c, col0, wdt: war[:, F_WG + c * IH_F + col0:
                                                 F_WG + c * IH_F + col0 + wdt],
                        OGW, None, XQT_OK, gate_evac,
                        og_waits=lambda og, c0, w: wq_gate("g", c0, w))
                SG_OK = S.now("ss")[0]
                if H == 0 and i == 0 and "sg" in dmp:
                    dma("sync", dmp["sg"][0:P, :], tb[:, hb:hb + IH_F],
                        waits=[SG_OK], inc="ho")
                    qstate["sd"] = S.now("ho")[0]

                def up_evac(og, pb, wdt, done, col0):
                    return vop(lambda v, a=pb, c0=col0, w=wdt: v.tensor_tensor(
                        tb[:, hb + c0: hb + c0 + w], a[:, 0:w],
                        tb[:, hb + c0: hb + c0 + w], ALU.mult),
                        waits=[done, SG_OK])

                project(8, lambda c: xqT[:, c * n_tok + i * P: c * n_tok + (i + 1) * P],
                        lambda c, col0, wdt: war[:, F_WU + c * IH_F + col0:
                                                 F_WU + c * IH_F + col0 + wdt],
                        OGW, None, [], up_evac,
                        og_waits=lambda og, c0, w: wq_gate("u", c0, w))
                return dict(H_OK=S.now("vv")[0], hb=hb, par=par)

            def coreF(i, h):
                hb = h["hb"]
                qd = quant_emit(tb[:, hb:hb + IH_F], IH_F, tb[:, hb:hb + IH_F],
                                [h["H_OK"]])
                tile_qv[0] = qd
                for c in range(NCH_F):
                    do_transpose(tb[:, hb + c * P: hb + (c + 1) * P], P,
                                 tb[:, F_HQT + c * P: F_HQT + (c + 1) * P],
                                 [qd] if c == 0 else [], "v" if c % 2 == 0 else "s")
                ht_guard[h["par"]] = S.now("pe")[0]
                HQT_OK = S.now("vv", "ss")

                def dn_evac(og, pb, wdt, done, col0, i_=i):
                    return vop(lambda v, a=pb, og_=og, j_=i_: v.tensor_tensor(
                        racc[:, j_ * D + og_ * 512: j_ * D + og_ * 512 + 512], a[:, 0:512],
                        racc[:, j_ * D + og_ * 512: j_ * D + og_ * 512 + 512], ALU.add),
                        waits=[done])

                project(NCH_F, lambda c: tb[:, F_HQT + c * P: F_HQT + (c + 1) * P],
                        lambda c, col0, wdt: war[:, F_WD + c * D + col0:
                                                 F_WD + c * D + col0 + wdt],
                        [512] * 2, None, HQT_OK, dn_evac,
                        og_waits=lambda og, c0, w: wq_gate("d", c0, w))
                if H == 1 and stop >= 6:
                    DN_OK = S.now("vv")[0]
                    og_ = [ob_guard[0]] if ob_guard[0] else []
                    ev = ln_apply(racc[:, ts(i, D)], outb[:, :], D,
                                  [], [DN_OK] + (list(WQ_OK) if i == 0 else []),
                                  out_guards=og_, jkc=F_JK)
                    dma("sync", out_hbm[ts(i, P), :], outb[:, :], waits=[ev], inc="ho")
                    ob_guard[0] = S.now("ho")[0]

            hF = stageF(0)
            for i in range(NT):
                hN = stageF(i + 1) if i + 1 < NT else None
                coreF(i, hF)
                hF = hN
        X5_OK = S.now("vv")[0]

        if stop < 6:
            for i in range(NT):
                og_ = [ob_guard[0]] if ob_guard[0] else []
                gate = S.now("vv", "ss", "pe")
                ev = sop(lambda s_, b=racc[:, ts(i, D)]: s_.copy(outb[:, :], b),
                         waits=(gate if i == 0 else []) + og_)
                dma("sync", out_hbm[ts(i, P), :], outb[:, :], waits=[ev], inc="ho")
                ob_guard[0] = S.now("ho")[0]
        FINAL_HO = S.at("ho")

        def realize(eng_name):
            def body(eng):
                for fn, waits, inc, amt in S.ops[eng_name]:
                    for (sn, v) in waits:
                        eng.wait_ge(sems[sn], v)
                    r = fn(eng)
                    if inc is not None:
                        r.then_inc(sems[inc], amt)
                if eng_name == "sync":
                    eng.wait_ge(sems["ho"], FINAL_HO)
            return body

        blk.sync(realize("sync"))
        blk.gpsimd(realize("gpsimd"))
        blk.tensor(realize("tensor"))
        blk.vector(realize("vector"))
        blk.scalar(realize("scalar"))

    return nc


def _prep_inputs(L0_attn_x, L0_attn_ln1_w, L0_attn_ln1_b, L0_attn_q_w, L0_attn_q_b,
                 L0_attn_k_w, L0_attn_k_b, L0_attn_v_w, L0_attn_v_b, L0_attn_o_w,
                 L0_attn_o_b, L0_attn_ln2_w, L0_attn_ln2_b, L0_attn_fc_w, L0_attn_fc_b,
                 L0_attn_proj_w, L0_attn_proj_b, L0_ffn_w_gate, L0_ffn_w_up, L0_ffn_w_down):
    f = np.float32
    c = np.ascontiguousarray

    def t(a):
        return c(np.asarray(a).T.astype(f, copy=False))

    base = {
        "wqkv_t": c(np.concatenate([np.asarray(L0_attn_q_w), np.asarray(L0_attn_k_w),
                                    np.asarray(L0_attn_v_w)], axis=0).T.astype(f, copy=False)),
        "wo_t": t(L0_attn_o_w),
        "wfc_t": t(L0_attn_fc_w),
        "wproj_t": t(L0_attn_proj_w),
        "wg_n": c(np.asarray(L0_ffn_w_gate).astype(f, copy=False)),
        "wu_n": c(np.asarray(L0_ffn_w_up).astype(f, copy=False)),
        "wd_n": c(np.asarray(L0_ffn_w_down).astype(f, copy=False)),
        "ln1w": c(np.asarray(L0_attn_ln1_w).reshape(8, P).T.astype(f, copy=False)),
        "ln1b": c(np.asarray(L0_attn_ln1_b).reshape(8, P).T.astype(f, copy=False)),
        "ln2w": c(np.asarray(L0_attn_ln2_w).reshape(8, P).T.astype(f, copy=False)),
        "ln2b": c(np.asarray(L0_attn_ln2_b).reshape(8, P).T.astype(f, copy=False)),
        "bqkv": c(np.concatenate([np.asarray(L0_attn_q_b), np.asarray(L0_attn_k_b),
                                  np.asarray(L0_attn_v_b)]).reshape(1, -1).astype(f, copy=False)),
        "bo": c(np.asarray(L0_attn_o_b).reshape(1, -1).astype(f, copy=False)),
        "bfc": c(np.asarray(L0_attn_fc_b).reshape(1, -1).astype(f, copy=False)),
        "bproj": c(np.asarray(L0_attn_proj_b).reshape(1, -1).astype(f, copy=False)),
    }
    xfull = np.ascontiguousarray(np.asarray(L0_attn_x).astype(f, copy=False).reshape(-1, D))
    return xfull, base


def kernel(**inputs):
    xfull, base = _prep_inputs(**inputs)
    n_tok = xfull.shape[0] // NCORES
    shards = np.split(xfull, NCORES, axis=0)
    nc = build(n_tok=n_tok)
    in_maps = [dict(base, x=np.ascontiguousarray(s)) for s in shards]
    res = run_bass_kernel_spmd(nc, in_maps, core_ids=list(range(NCORES)))
    out = np.concatenate([np.asarray(r["out"]) for r in res.results], axis=0)
    return out.reshape(np.asarray(inputs["L0_attn_x"]).shape)


# version 24
# speedup vs baseline: 1.0257x; 1.0030x over previous
import numpy as np
import concourse.bass as bass
import concourse.mybir as mybir
from concourse.bass_utils import run_bass_kernel_spmd

F32 = mybir.dt.float32
BF16 = mybir.dt.bfloat16
AF = mybir.ActivationFunctionType
ALU = mybir.AluOpType
AX = mybir.AxisListType

P = 128
D = 1024
NH = 16
HD = 64
DFC = 4096
DFF = 2816
EPS = 1e-5
NCORES = 8

ENGS = ("sync", "gpsimd", "tensor", "vector", "scalar")


class Sched:
    def __init__(self):
        self.ops = {e: [] for e in ENGS}
        self.counts = {}

    def at(self, sem):
        return self.counts.get(sem, 0)

    def now(self, *sems):
        return [(s, self.at(s)) for s in sems]

    def emit(self, eng, fn, waits=(), inc=None, amt=1):
        ws = [(s, v) for (s, v) in waits if v > 0]
        self.ops[eng].append((fn, ws, inc, amt))
        if inc is not None:
            self.counts[inc] = self.counts.get(inc, 0) + amt
        return (inc, self.counts.get(inc, 0)) if inc else None


class Slots:
    def __init__(self, n):
        self.n = n
        self.guard = [[] for _ in range(n)]
        self.i = 0

    def use(self):
        s = self.i % self.n
        self.i += 1
        w = self.guard[s]
        self.guard[s] = []
        return s, w

    def consumed(self, s, sem_val):
        if sem_val is not None:
            self.guard[s].append(sem_val)


def ts(i, n):
    return slice(i * n, (i + 1) * n)


def build(n_tok=2048, dump=(), stop=6, asub=7):
    assert n_tok % P == 0
    NT = n_tok // P
    IH_M = DFC // 2
    NCH_M = IH_M // P
    IH_F = DFF // 2
    NCH_F = IH_F // P

    nc = bass.Bass("TRN2", target_bir_lowering=False, detect_race_conditions=False)

    x_hbm = nc.dram_tensor("x", [n_tok, D], F32, kind="ExternalInput")
    wqkv = nc.dram_tensor("wqkv_t", [D, 3 * D], F32, kind="ExternalInput")
    wo = nc.dram_tensor("wo_t", [D, D], F32, kind="ExternalInput")
    wfc = nc.dram_tensor("wfc_t", [D, DFC], F32, kind="ExternalInput")
    wproj = nc.dram_tensor("wproj_t", [DFC, D], F32, kind="ExternalInput")
    wg_n = nc.dram_tensor("wg_n", [DFF, D], F32, kind="ExternalInput")
    wu_n = nc.dram_tensor("wu_n", [DFF, D], F32, kind="ExternalInput")
    wd_n = nc.dram_tensor("wd_n", [D, DFF], F32, kind="ExternalInput")
    ln1w = nc.dram_tensor("ln1w", [P, 8], F32, kind="ExternalInput")
    ln1b = nc.dram_tensor("ln1b", [P, 8], F32, kind="ExternalInput")
    ln2w = nc.dram_tensor("ln2w", [P, 8], F32, kind="ExternalInput")
    ln2b = nc.dram_tensor("ln2b", [P, 8], F32, kind="ExternalInput")
    bqkv = nc.dram_tensor("bqkv", [1, 3 * D], F32, kind="ExternalInput")
    bo = nc.dram_tensor("bo", [1, D], F32, kind="ExternalInput")
    bfc = nc.dram_tensor("bfc", [1, DFC], F32, kind="ExternalInput")
    bproj = nc.dram_tensor("bproj", [1, D], F32, kind="ExternalInput")
    out_hbm = nc.dram_tensor("out", [n_tok, D], F32, kind="ExternalOutput")
    dmp = {}
    for name, shape, dt in dump:
        dmp[name] = nc.dram_tensor("dump_" + name, shape, dt, kind="ExternalOutput")

    from contextlib import ExitStack
    with ExitStack() as ctx:
        e = ctx.enter_context

        racc = e(nc.sbuf_tensor("racc", [P, NT * D], BF16))
        x2T = e(nc.sbuf_tensor("x2T", [P, 8 * n_tok], BF16))
        WARC = 36992
        war = e(nc.sbuf_tensor("war", [P, WARC], BF16))
        TBC = 17664
        tb = e(nc.sbuf_tensor("tb", [P, TBC], BF16))
        xt = e(nc.sbuf_tensor("xt", [P, 2 * D], F32))
        s5 = e(nc.sbuf_tensor("s5", [P, D], F32))
        elu_f = e(nc.sbuf_tensor("elu_f", [P, 1024], F32))
        outb = elu_f
        stage = None
        st = e(nc.sbuf_tensor("st", [P, 48], F32))
        qs = e(nc.sbuf_tensor("qs", [P, 3 * 96], F32))
        ident = e(nc.sbuf_tensor("ident", [P, P], BF16))
        maskS = e(nc.sbuf_tensor("maskS", [P, P], BF16))
        lnv = e(nc.sbuf_tensor("lnv", [P, 32], F32))
        lnb_bf = e(nc.sbuf_tensor("lnb_bf", [P, 16], BF16))
        brf = e(nc.sbuf_tensor("brf", [1, 512], F32))
        brf2 = e(nc.sbuf_tensor("brf2", [1, 512], F32))
        biasf = e(nc.sbuf_tensor("biasf", [1, 3 * D], BF16))

        pb0 = e(nc.psum_tensor("pb0", [P, 512], F32))
        pb1 = e(nc.psum_tensor("pb1", [P, 512], F32))
        ptr0 = e(nc.psum_tensor("ptr0", [P, 1024], BF16))
        ptr1 = e(nc.psum_tensor("ptr1", [P, 1024], BF16))
        ptr2 = e(nc.psum_tensor("ptr2", [P, 1024], BF16))
        ps1a = e(nc.psum_tensor("ps1a", [P, 512], F32))
        ps1b = e(nc.psum_tensor("ps1b", [P, 512], F32))
        ps2 = e(nc.psum_tensor("ps2", [P, 512], F32))
        pbias = ps2

        sems = {}
        for sname in ("hx", "ho", "gw", "gm", "pe", "vv", "ss", "gg"):
            sems[sname] = e(nc.semaphore(sname))
        blk = e(nc.Block())

        S = Sched()
        PB = (pb0, pb1)
        pbig = Slots(2)
        ptrs = Slots(3)
        PTR = None
        ps1s = Slots(2)
        TBP = TBC

        A_XN, A_XNT, A_QKV, A_QT, A_KT, A_VG, A_SS, A_AM, A_AT, A_ER, A_EE, A_JK = (
            0, 1024, 2048, 5120, 7168, 9216, 10256, 10512, 11536, 13584, 14608, 15632)
        M_XN, M_MID, M_MIDT, M_JK = 0, 1024, 3072, 15632
        F_XQD, F_QXN, F_QV, F_B1, F_B2, F_B3, F_B4, F_B5, F_WQN, F_HT, F_HQ, F_HQT, F_JK = (
            0, 1024, 2432, 3840, 5248, 6656, 8064, 9472, 10880, 12288, 13696, 15104, 16512)
        A_WQKV, A_WO, A_BR = 0, 24576, 32768
        M_WFC, M_WPR, M_BR = 0, 16384, 32768
        F_WG, F_WU, F_WD = 0, 11264, 22528

        def mm(psum_ap, lhsT_ap, rhs_ap, start, stop, waits=(), inc=None, tp=None):
            def fn(t, a=psum_ap, b=lhsT_ap, c=rhs_ap, st_=start, sp=stop, tp_=tp):
                return t.matmul(a, b, c, start=st_, stop=sp, tile_position=tp_)
            return S.emit("tensor", fn, waits, inc)

        def vop(fn, waits=()):
            return S.emit("vector", fn, waits, "vv")

        def sop(fn, waits=()):
            return S.emit("scalar", fn, waits, "ss")

        def gop(fn, waits=()):
            return S.emit("gpsimd", fn, waits, "gg")

        def dma(eng, out_ap, in_ap, waits=(), inc="hx"):
            def fn(g, a=out_ap, b=in_ap):
                return g.dma_start(out=a, in_=b)
            return S.emit(eng, fn, waits, inc, amt=16)

        gop(lambda g: g.memset(ident[:, :], 0.0))
        gop(lambda g: g.affine_select(out=ident[:, :], in_=ident[:, :],
                                      compare_op=ALU.not_equal, fill=1.0, base=0,
                                      pattern=[[-1, P]], channel_multiplier=1),
            waits=S.now("gg"))
        Bm = tb[0:8, 0:P]
        gop(lambda g: g.memset(Bm, 1.0), waits=S.now("gg"))
        gop(lambda g: g.affine_select(out=Bm, in_=Bm, compare_op=ALU.is_ge,
                                      fill=0.0, base=0, pattern=[[1, P]],
                                      channel_multiplier=-16), waits=S.now("gg"))
        gop(lambda g: g.affine_select(out=Bm, in_=Bm, compare_op=ALU.is_ge,
                                      fill=0.0, base=15, pattern=[[-1, P]],
                                      channel_multiplier=16), waits=S.now("gg"))
        def _mask_mm(t):
            return t.matmul(ps1a[:, 0:P], Bm, Bm, start=True, stop=True)
        S.emit("tensor", _mask_mm, S.now("gg"), "pe")
        vop(lambda v: v.tensor_copy(maskS[:, :], ps1a[:, 0:P]), waits=S.now("pe"))
        for G in range(16):
            gop(lambda g, G_=G: g.memset(tb[:, A_VG + G_ * 65 + 64: A_VG + G_ * 65 + 65], 1.0))
        CONST = S.now("gg")[0]
        CONSTV = S.now("vv")[0]

        dma("sync", lnv[:, 0:8], ln1w[:, :])
        dma("sync", lnv[:, 8:16], ln1b[:, :])
        dma("sync", lnv[:, 16:24], ln2w[:, :])
        dma("sync", lnv[:, 24:32], ln2b[:, :])
        LNV = S.now("hx")[0]
        dma("gpsimd", lnb_bf[:, 0:8], ln1b[:, :], inc="gw")
        dma("gpsimd", lnb_bf[:, 8:16], ln2b[:, :], inc="gw")
        LNBF = S.now("gw")[0]

        ln_state = {"i": 0, "act": {0: None, 1: None}}

        def ln_apply(src_ap, out_ap, width, waits_v, waits_s, out_guards=(), jkc=None):
            n = ln_state["i"]
            ln_state["i"] += 1
            b = 16 * (n % 2)
            g = ln_state["act"][n % 2]
            gg = [g] if g else []
            jk = tb[:, jkc:jkc + width]
            c_sum, c_sq, c_mu, c_var, c_t, c_inv, c_nmi = (b, b + 2, b + 4, b + 6,
                                                           b + 8, b + 10, b + 12)
            red = vop(lambda v, a=src_ap: v.reduce_sum(st[:, c_sum:c_sum + 1], a,
                                                       axis=AX.X),
                      waits=list(waits_v) + gg)
            sop(lambda s, a=src_ap: s.activation(jk, a, AF.Square,
                                                 accum_out=st[:, c_sq:c_sq + 1]),
                waits=list(waits_s) + gg)
            sqd = S.now("ss")[0]
            w1 = vop(lambda v: v.tensor_scalar(st[:, c_mu:c_mu + 1],
                                               st[:, c_sum:c_sum + 1],
                                               1.0 / width, None, ALU.mult),
                     waits=[red])
            w2 = vop(lambda v: v.tensor_scalar(st[:, c_var:c_var + 1],
                                               st[:, c_sq:c_sq + 1],
                                               1.0 / width, None, ALU.mult),
                     waits=[sqd])
            w3 = vop(lambda v: v.tensor_mul(st[:, c_t:c_t + 1], st[:, c_mu:c_mu + 1],
                                            st[:, c_mu:c_mu + 1]), waits=[w1])
            w4 = vop(lambda v: v.tensor_tensor(st[:, c_var:c_var + 1],
                                               st[:, c_var:c_var + 1],
                                               st[:, c_t:c_t + 1], ALU.subtract),
                     waits=[w2, w3])
            ved = vop(lambda v: v.tensor_scalar(st[:, c_var:c_var + 1],
                                                st[:, c_var:c_var + 1],
                                                EPS, None, ALU.add), waits=[w4])
            sop(lambda s: s.activation(st[:, c_t:c_t + 1], st[:, c_var:c_var + 1],
                                       AF.Ln), waits=[ved])
            sop(lambda s: s.activation(st[:, c_inv:c_inv + 1], st[:, c_t:c_t + 1],
                                       AF.Exp, scale=np.float32(-0.5)))
            inv = S.now("ss")[0]
            w5 = vop(lambda v: v.tensor_mul(st[:, c_nmi:c_nmi + 1],
                                            st[:, c_mu:c_mu + 1],
                                            st[:, c_inv:c_inv + 1]),
                     waits=[inv, w1])
            nmi = vop(lambda v: v.tensor_scalar(st[:, c_nmi:c_nmi + 1],
                                                st[:, c_nmi:c_nmi + 1],
                                                -1.0, None, ALU.mult), waits=[w5])
            act = sop(lambda s, a=src_ap, o=out_ap: s.activation(
                o, a, AF.Identity, bias=st[:, c_nmi:c_nmi + 1],
                scale=st[:, c_inv:c_inv + 1]),
                waits=[nmi] + list(out_guards))
            ln_state["act"][n % 2] = act
            return act

        def do_transpose(src_ap, width, dst_ap, waits, ee):
            sl, g = ptrs.use()
            pt = (ptr0, ptr1, ptr2)[sl]
            transpose_ap = pt[0:width, 0:P]

            def fn(t, a=transpose_ap, b=src_ap):
                return t.transpose(a, b, ident[:, :])
            S.emit("tensor", fn, list(waits) + list(g), "pe")
            ped = S.now("pe")[0]
            if ee == "v":
                ev = vop(lambda v, a=dst_ap, b=transpose_ap: v.tensor_copy(a, b),
                         waits=[ped])
            else:
                ev = sop(lambda s, a=dst_ap, b=transpose_ap: s.copy(a, b), waits=[ped])
            ptrs.consumed(sl, ev)
            return ev

        def project(nchunks, lhsT_of, rhs_of, out_widths, bias_col, first_waits, evac,
                    og_waits=None):
            col0 = 0
            for og, wdt in enumerate(out_widths):
                sl, guard = pbig.use()
                pb = PB[sl]
                fw = list(first_waits) + list(guard)
                if og_waits is not None:
                    fw += list(og_waits(og, col0, wdt))
                first = True
                for c in range(nchunks):
                    last = c == nchunks - 1
                    mm(pb[:, 0:wdt], lhsT_of(c), rhs_of(c, col0, wdt), first, last,
                       waits=fw if first else [], inc="pe" if last else None)
                    first, fw = False, []
                done = S.now("pe")[0]
                ev = evac(og, pb, wdt, done, col0)
                pbig.consumed(sl, ev)
                col0 += wdt

        def quant_emit(src2d, W, dst2d, waits_v, xn32=None, v32=None):
            nb = W // 16
            src3 = src2d.rearrange("p (b k) -> p b k", k=16)
            QXN = xn32 if xn32 is not None else tb[:, F_QXN:F_QXN + W]
            QV = v32 if v32 is not None else tb[:, F_QV:F_QV + W]
            B1 = tb[:, F_B1:F_B1 + W]
            B2 = tb[:, F_B2:F_B2 + W]
            B3 = tb[:, F_B3:F_B3 + W]
            B4 = tb[:, F_B4:F_B4 + W]
            B5 = tb[:, F_B5:F_B5 + W]
            bmr = vop(lambda v: v.tensor_reduce(qs[:, 0:nb], src3, axis=AX.X,
                                                op=ALU.max, apply_absolute_value=True),
                      waits=waits_v)
            scw = vop(lambda v: v.tensor_scalar(qs[:, 96:96 + nb], qs[:, 0:nb],
                                                1.0 / 6.0, 1e-12, ALU.mult, ALU.max),
                      waits=[bmr])
            rsr = vop(lambda v: v.reciprocal(qs[:, 192:192 + nb], qs[:, 96:96 + nb]),
                      waits=[scw])
            pv = qstate["sd"]
            vop(lambda v: v.tensor_tensor(
                QXN.rearrange("p (b k) -> p b k", k=16), src3,
                qs[:, 192:192 + nb].unsqueeze(2).broadcast_to([P, nb, 16]), ALU.mult),
                waits=([pv] if pv else []) + [rsr])
            xnd = S.now("vv")[0]
            sop(lambda s: s.activation(QV, QXN, AF.Abs), waits=[xnd])
            sop(lambda s: s.activation(B5, QXN, AF.Sign))
            sd = S.now("ss")[0]
            pvg = qstate["s_guard"]
            qstate["sd"] = sd
            NUD = 1.0 + 2.0 ** -40
            if True:
                vop(lambda v: v.tensor_scalar(B1, QV, 0.25 * NUD, None, ALU.is_gt),
                    waits=[sd])
                vop(lambda v: v.tensor_scalar(B2, QV, 0.75 * NUD, None, ALU.is_gt))
                vop(lambda v: v.tensor_add(B1, B1, B2))
                vop(lambda v: v.tensor_scalar(B2, QV, 1.25 * NUD, None, ALU.is_gt))
                vop(lambda v: v.tensor_scalar(B3, QV, 1.75 * NUD, None, ALU.is_gt))
                vop(lambda v: v.tensor_add(B2, B2, B3))
                vop(lambda v: v.tensor_add(B1, B1, B2))
                vop(lambda v: v.tensor_scalar(B2, QV, 2.5 * NUD, None, ALU.is_gt))
                vop(lambda v: v.tensor_scalar(B3, QV, 3.5 * NUD, None, ALU.is_gt))
                vop(lambda v: v.tensor_scalar(B4, QV, 5.0 * NUD, None, ALU.is_gt))
                vop(lambda v: v.scalar_tensor_tensor(B3, B4, 2.0, B3, ALU.mult, ALU.add))
                vop(lambda v: v.tensor_add(B3, B3, B2))
                vop(lambda v: v.scalar_tensor_tensor(B1, B1, 0.5, B3, ALU.mult, ALU.add))
            vop(lambda v: v.tensor_mul(B1, B1, B5))
            qv = vop(lambda v, d=dst2d: v.tensor_tensor(
                d.rearrange("p (b k) -> p b k", k=16),
                B1.rearrange("p (b k) -> p b k", k=16),
                qs[:, 96:96 + nb].unsqueeze(2).broadcast_to([P, nb, 16]), ALU.mult),
                waits=S.now("pe"))
            qstate["s_guard"] = qv
            return qv

        qstate = {"s_guard": None, "sd": None}
        tile_qv = [None]
        ht_guard = [None, None]
        ob_guard = [None]

        dma("gpsimd", biasf[0:1, 0:3072], bqkv[:, :], inc="gw")
        dma("gpsimd", war[:, A_BR + 3072: A_BR + 4096],
            bass.AP(bo, 0, [[D, 1], [0, P], [1, D]]), inc="gw")
        BQ_RDY = S.now("gw")[0]
        for c in range(8):
            dma("gpsimd", war[:, A_WQKV + c * 3072: A_WQKV + (c + 1) * 3072],
                bass.AP(wqkv, c * P * 3 * D, [[3 * D, P], [1, 3 * D]]), inc="gw")
        WQKV_RAW = S.now("gw")[0]
        for og in range(6):
            for c in range(8):
                mm(pbias[0:1, 0:512], lnb_bf[:, c:c + 1],
                   war[:, A_WQKV + c * 3072 + og * 512: A_WQKV + c * 3072 + og * 512 + 512],
                   c == 0, c == 7,
                   waits=([LNBF, WQKV_RAW] + S.now("vv")) if c == 0 else [],
                   inc="pe" if c == 7 else None)
            done = S.now("pe")[0]
            bcw = vop(lambda v: v.tensor_copy(brf[0:1, 0:512], pbias[0:1, 0:512]),
                      waits=[done, BQ_RDY])
            vop(lambda v, og_=og: v.tensor_tensor(
                brf2[0:1, 0:512], brf[0:1, 0:512],
                biasf[0:1, og_ * 512: og_ * 512 + 512], ALU.add),
                waits=S.now("gw") + [bcw])
            dma("gpsimd", war[:, A_BR + og * 512: A_BR + og * 512 + 512],
                bass.AP(brf2, 0, [[512, 1], [0, P], [1, 512]]),
                waits=S.now("vv"), inc="gw")
        BROW_A = S.now("vv", "gw")
        for c in range(8):
            dma("gpsimd", war[:, A_WO + c * D: A_WO + (c + 1) * D],
                bass.AP(wo, c * P * D, [[D, P], [1, D]]), inc="gw")
        WA_RAW = S.now("gw")[0]
        for c in range(8):
            vop(lambda v, c_=c: v.tensor_scalar(
                war[:, A_WQKV + c_ * 3072: A_WQKV + (c_ + 1) * 3072],
                war[:, A_WQKV + c_ * 3072: A_WQKV + (c_ + 1) * 3072],
                lnv[:, c_:c_ + 1], None, ALU.mult),
                waits=BROW_A + [LNV])
        WQKV_OK = S.now("vv")[0]

        x_slots = Slots(2)

        def load_x(i, slots):
            sl, g = slots.use()
            dma("sync", xt[:, ts(sl, D)], x_hbm[ts(i, P), :], waits=g, inc="hx")
            return sl, S.now("hx")[0]

        xload = [load_x(0, x_slots)] if stop >= 2 else []
        elu_guard = {0: None, 1: None}
        rec_guard = [None] * 4
        vg_guard = [None]

        stA = {}
        um_guard = [None]

        def emit_stageA(i):
            xsl, xrdy = xload[i]
            xtile = xt[:, ts(xsl, D)]
            XN = tb[:, A_XN:A_XN + D]
            gm_prev = S.now("gm")[0]
            xn_ok = ln_apply(xtile, XN, D, [xrdy], [xrdy], out_guards=S.now("pe"), jkc=A_JK)
            if i == 0 and "xn" in dmp:
                dma("sync", dmp["xn"][0:P, :], XN, waits=[xn_ok], inc="ho")
            if i == 0 and "st" in dmp:
                dma("sync", dmp["st"][0:P, :], st[:, 0:32], waits=[xn_ok], inc="ho")
            for c in range(8):
                do_transpose(XN[:, ts(c, P)], P, tb[:, A_XNT + c * P: A_XNT + (c + 1) * P],
                             [xn_ok] if c == 0 else [], "v" if c % 2 == 0 else "s")
            XNT_OK = S.now("vv", "ss")

            def qkv_evac(og, pb, wdt, done, col0):
                bias = war[:, A_BR + og * 512: A_BR + og * 512 + 512]
                if og < 4:
                    par = og % 2
                    eg = elu_guard[par]
                    ef = elu_f[:, par * 512:(par + 1) * 512]
                    er = tb[:, A_ER + par * 512: A_ER + (par + 1) * 512]
                    ee = tb[:, A_EE + par * 512: A_EE + (par + 1) * 512]
                    td = vop(lambda v, a=pb, ef_=ef, b_=bias: v.tensor_tensor(
                        ef_, a[:, 0:512], b_, ALU.add),
                        waits=[done] + (list(eg) if eg else []))
                    rd = sop(lambda s, ef_=ef, er_=er: s.activation(er_, ef_, AF.Relu),
                             waits=[td])
                    md = vop(lambda v, ef_=ef: v.tensor_scalar(
                        ef_, ef_, 0.0, None, ALU.min), waits=[rd])
                    ed = sop(lambda s, ef_=ef, ee_=ee: s.activation(ee_, ef_, AF.Exp),
                             waits=[md])
                    ad = vop(lambda v, og_=og, er_=er, ee_=ee: v.tensor_add(
                        tb[:, A_QKV + og_ * 512:A_QKV + og_ * 512 + 512], er_, ee_),
                        waits=[ed, gm_prev])
                    elu_guard[par] = (ed, ad)
                    return ad
                else:
                    return vop(lambda v, a=pb, og_=og, b_=bias: v.tensor_tensor(
                        tb[:, A_QKV + og_ * 512:A_QKV + og_ * 512 + 512], a[:, 0:512],
                        b_, ALU.add),
                        waits=[done, gm_prev])

            qkv_evs = [None] * 6

            if asub >= 2:
                def qkv_evac_plain(og, pb, wdt, done, col0):
                    return sop(lambda s_, a=pb, og_=og: s_.copy(
                        tb[:, A_QKV + og_ * 512:A_QKV + og_ * 512 + 512], a[:, 0:512]),
                        waits=[done, gm_prev])
                _ev0 = qkv_evac if asub not in (20, 21) else qkv_evac_plain

                def _ev(og, pb, wdt, done, col0):
                    r = _ev0(og, pb, wdt, done, col0)
                    qkv_evs[og] = r
                    return r
                project(8, lambda c: tb[:, A_XNT + c * P: A_XNT + (c + 1) * P],
                        lambda c, col0, wdt: war[:, A_WQKV + c * 3072 + col0:
                                                 A_WQKV + c * 3072 + col0 + wdt],
                        [512] * 6, None, XNT_OK + [WQKV_OK] + BROW_A, _ev)
            QKV_OK = S.now("vv", "ss")
            if i == 0 and "qkv" in dmp:
                dma("sync", dmp["qkv"][0:P, :], tb[:, A_QKV:A_QKV + 3 * D],
                    waits=QKV_OK, inc="ho")
            return dict(xsl=xsl, xrdy=xrdy, xtile=xtile, QKV_OK=QKV_OK,
                        qkv_evs=qkv_evs)

        def emit_marshalA(i):
            h = stA[i]
            QKV_OK = h["QKV_OK"]
            evs = h["qkv_evs"]

            def ogw(og):
                return [evs[og]] if evs[og] else QKV_OK
            for hh in range(NH if asub >= 3 else 0):
                do_transpose(tb[:, A_QKV + hh * HD: A_QKV + (hh + 1) * HD], HD,
                             bass.AP(tb, A_QT + hh, [[TBP, 64], [P, 16], [16, 8]]),
                             ogw(hh // 8) if hh % 8 == 0 else [],
                             "v" if hh % 2 else "s")
            for hh in range(NH if asub >= 3 else 0):
                do_transpose(tb[:, A_QKV + D + hh * HD: A_QKV + D + (hh + 1) * HD], HD,
                             bass.AP(tb, A_KT + hh, [[TBP, 64], [P, 16], [16, 8]]),
                             ogw(2 + hh // 8) if hh % 8 == 0 else [],
                             "s" if hh % 2 else "v")
            h["QKT_OK"] = S.now("vv", "ss")

            vm_first = ([evs[4], evs[5]] if evs[5] else QKV_OK) + \
                       ([vg_guard[0]] if vg_guard[0] else []) + [CONST]
            h["gm_base"] = S.at("gm")
            for G in range(16 if asub >= 4 else 0):
                dma("gpsimd",
                    bass.AP(tb, A_VG + G * 65, [[TBP, P], [1, HD]]),
                    bass.AP(tb, (8 * G) * TBP + A_QKV + 2 * D, [[TBP, 8], [HD, NH], [1, HD]]),
                    waits=vm_first if G == 0 else [], inc="gm")

        def emit_coreA(i):
            h = stA[i]
            QKT_OK = h["QKT_OK"]
            gm_base = h["gm_base"]
            xtile = h["xtile"]
            xrdy = h["xrdy"]
            evms = [None] * 16
            am_evs = [None] * 16
            amg = [um_guard[0]] if um_guard[0] else []
            NG = 16 if asub >= 5 else 0
            UNM = asub >= 6

            def unmarshal(G):
                for pb_ in (0, 64):
                    sl, g = ptrs.use()
                    pt = (ptr0, ptr1, ptr2)[sl]
                    tout = pt[pb_:pb_ + 64, 0:P]
                    src_t = tb[:, A_AM + G * HD: A_AM + (G + 1) * HD]

                    def fn(t, a=tout, b=src_t):
                        return t.transpose(a, b, ident[:, :])
                    S.emit("tensor", fn,
                           ([am_evs[G]] if pb_ == 0 else []) + list(g), "pe")
                    ped = S.now("pe")[0]
                    par = pb_ // 64
                    src_e = bass.AP(pt, pb_ * 1024 + par,
                                    [[1024, 64], [16, 8], [2, 8]])
                    dst_e = bass.AP(tb, pb_ * TBP + A_AT + 8 * G,
                                    [[TBP, 64], [1, 8], [P, 8]])
                    if (G + par) % 2 == 0:
                        ev = vop(lambda v, a=dst_e, b=src_e: v.tensor_copy(a, b),
                                 waits=[ped])
                    else:
                        ev = sop(lambda s_, a=dst_e, b=src_e: s_.copy(a, b), waits=[ped])
                    ptrs.consumed(sl, ev)

            for G in range(NG + 1):
                if G < NG:
                    sl1, g1 = ps1s.use()
                    p1 = (ps1a, ps1b)[sl1]
                    mm(p1[:, 0:P], tb[0:64, A_KT + G * P: A_KT + (G + 1) * P],
                       tb[0:64, A_QT + G * P: A_QT + (G + 1) * P], True, True,
                       waits=(QKT_OK if G == 0 else []) + list(g1), inc="pe")
                    s1d = S.now("pe")[0]
                    ssl = G % 2
                    evms[G] = vop(lambda v, a=p1[:, 0:P], ssl_=ssl: v.tensor_tensor(
                        tb[:, A_SS + ssl_ * P: A_SS + (ssl_ + 1) * P], a, maskS[:, :],
                        ALU.mult), waits=[s1d, CONST, CONSTV])
                    ps1s.consumed(sl1, evms[G])
                if G >= 1:
                    Gp = G - 1
                    sl2, g2 = pbig.use()
                    pb2 = PB[sl2]
                    mm(pb2[:, 0:65],
                       tb[:, A_SS + (Gp % 2) * P: A_SS + (Gp % 2 + 1) * P],
                       tb[:, A_VG + Gp * 65: A_VG + Gp * 65 + 65],
                       True, True,
                       waits=[evms[Gp], ("gm", gm_base + 16 * (Gp + 1))] + list(g2),
                       inc="pe")
                    s2d = S.now("pe")[0]
                    r = Gp % 4
                    rg = rec_guard[r]
                    rv = vop(lambda v, a=pb2[:, 64:65], r_=r: v.reciprocal(
                        st[:, 32 + 2 * r_:33 + 2 * r_], a),
                        waits=[s2d] + ([rg] if rg else []))
                    ev2 = sop(lambda s, a=pb2[:, 0:64], G_=Gp, r_=r: s.activation(
                        tb[:, A_AM + G_ * HD: A_AM + (G_ + 1) * HD], a, AF.Identity,
                        scale=st[:, 32 + 2 * r_:33 + 2 * r_]),
                        waits=[rv] + (amg if Gp == 0 else []))
                    am_evs[Gp] = ev2
                    rec_guard[r] = ev2
                    pbig.consumed(sl2, ev2)
                    if UNM and Gp >= 2:
                        unmarshal(Gp - 2)
            if NG:
                vg_guard[0] = S.now("pe")[0]
            AM_OK = S.now("ss")[0]
            if UNM:
                for G in (13, 14, 15):
                    unmarshal(G)
            if UNM:
                um_guard[0] = S.now("pe")[0]
            AT_OK = S.now("vv", "ss")

            def o_evac(og, pb, wdt, done, col0, i_=i, xp=xtile, xr=xrdy):
                vop(lambda v, a=pb, og_=og: v.tensor_tensor(
                    s5[:, og_ * 512: og_ * 512 + 512], a[:, 0:512],
                    war[:, A_BR + 3072 + og_ * 512: A_BR + 3072 + og_ * 512 + 512],
                    ALU.add), waits=[done, WA_RAW])
                return vop(lambda v, og_=og, j_=i_, x_=xp: v.tensor_tensor(
                    racc[:, j_ * D + og_ * 512: j_ * D + og_ * 512 + 512],
                    s5[:, og_ * 512: og_ * 512 + 512],
                    x_[:, og_ * 512: og_ * 512 + 512], ALU.add),
                    waits=[xr])

            if asub >= 7:
                project(8, lambda c: tb[:, A_AT + c * P: A_AT + (c + 1) * P],
                        lambda c, col0, wdt: war[:, A_WO + c * D + col0:
                                                 A_WO + c * D + col0 + wdt],
                        [512] * 2, A_BR + 3072, AT_OK + [WA_RAW], o_evac)
            else:
                vop(lambda v, i_=i, xp=xtile: v.tensor_copy(
                    racc[:, ts(i_, D)], xp), waits=[xrdy])
            x_slots.consumed(h["xsl"], S.now("vv")[0])
            x_slots.consumed(h["xsl"], S.now("ss")[0])
            if i == 0:
                if "attn" in dmp:
                    dma("sync", dmp["attn"][0:P, :], tb[:, A_AM:A_AM + D],
                        waits=[AM_OK], inc="ho")
                if "racc_a" in dmp:
                    dma("sync", dmp["racc_a"][0:P, :], racc[:, 0:D],
                        waits=S.now("vv"), inc="ho")

        for i in range(NT if stop >= 2 else 0):
            if i + 1 < NT:
                xload.append(load_x(i + 1, x_slots))
            if i == 0:
                stA[0] = emit_stageA(0)
            emit_marshalA(i)
            if i + 1 < NT:
                stA[i + 1] = emit_stageA(i + 1)
            emit_coreA(i)
            del stA[i]
        RACC_A = S.now("vv")[0]

        for H in range(2 if stop >= 3 else 0):
            phase_gate = S.now("pe", "vv", "ss")
            dma("gpsimd", biasf[0:1, 0:IH_M],
                bass.AP(bfc, H * IH_M, [[1, 1], [1, IH_M]]),
                waits=phase_gate, inc="gw")
            if H == 0:
                dma("gpsimd", war[:, M_BR + 2048: M_BR + 3072],
                    bass.AP(bproj, 0, [[D, 1], [0, P], [1, D]]), inc="gw")
            BV_RAW = S.now("gw", "gg")
            for c in range(8):
                dma("gpsimd", war[:, M_WFC + c * IH_M: M_WFC + (c + 1) * IH_M],
                    bass.AP(wfc, c * P * DFC + H * IH_M, [[DFC, P], [1, IH_M]]),
                    inc="gw")
            WFC_RAW = S.now("gw")[0]
            for og in range(4):
                for c in range(8):
                    mm(pbias[0:1, 0:512], lnb_bf[:, 8 + c:8 + c + 1],
                       war[:, M_WFC + c * IH_M + og * 512: M_WFC + c * IH_M + og * 512 + 512],
                       c == 0, c == 7,
                       waits=([LNBF, WFC_RAW] + S.now("vv", "ss")) if c == 0 else [],
                       inc="pe" if c == 7 else None)
                done = S.now("pe")[0]
                bcw = vop(lambda v: v.tensor_copy(brf[0:1, 0:512], pbias[0:1, 0:512]),
                          waits=[done] + list(BV_RAW))
                vop(lambda v, og_=og: v.tensor_tensor(
                    brf2[0:1, 0:512], brf[0:1, 0:512],
                    biasf[0:1, og_ * 512: og_ * 512 + 512], ALU.add),
                    waits=S.now("gw") + [bcw])
                dma("gpsimd", war[:, M_BR + og * 512: M_BR + og * 512 + 512],
                    bass.AP(brf2, 0, [[512, 1], [0, P], [1, 512]]),
                    waits=S.now("vv"), inc="gw")
            BROW_M = S.now("vv", "gw")
            for c in range(8):
                vop(lambda v, c_=c: v.tensor_scalar(
                    war[:, M_WFC + c_ * IH_M: M_WFC + (c_ + 1) * IH_M],
                    war[:, M_WFC + c_ * IH_M: M_WFC + (c_ + 1) * IH_M],
                    lnv[:, 16 + c_:16 + c_ + 1], None, ALU.mult),
                    waits=BROW_M + [LNV])
            WFC_OK = S.now("vv")[0]
            for c in range(NCH_M):
                dma("gpsimd", war[:, M_WPR + c * D: M_WPR + (c + 1) * D],
                    bass.AP(wproj, (H * IH_M + c * P) * D, [[D, P], [1, D]]), inc="gw")
            WPR_RAW = S.now("gw")[0]

            fc_sg = {0: None, 1: None}
            for i in range(NT):
                if H == 0:
                    XN = tb[:, M_XN:M_XN + D]
                    xn_ok = ln_apply(racc[:, ts(i, D)], XN, D, [RACC_A], [RACC_A],
                                     out_guards=S.now("pe"), jkc=M_JK)
                    for c in range(8):
                        do_transpose(XN[:, ts(c, P)], P,
                                     x2T[:, c * n_tok + i * P: c * n_tok + (i + 1) * P],
                                     [xn_ok] if c == 0 else [], "v" if c % 2 == 0 else "s")
                XN2_OK = S.now("vv", "ss")

                def fc_evac(og, pb, wdt, done, col0):
                    td = vop(lambda v, a=pb, og_=og: v.tensor_tensor(
                        s5[:, og_ % 2 * 512: og_ % 2 * 512 + 512], a[:, 0:512],
                        war[:, M_BR + og_ * 512: M_BR + og_ * 512 + 512], ALU.add),
                        waits=[done] + ([fc_sg[og % 2]] if fc_sg[og % 2] else []))
                    ev = sop(lambda s, og_=og: s.activation(
                        tb[:, M_MID + og_ * 512: M_MID + og_ * 512 + 512],
                        s5[:, og_ % 2 * 512: og_ % 2 * 512 + 512], AF.Gelu),
                        waits=[td])
                    fc_sg[og % 2] = ev
                    return ev

                project(8, lambda c: x2T[:, c * n_tok + i * P: c * n_tok + (i + 1) * P],
                        lambda c, col0, wdt: war[:, M_WFC + c * IH_M + col0:
                                                 M_WFC + c * IH_M + col0 + wdt],
                        [512] * 4, None, XN2_OK + [WFC_OK] + BROW_M, fc_evac)
                MID_OK = S.now("ss")[0]
                for c in range(NCH_M):
                    do_transpose(tb[:, M_MID + c * P: M_MID + (c + 1) * P], P,
                                 tb[:, M_MIDT + c * P: M_MIDT + (c + 1) * P],
                                 [MID_OK] if c == 0 else [], "v" if c % 2 == 0 else "s")
                MIDT_OK = S.now("vv", "ss")

                def pr_evac(og, pb, wdt, done, col0, i_=i, H_=H):
                    if H_ == 0:
                        vop(lambda v, a=pb, og_=og: v.tensor_tensor(
                            s5[:, og_ * 512: og_ * 512 + 512], a[:, 0:512],
                            war[:, M_BR + 2048 + og_ * 512: M_BR + 2048 + og_ * 512 + 512],
                            ALU.add), waits=[done] + list(BV_RAW))
                        return vop(lambda v, og_=og, j_=i_: v.tensor_tensor(
                            racc[:, j_ * D + og_ * 512: j_ * D + og_ * 512 + 512],
                            s5[:, og_ * 512: og_ * 512 + 512],
                            racc[:, j_ * D + og_ * 512: j_ * D + og_ * 512 + 512],
                            ALU.add))
                    return vop(lambda v, a=pb, og_=og, j_=i_: v.tensor_tensor(
                        racc[:, j_ * D + og_ * 512: j_ * D + og_ * 512 + 512], a[:, 0:512],
                        racc[:, j_ * D + og_ * 512: j_ * D + og_ * 512 + 512], ALU.add),
                        waits=[done])

                project(NCH_M, lambda c: tb[:, M_MIDT + c * P: M_MIDT + (c + 1) * P],
                        lambda c, col0, wdt: war[:, M_WPR + c * D + col0:
                                                 M_WPR + c * D + col0 + wdt],
                        [512] * 2, None,
                        MIDT_OK + [WPR_RAW] + list(BV_RAW), pr_evac)
                if i == 0 and H == 0 and "mid" in dmp:
                    dma("sync", dmp["mid"][0:P, :], tb[:, M_MID:M_MID + IH_M],
                        waits=[MID_OK], inc="ho")
        RACC_M = S.now("vv")[0]

        xqT = x2T
        x_slots2 = Slots(2)
        for _sl in range(2):
            for g_ in S.now("vv", "ss"):
                x_slots2.guard[_sl].append(g_)
        xload2 = [None] * NT
        if "racc15" in dmp and stop >= 4:
            dma("sync", dmp["racc15"][0:P, :], racc[:, (NT - 1) * D: NT * D],
                waits=[RACC_M], inc="ho")
        if stop >= 4:
            sl0, g0 = x_slots2.use()
            dma("sync", xt[:, ts(sl0, D)], x_hbm[0:P, :], waits=list(g0), inc="hx")
            xload2[0] = (sl0, S.now("hx")[0])
        for i in range(NT if stop >= 4 else 0):
            xsl, xrdy = xload2[i]
            if i + 1 < NT:
                sl_, g_ = x_slots2.use()
                dma("sync", xt[:, ts(sl_, D)], x_hbm[ts(i + 1, P), :], waits=list(g_), inc="hx")
                xload2[i + 1] = (sl_, S.now("hx")[0])
            xtile = xt[:, ts(xsl, D)]
            rtile = racc[:, ts(i, D)]
            sad = vop(lambda v, a=xtile, b=rtile: v.tensor_tensor(s5[:, :], a, b, ALU.add),
                      waits=[xrdy, RACC_M] + S.now("ss"))
            x_slots2.consumed(xsl, sad)
            x4_ok = ln_apply(s5[:, :], rtile, D, [sad], [sad], out_guards=[], jkc=F_JK)
            qd = quant_emit(rtile, D, tb[:, F_XQD:F_XQD + D], [x4_ok] + S.now("pe"))
            for c in range(8):
                do_transpose(tb[:, F_XQD + c * P: F_XQD + (c + 1) * P], P,
                             xqT[:, c * n_tok + i * P: c * n_tok + (i + 1) * P],
                             [qd] if c == 0 else [], "v" if c % 2 == 0 else "s")
            if i == 0:
                if "x4" in dmp:
                    dma("sync", dmp["x4"][0:P, :], rtile, waits=[x4_ok], inc="ho")
                if "xq" in dmp:
                    dma("sync", dmp["xq"][0:P, :], tb[:, F_XQD:F_XQD + D],
                        waits=[qd], inc="ho")
            if i == NT - 1:
                if "s5f" in dmp:
                    dma("sync", dmp["s5f"][0:P, :], s5[:, :], waits=[x4_ok], inc="ho")
                if "x4f" in dmp:
                    dma("sync", dmp["x4f"][0:P, :], rtile, waits=[x4_ok], inc="ho")
        XQT_OK = S.now("vv", "ss")

        stg = Slots(2)
        stgq = [("sync", "hx"), ("gpsimd", "gw")]
        nst = [0]
        for H in range(2 if stop >= 5 else 0):
            phase_gate = S.now("pe", "vv", "ss")
            nfact = [0]
            ms = {"g": [], "u": [], "d": []}
            for wkey, wdram, base in (("g", wg_n, F_WG), ("u", wu_n, F_WU)):
                for j in range(NCH_F):
                    slg, gg_ = stg.use()
                    qe, qsem = stgq[nst[0] % 2]
                    nst[0] += 1
                    dma(qe, xt[:, ts(slg, D)],
                        bass.AP(wdram, (H * IH_F + j * P) * D, [[D, P], [1, D]]),
                        waits=list(gg_) + (phase_gate if nfact[0] < 2 else []),
                        inc=qsem)
                    nfact[0] += 1
                    srdy = S.now(qsem)[0]
                    qd = quant_emit(xt[:, ts(slg, D)], D, tb[:, F_WQN:F_WQN + D],
                                    [srdy], xn32=s5[:, 0:D], v32=elu_f[:, 0:D])
                    stg.consumed(slg, qd)
                    for c in range(8):
                        do_transpose(tb[:, F_WQN + c * P: F_WQN + (c + 1) * P], P,
                                     war[:, base + c * IH_F + j * P: base + c * IH_F + (j + 1) * P],
                                     [qd] if c == 0 else [], "v" if c % 2 == 0 else "s")
                    ms[wkey].append(S.now("vv", "ss"))
            for j in range(8):
                sl0, g0 = stg.use()
                sl1, g1 = stg.use()
                qe, qsem = stgq[nst[0] % 2]
                nst[0] += 1
                dma(qe, xt[:, 0:IH_F],
                    bass.AP(wd_n, j * P * DFF + H * IH_F, [[DFF, P], [1, IH_F]]),
                    waits=list(g0) + list(g1) + (phase_gate if nfact[0] < 2 else []),
                    inc=qsem)
                nfact[0] += 1
                srdy = S.now(qsem)[0]
                HW_ = IH_F // 2
                quant_emit(xt[:, 0:HW_], HW_, tb[:, F_WQN:F_WQN + HW_], [srdy],
                           xn32=s5[:, 0:HW_], v32=elu_f[:, 0:HW_])
                qd = quant_emit(xt[:, HW_:IH_F], HW_, tb[:, F_WQN + HW_:F_WQN + IH_F],
                                [srdy], xn32=s5[:, 0:HW_], v32=elu_f[:, 0:HW_])
                stg.consumed(sl0, qd)
                stg.consumed(sl1, qd)
                for c in range(NCH_F):
                    do_transpose(tb[:, F_WQN + c * P: F_WQN + (c + 1) * P], P,
                                 war[:, F_WD + c * D + j * P: F_WD + c * D + (j + 1) * P],
                                 [qd] if c == 0 else [], "v" if c % 2 == 0 else "s")
                ms["d"].append(S.now("vv", "ss"))
            WQ_OK = S.now("vv", "ss")

            def wq_gate(wkey, col0, wdt):
                j_hi = min((col0 + wdt - 1) // P, len(ms[wkey]) - 1)
                return ms[wkey][j_hi]
            if H == 0 and "wgq" in dmp:
                dma("sync", dmp["wgq"][0:P, :], war[:, F_WG:F_WG + IH_F], waits=WQ_OK, inc="ho")
            if H == 0 and "wuq" in dmp:
                dma("sync", dmp["wuq"][0:P, :], war[:, F_WU:F_WU + 8 * IH_F], waits=WQ_OK, inc="ho")

            OGW = [512, 512, 384]
            HTB = (F_HT, F_HQ)

            def stageF(i):
                par = i % 2
                hb = HTB[par]

                def gate_evac(og, pb, wdt, done, col0):
                    w8 = ([tile_qv[0]] if tile_qv[0] else []) + \
                         ([ht_guard[par]] if ht_guard[par] else [])
                    return sop(lambda s, a=pb, c0=col0, w=wdt: s.activation(
                        tb[:, hb + c0: hb + c0 + w], a[:, 0:w], AF.Silu),
                        waits=[done] + w8)

                project(8, lambda c: xqT[:, c * n_tok + i * P: c * n_tok + (i + 1) * P],
                        lambda c, col0, wdt: war[:, F_WG + c * IH_F + col0:
                                                 F_WG + c * IH_F + col0 + wdt],
                        OGW, None, XQT_OK, gate_evac,
                        og_waits=lambda og, c0, w: wq_gate("g", c0, w))
                SG_OK = S.now("ss")[0]
                if H == 0 and i == 0 and "sg" in dmp:
                    dma("sync", dmp["sg"][0:P, :], tb[:, hb:hb + IH_F],
                        waits=[SG_OK], inc="ho")
                    qstate["sd"] = S.now("ho")[0]

                def up_evac(og, pb, wdt, done, col0):
                    return vop(lambda v, a=pb, c0=col0, w=wdt: v.tensor_tensor(
                        tb[:, hb + c0: hb + c0 + w], a[:, 0:w],
                        tb[:, hb + c0: hb + c0 + w], ALU.mult),
                        waits=[done, SG_OK])

                project(8, lambda c: xqT[:, c * n_tok + i * P: c * n_tok + (i + 1) * P],
                        lambda c, col0, wdt: war[:, F_WU + c * IH_F + col0:
                                                 F_WU + c * IH_F + col0 + wdt],
                        OGW, None, [], up_evac,
                        og_waits=lambda og, c0, w: wq_gate("u", c0, w))
                return dict(H_OK=S.now("vv")[0], hb=hb, par=par)

            def coreF(i, h):
                hb = h["hb"]
                qd = quant_emit(tb[:, hb:hb + IH_F], IH_F, tb[:, hb:hb + IH_F],
                                [h["H_OK"]])
                tile_qv[0] = qd
                for c in range(NCH_F):
                    do_transpose(tb[:, hb + c * P: hb + (c + 1) * P], P,
                                 tb[:, F_HQT + c * P: F_HQT + (c + 1) * P],
                                 [qd] if c == 0 else [], "v" if c % 2 == 0 else "s")
                ht_guard[h["par"]] = S.now("pe")[0]
                HQT_OK = S.now("vv", "ss")

                def dn_evac(og, pb, wdt, done, col0, i_=i):
                    return vop(lambda v, a=pb, og_=og, j_=i_: v.tensor_tensor(
                        racc[:, j_ * D + og_ * 512: j_ * D + og_ * 512 + 512], a[:, 0:512],
                        racc[:, j_ * D + og_ * 512: j_ * D + og_ * 512 + 512], ALU.add),
                        waits=[done])

                project(NCH_F, lambda c: tb[:, F_HQT + c * P: F_HQT + (c + 1) * P],
                        lambda c, col0, wdt: war[:, F_WD + c * D + col0:
                                                 F_WD + c * D + col0 + wdt],
                        [512] * 2, None, HQT_OK, dn_evac,
                        og_waits=lambda og, c0, w: wq_gate("d", c0, w))
                if H == 1 and stop >= 6:
                    DN_OK = S.now("vv")[0]
                    og_ = [ob_guard[0]] if ob_guard[0] else []
                    ev = ln_apply(racc[:, ts(i, D)], outb[:, :], D,
                                  [], [DN_OK] + (list(WQ_OK) if i == 0 else []),
                                  out_guards=og_, jkc=F_JK)
                    dma("sync", out_hbm[ts(i, P), :], outb[:, :], waits=[ev], inc="ho")
                    ob_guard[0] = S.now("ho")[0]

            hF = stageF(0)
            for i in range(NT):
                hN = stageF(i + 1) if i + 1 < NT else None
                coreF(i, hF)
                hF = hN
        X5_OK = S.now("vv")[0]

        if stop < 6:
            for i in range(NT):
                og_ = [ob_guard[0]] if ob_guard[0] else []
                gate = S.now("vv", "ss", "pe")
                ev = sop(lambda s_, b=racc[:, ts(i, D)]: s_.copy(outb[:, :], b),
                         waits=(gate if i == 0 else []) + og_)
                dma("sync", out_hbm[ts(i, P), :], outb[:, :], waits=[ev], inc="ho")
                ob_guard[0] = S.now("ho")[0]
        FINAL_HO = S.at("ho")

        def realize(eng_name):
            def body(eng):
                for fn, waits, inc, amt in S.ops[eng_name]:
                    for (sn, v) in waits:
                        eng.wait_ge(sems[sn], v)
                    r = fn(eng)
                    if inc is not None:
                        r.then_inc(sems[inc], amt)
                if eng_name == "sync":
                    eng.wait_ge(sems["ho"], FINAL_HO)
            return body

        blk.sync(realize("sync"))
        blk.gpsimd(realize("gpsimd"))
        blk.tensor(realize("tensor"))
        blk.vector(realize("vector"))
        blk.scalar(realize("scalar"))

    return nc


def _prep_inputs(L0_attn_x, L0_attn_ln1_w, L0_attn_ln1_b, L0_attn_q_w, L0_attn_q_b,
                 L0_attn_k_w, L0_attn_k_b, L0_attn_v_w, L0_attn_v_b, L0_attn_o_w,
                 L0_attn_o_b, L0_attn_ln2_w, L0_attn_ln2_b, L0_attn_fc_w, L0_attn_fc_b,
                 L0_attn_proj_w, L0_attn_proj_b, L0_ffn_w_gate, L0_ffn_w_up, L0_ffn_w_down):
    f = np.float32
    c = np.ascontiguousarray

    def t(a):
        return c(np.asarray(a).T.astype(f, copy=False))

    base = {
        "wqkv_t": c(np.concatenate([np.asarray(L0_attn_q_w), np.asarray(L0_attn_k_w),
                                    np.asarray(L0_attn_v_w)], axis=0).T.astype(f, copy=False)),
        "wo_t": t(L0_attn_o_w),
        "wfc_t": t(L0_attn_fc_w),
        "wproj_t": t(L0_attn_proj_w),
        "wg_n": c(np.asarray(L0_ffn_w_gate).astype(f, copy=False)),
        "wu_n": c(np.asarray(L0_ffn_w_up).astype(f, copy=False)),
        "wd_n": c(np.asarray(L0_ffn_w_down).astype(f, copy=False)),
        "ln1w": c(np.asarray(L0_attn_ln1_w).reshape(8, P).T.astype(f, copy=False)),
        "ln1b": c(np.asarray(L0_attn_ln1_b).reshape(8, P).T.astype(f, copy=False)),
        "ln2w": c(np.asarray(L0_attn_ln2_w).reshape(8, P).T.astype(f, copy=False)),
        "ln2b": c(np.asarray(L0_attn_ln2_b).reshape(8, P).T.astype(f, copy=False)),
        "bqkv": c(np.concatenate([np.asarray(L0_attn_q_b), np.asarray(L0_attn_k_b),
                                  np.asarray(L0_attn_v_b)]).reshape(1, -1).astype(f, copy=False)),
        "bo": c(np.asarray(L0_attn_o_b).reshape(1, -1).astype(f, copy=False)),
        "bfc": c(np.asarray(L0_attn_fc_b).reshape(1, -1).astype(f, copy=False)),
        "bproj": c(np.asarray(L0_attn_proj_b).reshape(1, -1).astype(f, copy=False)),
    }
    xfull = np.ascontiguousarray(np.asarray(L0_attn_x).astype(f, copy=False).reshape(-1, D))
    return xfull, base


def kernel(**inputs):
    xfull, base = _prep_inputs(**inputs)
    n_tok = xfull.shape[0] // NCORES
    shards = np.split(xfull, NCORES, axis=0)
    nc = build(n_tok=n_tok)
    in_maps = [dict(base, x=np.ascontiguousarray(s)) for s in shards]
    res = run_bass_kernel_spmd(nc, in_maps, core_ids=list(range(NCORES)))
    out = np.concatenate([np.asarray(r["out"]) for r in res.results], axis=0)
    return out.reshape(np.asarray(inputs["L0_attn_x"]).shape)
